# version 07
# baseline (speedup 1.0000x reference)
import numpy as np
import ml_dtypes
from contextlib import ExitStack

import concourse.bass as bass
import concourse.mybir as mybir
from concourse.bass_utils import run_bass_kernel_spmd

BF16 = mybir.dt.bfloat16
F32 = mybir.dt.float32
U8 = mybir.dt.uint8
I32 = mybir.dt.int32
AF = mybir.ActivationFunctionType
ALU = mybir.AluOpType
AXX = mybir.AxisListType.X
NBF = ml_dtypes.bfloat16

NC = 8
S = 8192
DM = 4096
LS = S // NC
NCH = LS // 512
INTER_P = 11264
NTI = INTER_P // 128
NTIL = NTI // NC
SHN = INTER_P // NC
GG = 32
GD = 86
SCALE = 1.0 / float(np.sqrt(128.0))
EPS = 1e-5
NMAC = S // 512
DMT = DM // 128

GBLK = DM + 128 + 88 + 40
UBLK = DM + 128
DBLK = SHN + 128

ENGS = ("sync", "gpsimd", "vector", "scalar", "tensor")

_NC_CACHE = {}


def _bytes_view(qw):
    kk8, n = qw.shape
    b = qw.view('<u4').view(np.uint8).reshape(kk8, n, 4).transpose(0, 2, 1)
    return np.ascontiguousarray(b.reshape(kk8 * 4, n))


def _host_prep(inp):
    x = np.asarray(inp["L0_attn_a_x"])[0]
    xT = np.ascontiguousarray(x.T)

    w_inT = np.ascontiguousarray(np.asarray(inp["L0_attn_a_w_in"]).T)
    w_outT = np.ascontiguousarray(np.asarray(inp["L0_attn_a_w_out"]).T)
    WqT = np.ascontiguousarray(np.asarray(inp["L0_attn_b_Wq"]).T)
    WkT = np.ascontiguousarray(np.asarray(inp["L0_attn_b_Wk"]).T)
    WvT = np.ascontiguousarray(np.asarray(inp["L0_attn_b_Wv"]).T)
    WoT = np.ascontiguousarray(np.asarray(inp["L0_attn_b_Wo"]).T)
    bq = np.asarray(inp["L0_attn_b_bq"]); bk = np.asarray(inp["L0_attn_b_bk"])
    bv = np.asarray(inp["L0_attn_b_bv"]); bo = np.asarray(inp["L0_attn_b_bo"])

    qb_g = _bytes_view(np.asarray(inp["L0_ffn_gate_qweight"]))
    qb_u = _bytes_view(np.asarray(inp["L0_ffn_up_qweight"]))
    qb_d = _bytes_view(np.asarray(inp["L0_ffn_down_qweight"]))
    sc_g = np.asarray(inp["L0_ffn_gate_scales"]).astype(np.float32, copy=False)
    sc_u = np.asarray(inp["L0_ffn_up_scales"]).astype(np.float32, copy=False)
    sc_d = np.asarray(inp["L0_ffn_down_scales"]).astype(np.float32, copy=False)
    qz_g = np.asarray(inp["L0_ffn_gate_qzeros"]).astype(np.int32, copy=False)
    qz_u = np.asarray(inp["L0_ffn_up_qzeros"]).astype(np.int32, copy=False)
    qz_d = np.asarray(inp["L0_ffn_down_qzeros"]).astype(np.int32, copy=False)
    gi_g = np.asarray(inp["L0_ffn_gate_g_idx"]).astype(np.int32)
    gi_u = np.asarray(inp["L0_ffn_up_g_idx"]).astype(np.int32)
    gi_d = np.asarray(inp["L0_ffn_down_g_idx"]).astype(np.int32)

    gi_d_p = np.full(INTER_P, -1, np.int32)
    for r in range(NC):
        gi_d_p[r * SHN: r * SHN + 1376] = gi_d[r * 1376:(r + 1) * 1376]

    inv_freq = 1.0 / (10000.0 ** (np.arange(0, 128, 2, dtype=np.float32) / 128.0))
    ang = np.arange(S, dtype=np.float32)[:, None] * inv_freq[None, :]
    cos = np.cos(ang).astype(np.float32).T
    sin = np.sin(ang).astype(np.float32).T
    cosR = np.ascontiguousarray(np.concatenate([cos, cos], axis=0))
    sinS = np.ascontiguousarray(np.concatenate([-sin, sin], axis=0))

    tri = np.ascontiguousarray(
        (np.arange(512)[:, None] <= np.arange(512)[None, :]).astype(NBF))
    iota128x88 = np.ascontiguousarray(
        np.broadcast_to(np.arange(88, dtype=np.float32), (128, 88)))
    iota88c = np.ascontiguousarray(np.arange(88, dtype=np.float32)[:, None])
    ident = np.ascontiguousarray(np.eye(128, dtype=NBF))

    def col4(v):
        return np.ascontiguousarray(v.reshape(4, 128).T)

    in_maps = []
    for c in range(NC):
        qr = slice(512 * c, 512 * (c + 1))
        m = {
            "x0T": np.ascontiguousarray(xT[:, LS * c: LS * (c + 1)]),
            "waqT": np.ascontiguousarray(w_inT[:, qr]),
            "wakT": np.ascontiguousarray(w_inT[:, 4096 + 128 * c: 4096 + 128 * (c + 1)]),
            "wavT": np.ascontiguousarray(w_inT[:, 5120 + 128 * c: 5120 + 128 * (c + 1)]),
            "waoT": np.ascontiguousarray(w_outT[qr, :]),
            "wbqT": np.ascontiguousarray(WqT[:, qr]),
            "wbkT": np.ascontiguousarray(WkT[:, qr]),
            "wbvT": np.ascontiguousarray(WvT[:, qr]),
            "wboT": np.ascontiguousarray(WoT[qr, :]),
            "bq4": col4(bq[qr]), "bk4": col4(bk[qr]), "bv4": col4(bv[qr]),
            "bo32": np.ascontiguousarray(bo.reshape(DMT, 128).T),
            "qbg": np.ascontiguousarray(
                np.pad(qb_g[:, 1376 * c:1376 * (c + 1)], ((0, 0), (0, 32)))),
            "qbu": np.ascontiguousarray(
                np.pad(qb_u[:, 1376 * c:1376 * (c + 1)], ((0, 0), (0, 32)))),
            "qbd": np.ascontiguousarray(
                np.pad(qb_d[688 * c:688 * (c + 1), :], ((0, 16), (0, 0)))),
            "scg_sh": np.ascontiguousarray(
                np.pad(sc_g[:, 1376 * c:1376 * (c + 1)], ((0, 0), (0, 32)))),
            "scu_sh": np.ascontiguousarray(
                np.pad(sc_u[:, 1376 * c:1376 * (c + 1)], ((0, 0), (0, 32)))),
            "scd": sc_d,
            "qzg_sh": np.ascontiguousarray(
                np.pad(qz_g[:, 172 * c:172 * (c + 1)], ((0, 0), (0, 4)))),
            "qzu_sh": np.ascontiguousarray(
                np.pad(qz_u[:, 172 * c:172 * (c + 1)], ((0, 0), (0, 4)))),
            "qzd": qz_d,
            "gig": np.ascontiguousarray(gi_g[:, None]),
            "giu": np.ascontiguousarray(gi_u[:, None]),
            "gidp": np.ascontiguousarray(gi_d_p[:, None]),
            "gidsh": np.ascontiguousarray(gi_d_p[SHN * c: SHN * (c + 1)])[:, None],
            "cosR": cosR, "sinS": sinS, "tri": tri,
            "iota88": iota128x88, "iota88c": iota88c,
            "ident": ident,
        }
        in_maps.append(m)
    return in_maps


class Sem:
    __slots__ = ("h", "n")

    def __init__(self, handle):
        self.h = handle
        self.n = 0

    def bump(self, k):
        self.n += k
        return self.n


class Em:
    def __init__(self, nc):
        self.nc = nc
        self.q = {e: [] for e in ENGS}
        self.ctx = ExitStack()
        self.auto = {}

    def sem(self, name):
        return Sem(self.ctx.enter_context(self.nc.semaphore(name)))

    def sb(self, name, shape, dtype):
        return self.ctx.enter_context(self.nc.sbuf_tensor(name, shape, dtype))

    def psum(self, name, shape, dtype=F32):
        return self.ctx.enter_context(self.nc.psum_tensor(name, shape, dtype))

    def emit(self, eng, fn):
        self.q[eng].append(fn)

    def dma(self, eng, out, in_, sem, **kw):
        v = sem.bump(16)
        self.emit(eng, lambda e, o=out, i=in_, s=sem.h, kw=kw:
                  e.dma_start(out=o, in_=i, **kw).then_inc(s, 16))
        return v

    def wait(self, eng, sem, val):
        assert val <= sem.n
        if val > 0:
            self.emit(eng, lambda e, s=sem.h, vv=val: e.wait_ge(s, vv))

    def waitf(self, eng, sem):
        v = sem.n
        if v > 0:
            self.emit(eng, lambda e, s=sem.h, vv=v: e.wait_ge(s, vv))

    def mm(self, out, lhsT, rhs, start, stop, sem=None, tile_position=None):
        v = sem.bump(1) if sem else None
        import inspect
        fr = inspect.currentframe().f_back
        tag = f"mm@{fr.f_lineno}"
        def f(t, o=out, l=lhsT, r=rhs, st=start, sp=stop, s=sem, tp=tile_position,
              tag=tag):
            i = t.matmul(o, l, r, start=st, stop=sp, tile_position=tp)
            i.annotate(tag)
            if s is not None:
                i.then_inc(s.h, 1)
        self.emit("tensor", f)
        return v

    def trans(self, out, in_, ident, sem=None):
        v = sem.bump(1) if sem else None
        def f(t, o=out, i=in_, d=ident, s=sem):
            ins = t.transpose(o, i, d)
            if s is not None:
                ins.then_inc(s.h, 1)
        self.emit("tensor", f)
        return v

    def actv(self, out, in_, func, sem=None, bias=0.0, scale=1.0):
        s = self.auto["scalar"]
        pre = s.n
        v = s.bump(1)
        def f(a, o=out, i=in_, fu=func, b=bias, sc=scale, pre=pre, s=s):
            if pre > 0:
                a.wait_ge(s.h, pre)
            a.activation(o, i, fu, bias=b, scale=sc).then_inc(s.h, 1)
        self.emit("scalar", f)
        return v

    def vec(self, fn_name, *args, sem=None, **kw):
        s = self.auto["vector"]
        pre = s.n
        v = s.bump(1)
        def f(e, a=args, kw=kw, fn=fn_name, pre=pre, s=s):
            if pre > 0:
                e.wait_ge(s.h, pre)
            getattr(e, fn)(*a, **kw).then_inc(s.h, 1)
        self.emit("vector", f)
        return v

    def replay_into(self, nc):
        with nc.Block() as block:
            for name in ENGS:
                ops = self.q[name]
                def runner(e, ops=ops):
                    for op in ops:
                        op(e)
                getattr(block, name)(runner)


def build_nc(taps=(), stop_after=None):
    nc = bass.Bass("TRN2", target_bir_lowering=False, debug=False)

    def din(name, shape, dtype):
        return nc.dram_tensor(name, shape, dtype, kind="ExternalInput")

    x0T = din("x0T", [DM, LS], F32)
    waqT = din("waqT", [DM, 512], F32); wakT = din("wakT", [DM, 128], F32)
    wavT = din("wavT", [DM, 128], F32); waoT = din("waoT", [512, DM], F32)
    wbqT = din("wbqT", [DM, 512], F32); wbkT = din("wbkT", [DM, 512], F32)
    wbvT = din("wbvT", [DM, 512], F32); wboT = din("wboT", [512, DM], F32)
    bq4 = din("bq4", [128, 4], F32); bk4 = din("bk4", [128, 4], F32)
    bv4 = din("bv4", [128, 4], F32); bo32 = din("bo32", [128, DMT], F32)
    qbg = din("qbg", [2048, SHN], U8); qbu = din("qbu", [2048, SHN], U8)
    qbd = din("qbd", [704, DM], U8)
    scg_sh = din("scg_sh", [GG, SHN], F32); scu_sh = din("scu_sh", [GG, SHN], F32)
    scd = din("scd", [GD, DM], F32)
    qzg_sh = din("qzg_sh", [GG, SHN // 8], I32)
    qzu_sh = din("qzu_sh", [GG, SHN // 8], I32)
    qzd = din("qzd", [GD, 512], I32)
    gig = din("gig", [DM, 1], I32); giu = din("giu", [DM, 1], I32)
    gidp = din("gidp", [INTER_P, 1], I32); gidsh = din("gidsh", [SHN, 1], I32)
    cosR = din("cosR", [128, S], F32); sinS = din("sinS", [128, S], F32)
    tri = din("tri", [512, 512], BF16)
    iota88 = din("iota88", [128, 88], F32); iota88c = din("iota88c", [88, 1], F32)
    ident = din("ident", [128, 128], BF16)

    out_ext = nc.dram_tensor("out", [DM, LS], F32, kind="ExternalOutput")

    ln_loc = nc.dram_tensor("ln_loc", [2 * DM, 512], BF16)
    ln_full = nc.dram_tensor("ln_full", [2 * NC * DM, 512], BF16)
    partial = nc.dram_tensor("partial", [2 * NC * DM, 512], F32)
    red = nc.dram_tensor("red", [2 * DM, 512], F32)
    xA = nc.dram_tensor("xA", [DM, LS], F32)
    xB = nc.dram_tensor("xB", [DM, LS], F32)
    qaD = nc.dram_tensor("qaD", [512, S], BF16)
    kaD = nc.dram_tensor("kaD", [128, S], BF16)
    vaD = nc.dram_tensor("vaD", [S, 128], BF16)
    qbD = nc.dram_tensor("qbD", [512, S], BF16)
    kbD = nc.dram_tensor("kbD", [512, S], BF16)
    vbD = nc.dram_tensor("vbD", [S, 512], BF16)
    wsg_sh = nc.dram_tensor("wsg_sh", [128, NTIL * GBLK], BF16)
    wsu_sh = nc.dram_tensor("wsu_sh", [128, NTIL * UBLK], BF16)
    wsd_sh = nc.dram_tensor("wsd_sh", [128, DMT * DBLK], BF16)
    wsg = nc.dram_tensor("wsg", [NC * 128, NTIL * GBLK], BF16)
    wsu = nc.dram_tensor("wsu", [NC * 128, NTIL * UBLK], BF16)
    wsd = nc.dram_tensor("wsd", [NC * 128, DMT * DBLK], BF16)

    internal = {"ln_loc": ln_loc, "ln_full": ln_full, "partial": partial,
                "red": red, "xA": xA, "xB": xB, "qaD": qaD, "kaD": kaD,
                "vaD": vaD, "qbD": qbD, "kbD": kbD, "vbD": vbD,
                "wsg_sh": wsg_sh, "wsu_sh": wsu_sh, "wsd_sh": wsd_sh,
                "wsg": wsg, "wsu": wsu, "wsd": wsd}
    tap_pairs = []
    for tname in taps:
        t = internal[tname]
        tap_pairs.append((nc.dram_tensor("tap_" + tname, list(t.ap().shape),
                                         t.ap().dtype, kind="ExternalOutput"), t))

    em = Em(nc)

    R1 = em.sb("R1", [128, 45056], BF16)
    R2 = em.sb("R2", [128, 24576], BF16)
    R3 = em.sb("R3", [128, 17152], BF16)

    h_sb = R1[:, 0:45056]
    x_sb = R2[:, 0:16384]
    wo_sb = R2[:, 0:16384]

    dq_u8 = R1[:, 0:2048].bitcast(U8)
    dq_f = R1[:, 2048:10240].bitcast(F32)
    dq_b = R1[:, 10240:14336]
    zz_i = R1[:, 14336:23552].bitcast(I32)
    zz_f = R1[:, 23552:31744].bitcast(F32)
    zz_n = R1[:, 31744:39936].bitcast(F32)
    sc_bf = R1[:, 39936:44032]

    kv_k = R3[:, 0:4096]
    kv_v = R3[:, 4096:8192]
    q_mac = R3[:, 8192:10240]
    p_sb = R3[:, 10240:14336]
    att_sb = R3[:, 14336:16384]

    ones1 = em.sb("ones1", [1, 128], F32)
    onesc_b = em.sb("onesc_b", [128, 1], BF16)
    ident_s = em.sb("ident_s", [128, 128], BF16)
    tri_s = em.sb("tri_s", [128, 2048], BF16)
    iota_s = em.sb("iota_s", [128, 88], F32)
    iotac_s = em.sb("iotac_s", [88, 1], F32)
    ohxg_s = em.sb("ohxg_s", [128, DMT * GG], BF16)
    ohxu_s = em.sb("ohxu_s", [128, DMT * GG], BF16)
    bias_all = em.sb("bias_all", [128, 64], F32)
    bq_s = em.sb("bq_s", [128, 4], F32)
    bk_s = em.sb("bk_s", [128, 4], F32)
    bv_s = em.sb("bv_s", [128, 4], F32)
    bo_s = em.sb("bo_s", [128, DMT], F32)
    xg_s = em.sb("xg_s", [GG, 1024], BF16)
    hg_s = em.sb("hg_s", [88, 512], BF16)
    dq_gi = em.sb("dq_gi", [88, 256], I32)
    dq_gf = em.sb("dq_gf", [88, 256], F32)
    dq_ohT = em.sb("dq_ohT", [88, 256], BF16)
    gcol_i = em.sb("gcol_i", [128, 1], I32)
    gcol_f = em.sb("gcol_f", [128, 1], F32)
    oh1 = em.sb("oh1", [128, 88], BF16)

    t0 = em.sb("t0", [128, 1536], F32)
    t1 = em.sb("t1", [128, 1536], F32)
    tb0 = em.sb("tb0", [128, 1536], BF16)
    vtr = em.sb("vtr", [128, 512], BF16)
    rope_c = t0[:, 1024:1536]
    rope_s = t1[:, 1024:1536]
    den_s = t1[:, 0:512]
    rows = em.sb("rows", [1, 1536], F32)
    qn_s = em.sb("qn_s", [1, 64], F32)
    eps_c = em.sb("eps_c", [1, 1], F32)
    kn_s = em.sb("kn_s", [1, 80], F32)
    cneg_s = em.sb("cneg_s", [1, 64], F32)

    PA = em.psum("PA", [128, 1024])
    PBX = em.psum("PBX", [128, 2048])
    PB = PBX[:, 0:1024]
    PB2 = PBX[:, 1024:2048]
    PC = em.psum("PC", [128, 512])
    PT = em.psum("PT", [128, 512], BF16)

    sPE = em.sem("sPE"); sACT = em.sem("sACT"); sDVE = em.sem("sDVE")
    em.auto = {"vector": sDVE, "scalar": sACT}
    sCC = em.sem("sCC")
    sCN = em.sem("sCN")
    sTL = [em.sem(f"sTL{i}") for i in range(3)]
    sST = [em.sem(f"sST{i}") for i in range(3)]
    sX = em.sem("sX"); sQ = em.sem("sQ")
    sKV = [em.sem("sKV0"), em.sem("sKV1"), em.sem("sKV2"), em.sem("sKV3")]
    sW = [em.sem("sWg0"), em.sem("sWg1")]
    sD = [em.sem("sWd0"), em.sem("sWd1")]
    sPRS = [em.sem("sPRS0"), em.sem("sPRS1")]
    sDQ = em.sem("sDQ"); sDQ2 = em.sem("sDQ2"); sSC = em.sem("sSC"); sAW = em.sem("sAW")
    sWO = em.sem("sWO"); sSW = em.sem("sSW"); sSW2 = em.sem("sSW2"); sVT = em.sem("sVT")
    sQKV = em.sem("sQKV"); sWS = em.sem("sWS"); sWS2 = em.sem("sWS2"); sLNS = em.sem("sLNS")
    sOUT = em.sem("sOUT")

    rg = [list(range(NC))]

    def collective(kind, op, ins, outs):
        v = sCC.bump(1)
        def f(g, k=kind, o=op, i=ins, u=outs):
            g.collective_compute(k, o, replica_groups=rg, ins=i, outs=u).then_inc(sCC.h, 1)
        em.emit("gpsimd", f)
        return v

    em.dma("sync", ident_s[:, :], ident.ap(), sCN)
    for l in range(4):
        em.dma("sync", tri_s[:, l * 512:(l + 1) * 512],
               tri.ap()[128 * l:128 * (l + 1), :], sCN)
    em.dma("sync", iota_s[:, :], iota88.ap(), sCN)
    em.dma("sync", iotac_s[:, :], iota88c.ap(), sCN)
    em.dma("sync", bq_s[:, :], bq4.ap(), sCN)
    em.dma("sync", bk_s[:, :], bk4.ap(), sCN)
    em.dma("sync", bv_s[:, :], bv4.ap(), sCN)
    em.dma("sync", bo_s[:, :], bo32.ap(), sCN)
    for e in ("vector", "scalar", "tensor", "gpsimd"):
        em.waitf(e, sCN)
    em.vec("memset", ones1[:, :], 1.0)
    em.vec("memset", eps_c[:, :], EPS)
    em.vec("memset", tb0[:, 0:1024], 0.0)
    em.vec("memset", onesc_b[:, :], 1.0, sem=sDVE)
    consts_v = sDVE.n
    em.wait("tensor", sDVE, consts_v)
    em.wait("scalar", sDVE, consts_v)
    em.wait("gpsimd", sDVE, consts_v)
    for shd, w in ((wsg_sh, NTIL * GBLK), (wsu_sh, NTIL * UBLK),
                   (wsd_sh, DMT * DBLK)):
        w1 = (w // 1024) * 1024
        em.dma("gpsimd", shd.ap()[:, 0:w1].rearrange("p (r c) -> p r c", c=1024),
               tb0[:, 0:1024].unsqueeze(1).broadcast_to([128, w1 // 1024, 1024]),
               sWS)
        if w1 < w:
            em.dma("gpsimd", shd.ap()[:, w1:w], tb0[:, 0:w - w1], sWS)
    em.waitf("gpsimd", sWS)
    em.waitf("scalar", sWS)
    em.waitf("vector", sWS)

    def emit_merge_ln_chunk(cc, src_x, src_red, dst_x, dst_lnloc, bias_cols):
        col = cc * 512
        em.waitf("vector", sLNS)
        em.wait("vector", sPE, sPE.n)
        em.wait("sync", sDVE, sDVE.n)
        em.wait("sync", sACT, sACT.n)
        em.waitf("sync", sOUT)
        em.waitf("sync", sPRS[0])
        em.waitf("sync", sPRS[1])
        slot_act = {}
        slot_dve = {}
        prev_stat = 0
        for ft in range(DMT):
            r = ft % 3
            a = t0[:, r * 512:(r + 1) * 512]
            if ft >= 3:
                em.waitf("sync", sST[r])
                em.wait("sync", sACT, slot_act.get(ft - 3, 0))
                em.wait("sync", sDVE, slot_dve.get(ft - 3, 0))
            em.dma("sync", a, src_x[128 * ft:128 * (ft + 1), col:col + 512], sTL[r])
            if src_red is not None:
                b = t1[:, r * 512:(r + 1) * 512]
                em.dma("sync", b, src_red[128 * ft:128 * (ft + 1), 0:512],
                       sTL[r])
                em.waitf("vector", sTL[r])
                if bias_cols is not None:
                    em.vec("tensor_scalar", b, b, bias_cols[:, ft:ft + 1], None,
                           ALU.add)
                em.vec("tensor_add", a, a, b, sem=sDVE)
                mv = sDVE.n
                slot_dve[ft] = mv
                em.wait("gpsimd", sDVE, mv)
                em.dma("gpsimd", dst_x[128 * ft:128 * (ft + 1), col:col + 512], a,
                       sST[r])
                em.wait("scalar", sDVE, mv)
            else:
                em.waitf("scalar", sTL[r])
            xb = tb0[:, 0:512]
            sq = tb0[:, 512:1024]
            em.wait("scalar", sPE, prev_stat)
            em.actv(xb, a, AF.Copy)
            em.actv(sq, a, AF.Square, sem=sACT)
            av = sACT.n
            slot_act[ft] = av
            em.wait("tensor", sACT, av)
            em.mm(PA[0:1, 0:512], onesc_b[:, :], xb, start=(ft == 0),
                  stop=(ft == DMT - 1))
            prev_stat = em.mm(PA[0:1, 512:1024], onesc_b[:, :], sq, start=(ft == 0),
                              stop=(ft == DMT - 1), sem=sPE)
        pe_done = sPE.n
        em.wait("vector", sPE, pe_done)
        em.vec("tensor_scalar", rows[0:1, 0:512], PA[0:1, 0:512], 1.0 / DM, None,
               ALU.mult)
        em.vec("tensor_scalar", rows[0:1, 512:1024], PA[0:1, 512:1024], 1.0 / DM,
               None, ALU.mult)
        em.vec("tensor_mul", rows[0:1, 1024:1536], rows[0:1, 0:512],
               rows[0:1, 0:512])
        em.vec("tensor_sub", rows[0:1, 512:1024], rows[0:1, 512:1024],
               rows[0:1, 1024:1536], sem=sDVE)
        vv = sDVE.n
        em.wait("scalar", sDVE, vv)
        em.actv(rows[0:1, 1024:1536], rows[0:1, 512:1024], AF.Ln,
                bias=eps_c[0:1, :])
        em.actv(rows[0:1, 512:1024], rows[0:1, 1024:1536], AF.Exp, scale=-0.5,
                sem=sACT)
        av = sACT.n
        em.wait("vector", sACT, av)
        em.vec("tensor_mul", rows[0:1, 1024:1536], rows[0:1, 0:512],
               rows[0:1, 512:1024], sem=sDVE)
        mrv = sDVE.n
        em.wait("tensor", sDVE, mrv)
        em.mm(PB[:, 0:512], ones1[:, :], rows[0:1, 512:1024], start=True, stop=True)
        em.mm(PB[:, 512:1024], ones1[:, :], rows[0:1, 1024:1536], start=True,
              stop=True, sem=sPE)
        bcv = sPE.n
        rdsrc = dst_x if (src_red is not None and dst_x is not None) else src_x
        for i in range(3):
            em.waitf("sync", sST[i])
        em.wait("sync", sDVE, sDVE.n)
        em.wait("sync", sACT, sACT.n)
        slot_dve2 = {}
        for ft in range(DMT):
            r = ft % 3
            a = t0[:, r * 512:(r + 1) * 512]
            if ft >= 3:
                em.wait("sync", sDVE, slot_dve2.get(ft - 3, 0))
            em.dma("sync", a, rdsrc[128 * ft:128 * (ft + 1), col:col + 512], sTL[r])
            em.waitf("vector", sTL[r])
            em.wait("vector", sPE, bcv)
            b = t1[:, r * 512:(r + 1) * 512]
            em.vec("tensor_mul", b, a, PB[:, 0:512])
            ln_t = x_sb[:, ft * 512:(ft + 1) * 512]
            em.vec("tensor_sub", ln_t, b, PB[:, 512:1024], sem=sDVE)
            lv = sDVE.n
            slot_dve2[ft] = lv
            if dst_lnloc is not None:
                em.wait("gpsimd", sDVE, lv)
                em.dma("gpsimd", dst_lnloc[128 * ft:128 * (ft + 1), 0:512],
                       ln_t, sLNS)

    ag_half = {}

    def mk_ag_ln(h):
        return collective(
            "AllGather", ALU.bypass,
            [ln_loc.ap()[h * DM:(h + 1) * DM, :].opt()],
            [ln_full.ap()[h * NC * DM:(h + 1) * NC * DM, :].opt()])

    for cc in range(NCH):
        emit_merge_ln_chunk(cc, x0T.ap(), None, None,
                            ln_loc.ap()[cc * DM:(cc + 1) * DM, :], None)
        em.waitf("gpsimd", sLNS)
        ag_half[cc] = mk_ag_ln(cc)
    ag1 = ag_half[1]

    if stop_after == "ln1":
        _finish(em, tap_pairs, sOUT, sCC)
        em.replay_into(nc)
        return nc

    def blocked(dst_dram, blk):
        return dst_dram.ap().rearrange("p (b c) -> p b c", c=blk)

    def emit_dequant_gen(qb_ap, sc_ap, qz_ap, gidx_ap, nkt, ncols, G, nblk, blk,
                         dst_dram, want_ohh, gidx_oh_ap, wofs):
        em.wait("sync", sDVE, sDVE.n)
        em.wait("gpsimd", sPE, sPE.n)
        em.dma("gpsimd", sc_bf[0:G, 0:ncols], sc_ap, sSC)
        em.waitf("tensor", sSC)
        dst = blocked(dst_dram, blk)
        dbuf = ncols <= 2048
        back = 2 if dbuf else 1
        unp0, mm0 = sDVE.n, sPE.n
        st0 = {0: sWS.n, 1: sWS2.n}
        unp_mark = {}
        mm_mark = {}
        store_mark = {}
        for kt in range(nkt):
            p = (kt % 2) if dbuf else 0
            u8v = dq_u8[:, p * 2048:p * 2048 + ncols]
            fv = dq_f[:, p * 2048:p * 2048 + ncols]
            bv = dq_b[:, p * 2048:p * 2048 + ncols]
            giv = dq_gi[0:G, p * 128:p * 128 + 128]
            gfv = dq_gf[0:G, p * 128:p * 128 + 128]
            ohv = dq_ohT[0:G, p * 128:p * 128 + 128]
            em.wait("sync", sDVE, unp_mark.get(kt - back, unp0))
            bsrc = qb_ap[64 * kt:64 * (kt + 1), :].unsqueeze(0) \
                .broadcast_to([2, 64, ncols])
            dqsem = sDQ2 if (dbuf and p == 1) else sDQ
            em.dma("sync", u8v, bsrc, dqsem)
            gsrc = gidx_ap[kt * 128:(kt + 1) * 128, 0:1] \
                .rearrange("k x -> x k").broadcast_to([G, 128])
            em.dma("sync", giv, gsrc, dqsem)
            em.waitf("vector", dqsem)
            wsem = sWS2 if (dbuf and p == 1) else sWS
            em.wait("vector", wsem, store_mark.get(kt - back, st0[p]))
            em.wait("vector", sPE, mm_mark.get(kt - back, mm0))
            em.vec("tensor_copy", gfv,
                   giv.rearrange("g (k2 o) -> g k2 o", o=2)
                   .transpose([0, 2, 1]))
            em.vec("tensor_scalar", ohv, gfv, iotac_s[0:G, :],
                   None, ALU.is_equal)
            em.vec("tensor_scalar", u8v[0:64, :], u8v[0:64, :],
                   15, None, ALU.bitwise_and)
            em.vec("tensor_scalar", u8v[64:128, :], u8v[64:128, :],
                   4, 15, ALU.logical_shift_right, ALU.bitwise_and)
            em.vec("tensor_copy", fv, u8v, sem=sDVE)
            dv = sDVE.n
            unp_mark[kt] = dv
            nchk = (ncols + 511) // 512
            for ch in range(nchk):
                a0 = ch * 512
                a1 = min(ncols, a0 + 512)
                em.wait("tensor", sDVE, dv if ch == 0 else sDVE.n)
                em.mm(PC[:, 0:a1 - a0], ohv, sc_bf[0:G, a0:a1],
                      start=True, stop=True, sem=sPE)
                em.wait("vector", sPE, sPE.n)
                em.vec("tensor_mul", bv[:, a0:a1], fv[:, a0:a1],
                       PC[:, 0:a1 - a0], sem=sDVE)
            mm_mark[kt] = sPE.n
            em.wait("gpsimd", sDVE, sDVE.n)
            dstp = dst_dram.ap().rearrange("(k2 o) q -> k2 o q", o=2) \
                .transpose([1, 0, 2]) \
                .rearrange("o k2 (b c) -> o k2 b c", c=blk)[
                    :, :, :, kt * 128:(kt + 1) * 128]
            cpb = ncols // nblk
            bstep2 = max(1, 1536 // 64)
            for o in range(2):
                for b0 in range(0, nblk, bstep2):
                    b1 = min(nblk, b0 + bstep2)
                    em.dma("gpsimd", dstp[o][:, b0:b1, :],
                           bv[64 * o:64 * (o + 1), b0 * cpb:b1 * cpb]
                           .rearrange("p (b c) -> p b c", b=b1 - b0), wsem)
            store_mark[kt] = wsem.n
            yield
        unpack_mark = unp_mark.get(nkt - 1, 0)
        nq = ncols // 8
        em.wait("sync", sDVE, unpack_mark)
        em.dma("sync", zz_i[0:G, 0:nq], qz_ap, sDQ)
        em.dma("sync", zz_n[0:G, 0:ncols], sc_ap, sDQ)
        em.waitf("vector", sDQ)
        em.waitf("vector", sWS)
        em.waitf("vector", sWS2)
        for sh in range(8):
            em.vec("tensor_scalar",
                   zz_i[0:G, 512:512 + ncols].rearrange("g (c e) -> g c e", e=8)[
                       :, :, sh:sh + 1],
                   zz_i[0:G, 0:nq], 4 * sh, 15,
                   ALU.logical_shift_right, ALU.bitwise_and)
        em.vec("tensor_copy", zz_f[0:G, 0:ncols], zz_i[0:G, 512:512 + ncols])
        em.vec("tensor_scalar", zz_n[0:G, 0:ncols], zz_n[0:G, 0:ncols], -1.0, None,
               ALU.mult)
        em.vec("scalar_tensor_tensor", dq_b[0:G, 0:ncols], zz_f[0:G, 0:ncols], 1.0,
               zz_n[0:G, 0:ncols], ALU.add, ALU.mult, sem=sDVE)
        em.wait("gpsimd", sDVE, sDVE.n)
        bstep = max(1, 1536 // G)
        for b0 in range(0, nblk, bstep):
            b1 = min(nblk, b0 + bstep)
            em.dma("gpsimd", dst[0:G, b0:b1, wofs:wofs + 128],
                   dq_b[0:G, b0 * (ncols // nblk):b1 * (ncols // nblk)]
                   .rearrange("g (b c) -> g b c", b=b1 - b0), sWS)
        yield
        if want_ohh:
            for bl in range(nblk):
                em.wait("sync", sDVE, sDVE.n)
                em.dma("sync", gcol_i[:, :],
                       gidx_oh_ap[bl * 128:(bl + 1) * 128, 0:1], sDQ)
                em.waitf("vector", sDQ)
                em.waitf("vector", sWS)
                em.waitf("vector", sWS2)
                em.vec("tensor_copy", gcol_f[:, :], gcol_i[:, :])
                em.vec("tensor_scalar", oh1[:, :], iota_s[:, 0:88], gcol_f[:, :],
                       None, ALU.is_equal, sem=sDVE)
                em.wait("gpsimd", sDVE, sDVE.n)
                em.dma("gpsimd", dst[:, bl:bl + 1, wofs + 128:wofs + 216],
                       oh1[:, :].unsqueeze(1), sWS)
                yield

    agd_d = {}

    def emit_ohx_gen(gidx_ap, dst):
        for kt in range(DMT):
            em.wait("sync", sDVE, sDVE.n)
            em.dma("sync", gcol_i[:, :], gidx_ap[kt * 128:(kt + 1) * 128, 0:1], sDQ)
            em.waitf("vector", sDQ)
            em.vec("tensor_copy", gcol_f[:, :], gcol_i[:, :])
            em.vec("tensor_scalar", dst[:, kt * GG:(kt + 1) * GG], iota_s[:, 0:GG],
                   gcol_f[:, :], None, ALU.is_equal, sem=sDVE)
            yield

    def deq_master():
        yield from emit_dequant_gen(qbg.ap(), scg_sh.ap(), qzg_sh.ap(), gig.ap(),
                                    DMT, SHN, GG, NTIL, GBLK, wsg_sh, True,
                                    gidsh.ap(), DM)
        em.waitf("gpsimd", sWS)
        em.waitf("gpsimd", sWS2)
        agd_d["agg"] = collective("AllGather", ALU.bypass, [wsg_sh.ap().opt()],
                                  [wsg.ap().opt()])
        yield
        yield from emit_dequant_gen(qbu.ap(), scu_sh.ap(), qzu_sh.ap(), giu.ap(),
                                    DMT, SHN, GG, NTIL, UBLK, wsu_sh, False,
                                    None, DM)
        em.waitf("gpsimd", sWS)
        em.waitf("gpsimd", sWS2)
        agd_d["agu"] = collective("AllGather", ALU.bypass, [wsu_sh.ap().opt()],
                                  [wsu.ap().opt()])
        yield
        yield from emit_dequant_gen(qbd.ap(), scd.ap(), qzd.ap(), gidsh.ap(),
                                    NTIL, DM, GD, DMT, DBLK, wsd_sh, False,
                                    None, SHN)
        em.waitf("gpsimd", sWS)
        em.waitf("gpsimd", sWS2)
        agd_d["agd"] = collective("AllGather", ALU.bypass, [wsd_sh.ap().opt()],
                                  [wsd.ap().opt()])
        yield
        yield from emit_ohx_gen(gig.ap(), ohxg_s)
        yield from emit_ohx_gen(giu.ap(), ohxu_s)

    deq = deq_master()
    if stop_after == "dequant":
        for _ in deq:
            pass
        _finish(em, tap_pairs, sOUT, sCC)
        em.replay_into(nc)
        return nc

    def emit_attn(cfg):
        nkt = cfg["kw"] // 128
        WREG = cfg["wreg"]
        XBUF = cfg["xbuf"]
        STP = cfg["statps"]
        if cfg["per_head_kv"]:
            passes = [[("q", 0), ("q", 1), ("q", 2), ("q", 3), ("k", 0), ("k", 1)],
                      [("k", 2), ("k", 3), ("v", 0), ("v", 1), ("v", 2), ("v", 3)]]
        else:
            passes = [[("q", 0), ("q", 1), ("q", 2), ("q", 3), ("k", 0), ("v", 0)]]
        dram_of = {"q": cfg["qT"], "k": cfg["kT"], "v": cfg["vT"]}
        wsrc_of = {"q": cfg["wq"], "k": cfg["wk"], "v": cfg["wv"]}
        for pi, plist in enumerate(passes):
            em.wait("gpsimd", sPE, sPE.n)
            em.wait("gpsimd", sDVE, sDVE.n)
            em.waitf("gpsimd", sLNS)
            em.waitf("sync", sLNS)
            for wi, (nm, nt) in enumerate(plist):
                src = wsrc_of[nm][:, nt * 128:(nt + 1) * 128] \
                    .rearrange("(dt p) n -> p dt n", p=128)
                em.dma("gpsimd", WREG[:, wi * DM:(wi + 1) * DM]
                       .rearrange("p (dt n) -> p dt n", dt=DMT), src, sAW)
            em.waitf("tensor", sAW)
            last_evac = {}
            for sc_i in (list(range(0, NMAC, 2)) + list(range(1, NMAC, 2))):
                em.wait("sync", sCC, cfg["wait_cc_half"][sc_i % 2])
                em.wait("sync", sPE, sPE.n)
                em.wait("sync", sDVE, sDVE.n)
                base = (sc_i % 2) * NC * DM + (sc_i // 2) * DM
                xsrc = ln_full.ap()[base:base + DM, 0:512] \
                    .rearrange("(dt p) c -> p dt c", p=128)
                em.dma("sync", XBUF[:, :].rearrange("p (dt c) -> p dt c", dt=DMT),
                       xsrc, sX)
                if cfg["rope"]:
                    em.dma("sync", rope_c,
                           cosR.ap()[:, sc_i * 512:(sc_i + 1) * 512], sX)
                    em.dma("sync", rope_s,
                           sinS.ap()[:, sc_i * 512:(sc_i + 1) * 512], sX)
                em.waitf("tensor", sX)
                em.waitf("vector", sX)
                for wi, (nm, nt) in enumerate(plist):
                    half = wi % 2
                    acc = PA[:, half * 512:(half + 1) * 512]
                    prevev = last_evac.get(half)
                    if prevev is not None:
                        em.wait("tensor", prevev[0], prevev[1])
                    for dt_i in range(DMT):
                        em.mm(acc, WREG[:, wi * DM + dt_i * 128:
                                        wi * DM + dt_i * 128 + 128],
                              XBUF[:, dt_i * 512:(dt_i + 1) * 512],
                              start=(dt_i == 0), stop=(dt_i == DMT - 1),
                              sem=(sPE if dt_i == DMT - 1 else None))
                    mmv = sPE.n
                    em.wait("scalar", sPE, mmv)
                    if nm == "v":
                        vt = tb0[:, 0:512]
                        em.wait("vector", sPE, sPE.n)
                        if cfg["biased"]:
                            em.vec("tensor_scalar", vt, acc, bv_s[:, nt:nt + 1],
                                   None, ALU.add, sem=sDVE)
                        else:
                            em.vec("tensor_copy", vt, acc, sem=sDVE)
                        av = sDVE.n
                        last_evac[half] = (sDVE, av)
                        em.wait("tensor", sDVE, av)
                        for sub in range(4):
                            em.trans(PT[:, sub * 128:(sub + 1) * 128],
                                     vt[:, sub * 128:(sub + 1) * 128],
                                     ident_s[:, :],
                                     sem=(sPE if sub == 3 else None))
                        tv = sPE.n
                        em.wait("vector", sPE, tv)
                        em.waitf("vector", sVT)
                        em.vec("tensor_copy", vtr[:, :], PT[:, :], sem=sDVE)
                        em.wait("sync", sDVE, sDVE.n)
                        em.dma("sync",
                               dram_of["v"][sc_i * 512:(sc_i + 1) * 512,
                                            nt * 128:(nt + 1) * 128]
                               .rearrange("(sub s) d -> sub s d", sub=4)
                               .transpose([1, 0, 2]),
                               vtr[:, :].rearrange("p (sub d) -> p sub d", sub=4),
                               sVT)
                        continue
                    if cfg["rope"]:
                        a = t0[:, 0:512]
                        em.waitf("scalar", sSW)
                        em.wait("scalar", sDVE, sDVE.n)
                        em.actv(a, acc, AF.Copy, sem=sACT)
                        av = sACT.n
                        last_evac[half] = (sACT, av)
                        em.wait("sync", sACT, av)
                        em.wait("sync", sDVE, sDVE.n)
                        em.dma("sync", t1[0:64, 0:512], a[64:128, 0:512], sSW)
                        em.dma("sync", t1[64:128, 0:512], a[0:64, 0:512], sSW)
                        em.waitf("vector", sSW)
                        em.vec("tensor_mul", t0[:, 512:1024], a, rope_c)
                        em.vec("tensor_mul", t1[:, 512:1024], t1[:, 0:512], rope_s)
                        qk_t = tb0[:, 512:1024]
                        em.waitf("vector", sQKV)
                        em.vec("tensor_add", qk_t, t0[:, 512:1024],
                               t1[:, 512:1024], sem=sDVE)
                        dv = sDVE.n
                        em.wait("sync", sDVE, dv)
                        em.wait("scalar", sDVE, dv)
                    else:
                        bias_ap = (bq_s if nm == "q" else bk_s)[:, nt:nt + 1]
                        qk_t = tb0[:, 512:1024]
                        em.waitf("vector", sQKV)
                        em.wait("vector", sPE, sPE.n)
                        em.wait("vector", sACT, sACT.n)
                        em.vec("tensor_scalar", qk_t, acc, bias_ap, None, ALU.add,
                               sem=sDVE)
                        av = sDVE.n
                        last_evac[half] = (sDVE, av)
                        em.wait("sync", sDVE, av)
                    em.dma("sync", dram_of[nm][nt * 128:(nt + 1) * 128,
                                                 sc_i * 512:(sc_i + 1) * 512],
                           qk_t, sQKV)
                    em.wait("scalar", sPE, sPE.n)
                    em.wait("scalar", sDVE, sDVE.n)
                    em.actv(tb0[:, 1024:1536], qk_t, AF.Square, sem=sACT)
                    av = sACT.n
                    em.wait("tensor", sACT, av)
                    em.wait("tensor", sDVE, sDVE.n)
                    em.mm(STP[0:1, 0:512], onesc_b[:, :], tb0[:, 1024:1536],
                          start=True, stop=True, sem=sPE)
                    em.wait("vector", sPE, sPE.n)
                    dst_row = qn_s if nm == "q" else kn_s
                    em.vec("tensor_reduce",
                           dst_row[0:1, nt * 16 + sc_i:nt * 16 + sc_i + 1],
                           STP[0:1, 0:512], AXX, ALU.max, sem=sDVE)
                    if wi == 2 and cfg.get("interleave"):
                        cfg["interleave"]()
                if cfg.get("interleave"):
                    cfg["interleave"]()

        em.wait("vector", sDVE, sDVE.n)
        for j in range(nkt):
            em.vec("tensor_reduce", kn_s[0:1, 64 + j:65 + j],
                   kn_s[0:1, j * 16:(j + 1) * 16], AXX, ALU.max)
        if nkt == 1:
            em.vec("tensor_mul", cneg_s[0:1, :], qn_s[0:1, :],
                   kn_s[0:1, 64:65].broadcast_to([1, 64]), sem=sDVE)
        else:
            em.vec("tensor_mul",
                   cneg_s[0:1, :].rearrange("o (h m) -> o h m", h=4),
                   qn_s[0:1, :].rearrange("o (h m) -> o h m", h=4),
                   kn_s[0:1, 64:68].unsqueeze(2).broadcast_to([1, 4, 16]),
                   sem=sDVE)
        dv = sDVE.n
        em.wait("scalar", sDVE, dv)
        em.actv(rows[0:1, 0:64], cneg_s[0:1, :], AF.Ln)
        em.actv(rows[0:1, 64:128], rows[0:1, 0:64], AF.Exp, scale=0.5, sem=sACT)
        av = sACT.n
        em.wait("vector", sACT, av)
        em.vec("tensor_scalar", cneg_s[0:1, :], rows[0:1, 64:128], -SCALE, None,
               ALU.mult, sem=sDVE)
        dv = sDVE.n
        em.wait("tensor", sDVE, dv)
        em.mm(PC[:, 0:64], ones1[:, :], cneg_s[0:1, :], start=True, stop=True,
              sem=sPE)
        em.wait("vector", sPE, sPE.n)
        em.vec("tensor_copy", bias_all[:, :], PC[:, 0:64], sem=sDVE)
        em.wait("tensor", sDVE, sDVE.n)

        em.wait("gpsimd", sPE, sPE.n)
        wo_src = cfg["wo"].rearrange("(et p) f -> p et f", p=128)
        em.dma("gpsimd", wo_sb[:, :].rearrange("p (et f) -> p et f", et=4),
               wo_src, sWO)

        ones33 = R3[:, 16384:16640].bitcast(F32)
        em.vec("memset", ones33[32:33, 0:128], 1.0, sem=sDVE)
        em.waitf("sync", sQKV)
        em.waitf("sync", sVT)
        pslot_pv = [0, 0, 0, 0]
        prev_exp = [None]
        bctr = 0
        pend = [None]
        kv_done = {}

        def flush_pend():
            if pend[0] is None:
                return
            (fh, fh2, fkc, fslot0, fmi, fuse_sem, fuse_val, fpar2) = pend[0]
            pend[0] = None
            em.wait("tensor", fuse_sem, fuse_val)
            pq = p_sb[:, fslot0 * 1024: fslot0 * 1024 + 2048]
            first = (fkc == 0)
            last = (fkc == fmi)
            for let in range(4):
                em.mm(PC[32 * fh2:32 * fh2 + 1, 0:512], onesc_b[:, :],
                      pq[:, let * 512:(let + 1) * 512],
                      start=(first and let == 0), stop=(last and let == 3),
                      tile_position=(0, 32 * fh2))
            v = None
            for let in range(4):
                if cfg["per_head_kv"]:
                    vl = kv_v[:, fpar2 * 2048 + let * 512 + fh * 128:
                              fpar2 * 2048 + let * 512 + fh * 128 + 128]
                else:
                    vl = kv_v[:, fpar2 * 512 + let * 128:
                              fpar2 * 512 + let * 128 + 128]
                v = em.mm(PA[:, fh2 * 512:(fh2 + 1) * 512], vl,
                          pq[:, let * 512:(let + 1) * 512],
                          start=(first and let == 0), stop=(last and let == 3),
                          sem=sPE)
            pslot_pv[fslot0] = v
            pslot_pv[fslot0 + 1] = v
            if fh2 == 1:
                kv_done[fkc] = v

        macro_order = list(range(0, NMAC, 2)) + list(range(1, NMAC, 2))
        for mseq, mi in enumerate(macro_order):
            em.wait("sync", sPE, sPE.n)
            em.dma("sync", q_mac[:, :].rearrange("p (h c) -> p h c", h=4),
                   cfg["qT"][0:512, mi * 512:(mi + 1) * 512]
                   .rearrange("(h p) c -> p h c", p=128), sQ)
            phk = cfg["per_head_kv"]
            nslot = 2 if phk else 4
            swid = 2048 if phk else 512
            pre = 1 if phk else 2
            for hp in range(2):
                kv_done.clear()
                kv_mark = {}
                em.wait("sync", sPE, sPE.n)

                def issue_kv(kc, mi=mi):
                    sl = kc % nslot
                    g = kv_done.get(kc - nslot)
                    if g:
                        em.wait("sync", sPE, g)
                    base = sl * swid
                    if phk:
                        for hf in range(2):
                            ks = cfg["kT"][hf * 256:(hf + 1) * 256,
                                           kc * 512:(kc + 1) * 512] \
                                .rearrange("(h p) c -> p h c", p=128)
                            em.dma("sync",
                                   kv_k[:, base + hf * 1024:base + hf * 1024 + 1024]
                                   .rearrange("p (h c) -> p h c", h=2), ks,
                                   sKV[sl])
                            vs = cfg["vT"][kc * 512 + hf * 256:
                                           kc * 512 + (hf + 1) * 256, 0:512] \
                                .rearrange("(l p) d -> p l d", p=128)
                            em.dma("sync",
                                   kv_v[:, base + hf * 1024:base + hf * 1024 + 1024]
                                   .rearrange("p (l d) -> p l d", l=2), vs,
                                   sKV[sl])
                    else:
                        ks = cfg["kT"][0:128, kc * 512:(kc + 1) * 512]
                        vs = cfg["vT"][kc * 512:(kc + 1) * 512, 0:128] \
                            .rearrange("(l p) d -> p l d", p=128)
                        em.dma("sync", kv_k[:, base:base + 512], ks, sKV[sl])
                        em.dma("sync", kv_v[:, base:base + 512]
                               .rearrange("p (l d) -> p l d", l=4), vs, sKV[sl])
                    kv_mark[kc] = sKV[sl].n

                for kp in range(min(pre, mi + 1)):
                    issue_kv(kp)
                for kc in range(mi + 1):
                    par2 = kc % nslot
                    em.wait("tensor", sKV[par2], kv_mark[kc])
                    nxt = kc + pre
                    if nxt <= mi and nxt not in kv_mark:
                        if phk:
                            flush_pend()
                        issue_kv(nxt)
                    em.waitf("tensor", sQ)
                    for h2 in range(2):
                        h = hp * 2 + h2
                        slot0 = bctr % 4
                        bctr += 2
                        if prev_exp[0]:
                            em.wait("tensor", sACT, prev_exp[0])
                        qk = None
                        for let in range(4):
                            if cfg["per_head_kv"]:
                                lt = kv_k[:, par2 * 2048 + h * 512 + let * 128:
                                          par2 * 2048 + h * 512 + let * 128 + 128]
                            else:
                                lt = kv_k[:, par2 * 512 + let * 128:
                                          par2 * 512 + let * 128 + 128]
                            qk = em.mm(PBX[:, let * 512:(let + 1) * 512], lt,
                                       q_mac[:, h * 512:(h + 1) * 512],
                                       start=True, stop=True,
                                       sem=(sPE if let >= 1 else None))
                            if let == 1:
                                flush_pend()
                        em.wait("scalar", sPE, qk)
                        if pslot_pv[slot0]:
                            em.wait("scalar", sPE, pslot_pv[slot0])
                        pq = p_sb[:, slot0 * 1024: slot0 * 1024 + 2048]
                        em.actv(pq, PBX[:, :], AF.Exp,
                                bias=bias_all[:, h * 16 + mi:h * 16 + mi + 1],
                                scale=SCALE, sem=sACT)
                        ev = sACT.n
                        prev_exp[0] = ev
                        use_sem, use_val = sACT, ev
                        if kc == mi:
                            em.wait("vector", sACT, ev)
                            em.vec("tensor_mul", pq, pq,
                                   tri_s[:, 0:2048], sem=sDVE)
                            use_sem, use_val = sDVE, sDVE.n
                        pend[0] = (h, h2, kc, slot0, mi, use_sem, use_val,
                                   par2)
                flush_pend()
                pv_all = sPE.n
                em.wait("vector", sPE, pv_all)
                em.waitf("vector", sPRS[0])
                em.waitf("vector", sPRS[1])
                em.waitf("vector", sSW)
                em.vec("tensor_copy", den_s[0:1, :], PC[0:1, :])
                em.vec("tensor_copy", den_s[32:33, :], PC[32:33, :])
                em.vec("reciprocal", den_s[0:1, :], den_s[0:1, :])
                em.vec("reciprocal", den_s[32:33, :], den_s[32:33, :], sem=sDVE)
                rv = sDVE.n
                em.wait("tensor", sDVE, rv)
                for h2 in range(2):
                    h = hp * 2 + h2
                    dsrc = den_s[0:1, :] if h2 == 0 else den_s[32:33, :]
                    lsrc = ones1[:, :] if h2 == 0 else ones33[32:33, :]
                    em.wait("tensor", sDVE, sDVE.n)
                    em.mm(PB[:, 0:512], lsrc, dsrc,
                          start=True, stop=True, sem=sPE)
                    bv_ = sPE.n
                    em.wait("scalar", sPE, bv_)
                    em.wait("scalar", sDVE, sDVE.n)
                    em.actv(t0[:, 0:512], PA[:, h2 * 512:(h2 + 1) * 512], AF.Copy,
                            sem=sACT)
                    av = sACT.n
                    em.wait("vector", sACT, av)
                    em.wait("vector", sPE, bv_)
                    em.vec("tensor_mul", att_sb[:, h * 512:(h + 1) * 512],
                           t0[:, 0:512], PB[:, 0:512], sem=sDVE)
                em.wait("tensor", sDVE, sDVE.n)
                prev_exp[0] = None
                em.wait("tensor", sACT, sACT.n)
            em.waitf("tensor", sWO)
            op_evac = {}
            for ft in range(DMT):
                half = ft % 2
                bank = PB2[:, half * 512:half * 512 + 512]
                if ft >= 2:
                    em.wait("tensor", sACT, op_evac[ft - 2])
                for et in range(4):
                    em.mm(bank, wo_sb[:, (et * DMT + ft) * 128:
                                      (et * DMT + ft) * 128 + 128],
                          att_sb[:, et * 512:(et + 1) * 512],
                          start=(et == 0), stop=(et == 3),
                          sem=(sPE if et == 3 else None))
                pv = sPE.n
                em.wait("scalar", sPE, pv)
                em.waitf("scalar", sPRS[0])
                em.waitf("scalar", sPRS[1])
                em.actv(t1[:, half * 512:half * 512 + 512], bank, AF.Copy,
                        sem=sACT)
                op_evac[ft] = sACT.n
                em.wait("gpsimd", sACT, op_evac[ft])
                em.dma("gpsimd",
                       partial.ap()[(mi % 2) * NC * DM + (mi // 2) * DM + 128 * ft:
                                    (mi % 2) * NC * DM + (mi // 2) * DM + 128 * (ft + 1),
                                    0:512],
                       t1[:, half * 512:half * 512 + 512], sPRS[mi % 2])
            em.wait("tensor", sACT, sACT.n)
            em.wait("tensor", sDVE, sDVE.n)
            if mseq == NMAC // 2 - 1:
                em.waitf("gpsimd", sPRS[0])
                cfg["rs_out"].append(cfg["mk_rs"](0))
            elif mseq == NMAC - 1:
                em.waitf("gpsimd", sPRS[1])
                cfg["rs_out"].append(cfg["mk_rs"](1))

    def mk_rs(halfsel):
        return collective(
            "ReduceScatter", ALU.add,
            [partial.ap()[halfsel * NC * DM:(halfsel + 1) * NC * DM, :].opt()],
            [red.ap()[halfsel * DM:(halfsel + 1) * DM, :].opt()])

    def inter_deq():
        for _ in range(8):
            if next(deq, "DONE") == "DONE":
                break

    cfg_a = dict(qT=qaD.ap(), kT=kaD.ap(), vT=vaD.ap(), per_head_kv=False,
                 rope=True, biased=False, wq=waqT.ap(), wk=wakT.ap(),
                 wv=wavT.ap(), wo=waoT.ap(), kw=128,
                 wait_cc_half=(ag_half[0], ag_half[1]),
                 wreg=R2, xbuf=R3[:, 0:16384], statps=PB2,
                 interleave=inter_deq, mk_rs=mk_rs, rs_out=[])
    emit_attn(cfg_a)
    for _ in deq:
        pass
    rs1 = cfg_a["rs_out"][1]

    if stop_after == "attn_a":
        _finish(em, tap_pairs, sOUT, sCC)
        em.replay_into(nc)
        return nc

    def emit_ffn(cc, src_x, dst_x, dst_lnloc, is_last):
        col = cc * 512
        em.wait("sync", sPE, sPE.n)
        em.wait("tensor", sDVE, sDVE.n)
        for kt in range(DMT):
            em.mm(PC[0:GG, 0:512], ohxg_s[:, kt * GG:(kt + 1) * GG],
                  x_sb[:, kt * 512:(kt + 1) * 512], start=(kt == 0),
                  stop=(kt == DMT - 1))
        for kt in range(DMT):
            em.mm(PB2[0:GG, 0:512], ohxu_s[:, kt * GG:(kt + 1) * GG],
                  x_sb[:, kt * 512:(kt + 1) * 512], start=(kt == 0),
                  stop=(kt == DMT - 1), sem=(sPE if kt == DMT - 1 else None))
        pv = sPE.n
        em.wait("vector", sPE, pv)
        em.vec("tensor_copy", xg_s[:, 0:512], PC[0:GG, 0:512])
        em.vec("tensor_copy", xg_s[:, 512:1024], PB2[0:GG, 0:512], sem=sDVE)
        em.wait("tensor", sDVE, sDVE.n)

        wpar_done = {}
        hpar_done = {}
        for nt in range(NTI):
            parw = nt % 2
            r, ntl = divmod(nt, NTIL)
            woff = parw * (GBLK + UBLK)
            if nt >= 2:
                em.wait("sync", sPE, wpar_done[nt - 2])
            em.dma("sync", R3[:, woff:woff + GBLK],
                   wsg.ap()[r * 128:(r + 1) * 128, ntl * GBLK:(ntl + 1) * GBLK],
                   sW[parw])
            em.dma("sync", R3[:, woff + GBLK:woff + GBLK + UBLK],
                   wsu.ap()[r * 128:(r + 1) * 128, ntl * UBLK:(ntl + 1) * UBLK],
                   sW[parw])
            em.waitf("tensor", sW[parw])
            half = nt % 2
            if nt >= 2:
                em.wait("tensor", sACT, hpar_done[nt - 2][0])
                em.wait("tensor", sDVE, hpar_done[nt - 2][1])
            for gi_, base in ((0, woff), (1, woff + GBLK)):
                acc = (PA if gi_ == 0 else PB)[:, half * 512:(half + 1) * 512]
                for kt in range(DMT):
                    em.mm(acc, R3[:, base + kt * 128:base + kt * 128 + 128],
                          x_sb[:, kt * 512:(kt + 1) * 512], start=(kt == 0),
                          stop=False)
                em.mm(acc, R3[0:GG, base + DM:base + DM + 128],
                      xg_s[0:GG, gi_ * 512:gi_ * 512 + 512],
                      start=False, stop=True, sem=sPE)
            gu = sPE.n
            em.wait("scalar", sPE, gu)
            sil = tb0[:, half * 512:half * 512 + 512]
            em.actv(sil, PA[:, half * 512:(half + 1) * 512], AF.Silu, sem=sACT)
            av = sACT.n
            em.wait("vector", sACT, av)
            em.vec("tensor_mul", h_sb[:, nt * 512:(nt + 1) * 512], sil,
                   PB[:, half * 512:(half + 1) * 512], sem=sDVE)
            hv = sDVE.n
            hpar_done[nt] = (av, hv)
            em.wait("tensor", sDVE, hv)
            wpar_done[nt] = em.mm(
                PC[0:88, 0:512], R3[:, woff + DM + 128:woff + DM + 216],
                h_sb[:, nt * 512:(nt + 1) * 512], start=(nt == 0),
                stop=(nt == NTI - 1), sem=sPE)
        hgv = sPE.n
        em.wait("vector", sPE, hgv)
        em.vec("tensor_copy", hg_s[:, :], PC[0:88, 0:512], sem=sDVE)
        em.wait("tensor", sDVE, sDVE.n)

        em.wait("sync", sPE, sPE.n)
        em.wait("sync", sDVE, sDVE.n)
        em.wait("scalar", sDVE, sDVE.n)
        prev_stat2 = 0
        wd_done = {}
        dn_evac = {}
        slot_dve = {}
        for ft in range(DMT):
            parw = ft % 2
            wdoff = parw * (NC * DBLK)
            if ft >= 2:
                em.wait("sync", sPE, wd_done[ft - 2])
            for r in range(NC):
                em.dma("sync", R2[:, wdoff + r * DBLK: wdoff + (r + 1) * DBLK],
                       wsd.ap()[r * 128:(r + 1) * 128,
                                ft * DBLK:(ft + 1) * DBLK], sD[parw])
            em.waitf("tensor", sD[parw])
            bank = PB2[:, parw * 512:parw * 512 + 512]
            if ft >= 2:
                em.wait("tensor", sACT, dn_evac[ft - 2])
            for kt in range(NTI):
                r2, kl = divmod(kt, NTIL)
                em.mm(bank, R2[:, wdoff + r2 * DBLK + kl * 128:
                               wdoff + r2 * DBLK + kl * 128 + 128],
                      h_sb[:, kt * 512:(kt + 1) * 512], start=(kt == 0),
                      stop=False)
            em.mm(bank, R2[0:GD, wdoff + SHN: wdoff + SHN + 128],
                  hg_s[0:GD, 0:512], start=False, stop=True, sem=sPE)
            dv_ = sPE.n
            wd_done[ft] = dv_
            em.wait("scalar", sPE, dv_)
            rsl = ft % 3
            a = t0[:, rsl * 512:(rsl + 1) * 512]
            if ft >= 3:
                em.waitf("scalar", sST[rsl])
                if is_last:
                    em.waitf("scalar", sOUT)
                em.wait("sync", sDVE, slot_dve.get(ft - 3, 0))
            em.actv(a, bank, AF.Copy, sem=sACT)
            dn_evac[ft] = sACT.n
            b = t1[:, rsl * 512:(rsl + 1) * 512]
            em.dma("sync", b, src_x[128 * ft:128 * (ft + 1), col:col + 512],
                   sTL[rsl])
            em.waitf("vector", sTL[rsl])
            em.wait("vector", sACT, dn_evac[ft])
            em.vec("tensor_add", a, a, b, sem=sDVE)
            mv = sDVE.n
            slot_dve[ft] = mv
            em.wait("gpsimd", sDVE, mv)
            if is_last:
                em.dma("gpsimd", out_ext.ap()[128 * ft:128 * (ft + 1),
                                              col:col + 512], a, sOUT)
            else:
                em.dma("gpsimd", dst_x[128 * ft:128 * (ft + 1), col:col + 512], a,
                       sST[rsl])
            if dst_lnloc is not None:
                xb = tb0[:, 1024:1536]
                em.wait("scalar", sPE, prev_stat2)
                em.wait("scalar", sDVE, mv)
                em.actv(xb, a, AF.Copy, sem=sACT)
                av2 = sACT.n
                em.wait("tensor", sACT, av2)
                em.mm(PA[0:1, 0:512], onesc_b[:, :], xb, start=(ft == 0),
                      stop=(ft == DMT - 1), sem=sPE)
                em.wait("scalar", sPE, sPE.n)
                em.actv(xb, a, AF.Square, sem=sACT)
                av3 = sACT.n
                em.wait("tensor", sACT, av3)
                prev_stat2 = em.mm(PA[0:1, 512:1024], onesc_b[:, :], xb,
                                   start=(ft == 0), stop=(ft == DMT - 1), sem=sPE)
        if dst_lnloc is None:
            return
        pe_done = sPE.n
        em.wait("vector", sPE, pe_done)
        em.vec("tensor_scalar", rows[0:1, 0:512], PA[0:1, 0:512], 1.0 / DM, None,
               ALU.mult)
        em.vec("tensor_scalar", rows[0:1, 512:1024], PA[0:1, 512:1024], 1.0 / DM,
               None, ALU.mult)
        em.vec("tensor_mul", rows[0:1, 1024:1536], rows[0:1, 0:512],
               rows[0:1, 0:512])
        em.vec("tensor_sub", rows[0:1, 512:1024], rows[0:1, 512:1024],
               rows[0:1, 1024:1536], sem=sDVE)
        vv = sDVE.n
        em.wait("scalar", sDVE, vv)
        em.actv(rows[0:1, 1024:1536], rows[0:1, 512:1024], AF.Ln,
                bias=eps_c[0:1, :])
        em.actv(rows[0:1, 512:1024], rows[0:1, 1024:1536], AF.Exp, scale=-0.5,
                sem=sACT)
        av = sACT.n
        em.wait("vector", sACT, av)
        em.vec("tensor_mul", rows[0:1, 1024:1536], rows[0:1, 0:512],
               rows[0:1, 512:1024], sem=sDVE)
        mrv = sDVE.n
        em.wait("tensor", sDVE, mrv)
        em.mm(PB[:, 0:512], ones1[:, :], rows[0:1, 512:1024], start=True, stop=True)
        em.mm(PB[:, 512:1024], ones1[:, :], rows[0:1, 1024:1536], start=True,
              stop=True, sem=sPE)
        bcv = sPE.n
        for i in range(3):
            em.waitf("sync", sST[i])
        em.waitf("vector", sLNS)
        em.wait("sync", sDVE, sDVE.n)
        em.wait("sync", sACT, sACT.n)
        slot_dve2 = {}
        for ft in range(DMT):
            rsl = ft % 3
            a = t0[:, rsl * 512:(rsl + 1) * 512]
            if ft >= 3:
                em.wait("sync", sDVE, slot_dve2.get(ft - 3, 0))
            em.dma("sync", a, dst_x[128 * ft:128 * (ft + 1), col:col + 512],
                   sTL[rsl])
            em.waitf("vector", sTL[rsl])
            em.wait("vector", sPE, bcv)
            b = t1[:, rsl * 512:(rsl + 1) * 512]
            em.vec("tensor_mul", b, a, PB[:, 0:512])
            ln_t = x_sb[:, ft * 512:(ft + 1) * 512]
            em.vec("tensor_sub", ln_t, b, PB[:, 512:1024], sem=sDVE)
            slot_dve2[ft] = sDVE.n
            em.wait("gpsimd", sDVE, sDVE.n)
            em.dma("gpsimd", dst_lnloc[128 * ft:128 * (ft + 1), 0:512],
                   ln_t, sLNS)

    em.wait("tensor", sCC, agd_d["agu"])
    em.wait("sync", sCC, agd_d["agd"])
    ag2_half = {}
    for cc in range(NCH):
        rsw = cfg_a["rs_out"][cc]
        em.wait("sync", sCC, rsw)
        em.wait("vector", sCC, rsw)
        emit_merge_ln_chunk(cc, x0T.ap(),
                            red.ap()[cc * DM:(cc + 1) * DM, :], xA.ap(), None,
                            None)
        em.wait("gpsimd", sCC, ag_half[1])
        emit_ffn(cc, xA.ap(), xB.ap(),
                 ln_loc.ap()[cc * DM:(cc + 1) * DM, :], is_last=False)
        em.waitf("gpsimd", sLNS)
        ag2_half[cc] = mk_ag_ln(cc)
    ag2a, ag2b = ag2_half[0], ag2_half[1]

    if stop_after == "ffn1":
        _finish(em, tap_pairs, sOUT, sCC)
        em.replay_into(nc)
        return nc

    cfg_b = dict(qT=qbD.ap(), kT=kbD.ap(), vT=vbD.ap(), per_head_kv=True,
                 rope=False, biased=True, wq=wbqT.ap(), wk=wbkT.ap(),
                 wv=wbvT.ap(), wo=wboT.ap(), kw=512,
                 wait_cc_half=(ag2a, ag2b),
                 wreg=R1, xbuf=x_sb, statps=PC,
                 interleave=None, mk_rs=mk_rs, rs_out=[])
    emit_attn(cfg_b)
    rs2 = cfg_b["rs_out"][1]

    for cc in range(NCH):
        rsw = cfg_b["rs_out"][cc]
        em.wait("sync", sCC, rsw)
        em.wait("vector", sCC, rsw)
        emit_merge_ln_chunk(cc, xB.ap(),
                            red.ap()[cc * DM:(cc + 1) * DM, :], xA.ap(), None,
                            bo_s)
        emit_ffn(cc, xA.ap(), None, None, is_last=True)

    _finish(em, tap_pairs, sOUT, sCC)
    em.replay_into(nc)
    return nc


def _finish(em, tap_pairs, sOUT, sCC):
    if sCC.n:
        em.wait("gpsimd", sCC, sCC.n)
    for tout, tsrc in tap_pairs:
        em.dma("gpsimd", tout.ap(), tsrc.ap(), sOUT)
    em.waitf("gpsimd", sOUT)
    em.waitf("sync", sOUT)


def kernel(**inputs):
    in_maps = _host_prep(inputs)
    if "nc" not in _NC_CACHE:
        _NC_CACHE["nc"] = build_nc()
    nc = _NC_CACHE["nc"]
    res = run_bass_kernel_spmd(nc, in_maps, core_ids=list(range(NC)))
    outs = [res.results[c]["out"] for c in range(NC)]
    full = np.concatenate([o.T for o in outs], axis=0)
    return full[None, :, :].astype(np.float32, copy=False)


# version 08
# speedup vs baseline: 1.1052x; 1.1052x over previous
import numpy as np
import ml_dtypes
from contextlib import ExitStack

import concourse.bass as bass
import concourse.mybir as mybir
from concourse.bass_utils import run_bass_kernel_spmd

BF16 = mybir.dt.bfloat16
F32 = mybir.dt.float32
U8 = mybir.dt.uint8
I32 = mybir.dt.int32
AF = mybir.ActivationFunctionType
ALU = mybir.AluOpType
AXX = mybir.AxisListType.X
NBF = ml_dtypes.bfloat16

NC = 8
S = 8192
DM = 4096
LS = S // NC
NCH = LS // 512
INTER_P = 11264
NTI = INTER_P // 128
NTIL = NTI // NC
SHN = INTER_P // NC
GG = 32
GD = 86
SCALE = 1.0 / float(np.sqrt(128.0))
EPS = 1e-5
NMAC = S // 512
DMT = DM // 128

GBLK = DM + 128 + 88 + 40
UBLK = DM + 128
DBLK = SHN + 128

ENGS = ("sync", "gpsimd", "vector", "scalar", "tensor")

_NC_CACHE = {}


def _bytes_view(qw):
    kk8, n = qw.shape
    b = qw.view('<u4').view(np.uint8).reshape(kk8, n, 4).transpose(0, 2, 1)
    return np.ascontiguousarray(b.reshape(kk8 * 4, n))


def _host_prep(inp):
    x = np.asarray(inp["L0_attn_a_x"])[0]
    xT = np.ascontiguousarray(x.T)

    w_inT = np.ascontiguousarray(np.asarray(inp["L0_attn_a_w_in"]).T)
    w_outT = np.ascontiguousarray(np.asarray(inp["L0_attn_a_w_out"]).T)
    WqT = np.ascontiguousarray(np.asarray(inp["L0_attn_b_Wq"]).T)
    WkT = np.ascontiguousarray(np.asarray(inp["L0_attn_b_Wk"]).T)
    WvT = np.ascontiguousarray(np.asarray(inp["L0_attn_b_Wv"]).T)
    WoT = np.ascontiguousarray(np.asarray(inp["L0_attn_b_Wo"]).T)
    bq = np.asarray(inp["L0_attn_b_bq"]); bk = np.asarray(inp["L0_attn_b_bk"])
    bv = np.asarray(inp["L0_attn_b_bv"]); bo = np.asarray(inp["L0_attn_b_bo"])

    qb_g = _bytes_view(np.asarray(inp["L0_ffn_gate_qweight"]))
    qb_u = _bytes_view(np.asarray(inp["L0_ffn_up_qweight"]))
    qb_d = _bytes_view(np.asarray(inp["L0_ffn_down_qweight"]))
    sc_g = np.asarray(inp["L0_ffn_gate_scales"]).astype(np.float32, copy=False)
    sc_u = np.asarray(inp["L0_ffn_up_scales"]).astype(np.float32, copy=False)
    sc_d = np.asarray(inp["L0_ffn_down_scales"]).astype(np.float32, copy=False)
    qz_g = np.asarray(inp["L0_ffn_gate_qzeros"]).astype(np.int32, copy=False)
    qz_u = np.asarray(inp["L0_ffn_up_qzeros"]).astype(np.int32, copy=False)
    qz_d = np.asarray(inp["L0_ffn_down_qzeros"]).astype(np.int32, copy=False)
    gi_g = np.asarray(inp["L0_ffn_gate_g_idx"]).astype(np.int32)
    gi_u = np.asarray(inp["L0_ffn_up_g_idx"]).astype(np.int32)
    gi_d = np.asarray(inp["L0_ffn_down_g_idx"]).astype(np.int32)

    gi_d_p = np.full(INTER_P, -1, np.int32)
    for r in range(NC):
        gi_d_p[r * SHN: r * SHN + 1376] = gi_d[r * 1376:(r + 1) * 1376]

    inv_freq = 1.0 / (10000.0 ** (np.arange(0, 128, 2, dtype=np.float32) / 128.0))
    ang = np.arange(S, dtype=np.float32)[:, None] * inv_freq[None, :]
    cos = np.cos(ang).astype(np.float32).T
    sin = np.sin(ang).astype(np.float32).T
    cosR = np.ascontiguousarray(np.concatenate([cos, cos], axis=0))
    sinS = np.ascontiguousarray(np.concatenate([-sin, sin], axis=0))

    tri = np.ascontiguousarray(
        (np.arange(512)[:, None] <= np.arange(512)[None, :]).astype(NBF))
    iota128x88 = np.ascontiguousarray(
        np.broadcast_to(np.arange(88, dtype=np.float32), (128, 88)))
    iota88c = np.ascontiguousarray(np.arange(88, dtype=np.float32)[:, None])
    ident = np.ascontiguousarray(np.eye(128, dtype=NBF))

    def col4(v):
        return np.ascontiguousarray(v.reshape(4, 128).T)

    in_maps = []
    for c in range(NC):
        qr = slice(512 * c, 512 * (c + 1))
        m = {
            "x0T": np.ascontiguousarray(xT[:, LS * c: LS * (c + 1)]),
            "waqT": np.ascontiguousarray(w_inT[:, qr]),
            "wakT": np.ascontiguousarray(w_inT[:, 4096 + 128 * c: 4096 + 128 * (c + 1)]),
            "wavT": np.ascontiguousarray(w_inT[:, 5120 + 128 * c: 5120 + 128 * (c + 1)]),
            "waoT": np.ascontiguousarray(w_outT[qr, :]),
            "wbqT": np.ascontiguousarray(WqT[:, qr]),
            "wbkT": np.ascontiguousarray(WkT[:, qr]),
            "wbvT": np.ascontiguousarray(WvT[:, qr]),
            "wboT": np.ascontiguousarray(WoT[qr, :]),
            "bq4": col4(bq[qr]), "bk4": col4(bk[qr]), "bv4": col4(bv[qr]),
            "bo32": np.ascontiguousarray(bo.reshape(DMT, 128).T),
            "qbg": np.ascontiguousarray(
                np.pad(qb_g[:, 1376 * c:1376 * (c + 1)], ((0, 0), (0, 32)))),
            "qbu": np.ascontiguousarray(
                np.pad(qb_u[:, 1376 * c:1376 * (c + 1)], ((0, 0), (0, 32)))),
            "qbd": np.ascontiguousarray(
                np.pad(qb_d[688 * c:688 * (c + 1), :], ((0, 16), (0, 0)))),
            "scg_sh": np.ascontiguousarray(
                np.pad(sc_g[:, 1376 * c:1376 * (c + 1)], ((0, 0), (0, 32)))),
            "scu_sh": np.ascontiguousarray(
                np.pad(sc_u[:, 1376 * c:1376 * (c + 1)], ((0, 0), (0, 32)))),
            "scd": sc_d,
            "qzg_sh": np.ascontiguousarray(
                np.pad(qz_g[:, 172 * c:172 * (c + 1)], ((0, 0), (0, 4)))),
            "qzu_sh": np.ascontiguousarray(
                np.pad(qz_u[:, 172 * c:172 * (c + 1)], ((0, 0), (0, 4)))),
            "qzd": qz_d,
            "gig": np.ascontiguousarray(gi_g[:, None]),
            "giu": np.ascontiguousarray(gi_u[:, None]),
            "gidp": np.ascontiguousarray(gi_d_p[:, None]),
            "gidsh": np.ascontiguousarray(gi_d_p[SHN * c: SHN * (c + 1)])[:, None],
            "cosR": cosR, "sinS": sinS, "tri": tri,
            "iota88": iota128x88, "iota88c": iota88c,
            "ident": ident,
        }
        in_maps.append(m)
    return in_maps


class Sem:
    __slots__ = ("h", "n")

    def __init__(self, handle):
        self.h = handle
        self.n = 0

    def bump(self, k):
        self.n += k
        return self.n


class Em:
    def __init__(self, nc):
        self.nc = nc
        self.q = {e: [] for e in ENGS}
        self.ctx = ExitStack()
        self.auto = {}

    def sem(self, name):
        return Sem(self.ctx.enter_context(self.nc.semaphore(name)))

    def sb(self, name, shape, dtype):
        return self.ctx.enter_context(self.nc.sbuf_tensor(name, shape, dtype))

    def psum(self, name, shape, dtype=F32):
        return self.ctx.enter_context(self.nc.psum_tensor(name, shape, dtype))

    def emit(self, eng, fn):
        self.q[eng].append(fn)

    def dma(self, eng, out, in_, sem, **kw):
        v = sem.bump(16)
        self.emit(eng, lambda e, o=out, i=in_, s=sem.h, kw=kw:
                  e.dma_start(out=o, in_=i, **kw).then_inc(s, 16))
        return v

    def wait(self, eng, sem, val):
        assert val <= sem.n
        if val > 0:
            self.emit(eng, lambda e, s=sem.h, vv=val: e.wait_ge(s, vv))

    def waitf(self, eng, sem):
        v = sem.n
        if v > 0:
            self.emit(eng, lambda e, s=sem.h, vv=v: e.wait_ge(s, vv))

    def mm(self, out, lhsT, rhs, start, stop, sem=None, tile_position=None):
        v = sem.bump(1) if sem else None
        import inspect
        fr = inspect.currentframe().f_back
        tag = f"mm@{fr.f_lineno}"
        def f(t, o=out, l=lhsT, r=rhs, st=start, sp=stop, s=sem, tp=tile_position,
              tag=tag):
            i = t.matmul(o, l, r, start=st, stop=sp, tile_position=tp)
            i.annotate(tag)
            if s is not None:
                i.then_inc(s.h, 1)
        self.emit("tensor", f)
        return v

    def trans(self, out, in_, ident, sem=None):
        v = sem.bump(1) if sem else None
        def f(t, o=out, i=in_, d=ident, s=sem):
            ins = t.transpose(o, i, d)
            if s is not None:
                ins.then_inc(s.h, 1)
        self.emit("tensor", f)
        return v

    def actv(self, out, in_, func, sem=None, bias=0.0, scale=1.0):
        s = self.auto["scalar"]
        pre = s.n
        v = s.bump(1)
        def f(a, o=out, i=in_, fu=func, b=bias, sc=scale, pre=pre, s=s):
            if pre > 0:
                a.wait_ge(s.h, pre)
            a.activation(o, i, fu, bias=b, scale=sc).then_inc(s.h, 1)
        self.emit("scalar", f)
        return v

    def vec(self, fn_name, *args, sem=None, **kw):
        s = self.auto["vector"]
        pre = s.n
        v = s.bump(1)
        def f(e, a=args, kw=kw, fn=fn_name, pre=pre, s=s):
            if pre > 0:
                e.wait_ge(s.h, pre)
            getattr(e, fn)(*a, **kw).then_inc(s.h, 1)
        self.emit("vector", f)
        return v

    def replay_into(self, nc):
        with nc.Block() as block:
            for name in ENGS:
                ops = self.q[name]
                def runner(e, ops=ops):
                    for op in ops:
                        op(e)
                getattr(block, name)(runner)


def build_nc(taps=(), stop_after=None):
    nc = bass.Bass("TRN2", target_bir_lowering=False, debug=False)

    def din(name, shape, dtype):
        return nc.dram_tensor(name, shape, dtype, kind="ExternalInput")

    x0T = din("x0T", [DM, LS], F32)
    waqT = din("waqT", [DM, 512], F32); wakT = din("wakT", [DM, 128], F32)
    wavT = din("wavT", [DM, 128], F32); waoT = din("waoT", [512, DM], F32)
    wbqT = din("wbqT", [DM, 512], F32); wbkT = din("wbkT", [DM, 512], F32)
    wbvT = din("wbvT", [DM, 512], F32); wboT = din("wboT", [512, DM], F32)
    bq4 = din("bq4", [128, 4], F32); bk4 = din("bk4", [128, 4], F32)
    bv4 = din("bv4", [128, 4], F32); bo32 = din("bo32", [128, DMT], F32)
    qbg = din("qbg", [2048, SHN], U8); qbu = din("qbu", [2048, SHN], U8)
    qbd = din("qbd", [704, DM], U8)
    scg_sh = din("scg_sh", [GG, SHN], F32); scu_sh = din("scu_sh", [GG, SHN], F32)
    scd = din("scd", [GD, DM], F32)
    qzg_sh = din("qzg_sh", [GG, SHN // 8], I32)
    qzu_sh = din("qzu_sh", [GG, SHN // 8], I32)
    qzd = din("qzd", [GD, 512], I32)
    gig = din("gig", [DM, 1], I32); giu = din("giu", [DM, 1], I32)
    gidp = din("gidp", [INTER_P, 1], I32); gidsh = din("gidsh", [SHN, 1], I32)
    cosR = din("cosR", [128, S], F32); sinS = din("sinS", [128, S], F32)
    tri = din("tri", [512, 512], BF16)
    iota88 = din("iota88", [128, 88], F32); iota88c = din("iota88c", [88, 1], F32)
    ident = din("ident", [128, 128], BF16)

    out_ext = nc.dram_tensor("out", [DM, LS], F32, kind="ExternalOutput")

    ln_loc = nc.dram_tensor("ln_loc", [2 * DM, 512], BF16)
    ln_full = nc.dram_tensor("ln_full", [2 * NC * DM, 512], BF16)
    partial = nc.dram_tensor("partial", [2 * NC * DM, 512], F32)
    red = nc.dram_tensor("red", [2 * DM, 512], F32)
    xA = nc.dram_tensor("xA", [DM, LS], F32)
    xB = nc.dram_tensor("xB", [DM, LS], F32)
    qaD = nc.dram_tensor("qaD", [512, S], BF16)
    kaD = nc.dram_tensor("kaD", [128, S], BF16)
    vaD = nc.dram_tensor("vaD", [S, 128], BF16)
    qbD = nc.dram_tensor("qbD", [512, S], BF16)
    kbD = nc.dram_tensor("kbD", [512, S], BF16)
    vbD = nc.dram_tensor("vbD", [S, 512], BF16)
    wsg_sh = nc.dram_tensor("wsg_sh", [128, NTIL * GBLK], BF16)
    wsu_sh = nc.dram_tensor("wsu_sh", [128, NTIL * UBLK], BF16)
    wsd_sh = nc.dram_tensor("wsd_sh", [128, DMT * DBLK], BF16)
    wsg = nc.dram_tensor("wsg", [NC * 128, NTIL * GBLK], BF16)
    wsu = nc.dram_tensor("wsu", [NC * 128, NTIL * UBLK], BF16)
    wsd = nc.dram_tensor("wsd", [NC * 128, DMT * DBLK], BF16)

    internal = {"ln_loc": ln_loc, "ln_full": ln_full, "partial": partial,
                "red": red, "xA": xA, "xB": xB, "qaD": qaD, "kaD": kaD,
                "vaD": vaD, "qbD": qbD, "kbD": kbD, "vbD": vbD,
                "wsg_sh": wsg_sh, "wsu_sh": wsu_sh, "wsd_sh": wsd_sh,
                "wsg": wsg, "wsu": wsu, "wsd": wsd}
    tap_pairs = []
    for tname in taps:
        t = internal[tname]
        tap_pairs.append((nc.dram_tensor("tap_" + tname, list(t.ap().shape),
                                         t.ap().dtype, kind="ExternalOutput"), t))

    em = Em(nc)

    R1 = em.sb("R1", [128, 45056], BF16)
    R2 = em.sb("R2", [128, 24576], BF16)
    R3 = em.sb("R3", [128, 17152], BF16)

    h_sb = R1[:, 0:45056]
    x_sb = R2[:, 0:16384]
    wo_sb = R2[:, 0:16384]

    dq_u8 = R1[:, 0:2048].bitcast(U8)
    dq_f = R1[:, 2048:10240].bitcast(F32)
    dq_b = R1[:, 10240:14336]
    zz_i = R1[:, 14336:23552].bitcast(I32)
    zz_f = R1[:, 23552:31744].bitcast(F32)
    zz_n = R1[:, 31744:39936].bitcast(F32)
    sc_bf = R1[:, 39936:44032]

    kv_k = R3[:, 0:4096]
    kv_v = R3[:, 4096:8192]
    q_mac = R3[:, 8192:10240]
    p_sb = R3[:, 10240:14336]
    att_sb = R3[:, 14336:16384]

    ones1 = em.sb("ones1", [1, 128], F32)
    onesc_b = em.sb("onesc_b", [128, 1], BF16)
    ident_s = em.sb("ident_s", [128, 128], BF16)
    tri_s = em.sb("tri_s", [128, 2048], BF16)
    iota_s = em.sb("iota_s", [128, 88], F32)
    iotac_s = em.sb("iotac_s", [88, 1], F32)
    ohxg_s = em.sb("ohxg_s", [128, DMT * GG], BF16)
    ohxu_s = em.sb("ohxu_s", [128, DMT * GG], BF16)
    bias_all = em.sb("bias_all", [128, 64], F32)
    bq_s = em.sb("bq_s", [128, 4], F32)
    bk_s = em.sb("bk_s", [128, 4], F32)
    bv_s = em.sb("bv_s", [128, 4], F32)
    bo_s = em.sb("bo_s", [128, DMT], F32)
    xg_s = em.sb("xg_s", [GG, 1024], BF16)
    hg_s = em.sb("hg_s", [88, 512], BF16)
    dq_gi = em.sb("dq_gi", [88, 256], I32)
    dq_gf = em.sb("dq_gf", [88, 256], F32)
    dq_ohT = em.sb("dq_ohT", [88, 256], BF16)
    gcol_i = em.sb("gcol_i", [128, 1], I32)
    gcol_f = em.sb("gcol_f", [128, 1], F32)
    oh1 = em.sb("oh1", [128, 88], BF16)

    t0 = em.sb("t0", [128, 1536], F32)
    t1 = em.sb("t1", [128, 1536], F32)
    tb0 = em.sb("tb0", [128, 1536], BF16)
    vtr = em.sb("vtr", [128, 512], BF16)
    rope_c = t0[:, 1024:1536]
    rope_s = t1[:, 1024:1536]
    den_s = t1[:, 0:512]
    rows = em.sb("rows", [1, 1536], F32)
    qn_s = em.sb("qn_s", [1, 64], F32)
    eps_c = em.sb("eps_c", [1, 1], F32)
    kn_s = em.sb("kn_s", [1, 80], F32)
    cneg_s = em.sb("cneg_s", [1, 64], F32)

    PA = em.psum("PA", [128, 1024])
    PB = em.psum("PB", [128, 1024])
    PB2 = em.psum("PB2", [128, 1024])
    PC = em.psum("PC", [128, 512])
    PT = em.psum("PT", [128, 512], BF16)

    sPE = em.sem("sPE"); sACT = em.sem("sACT"); sDVE = em.sem("sDVE")
    em.auto = {"vector": sDVE, "scalar": sACT}
    sCC = em.sem("sCC")
    sCN = em.sem("sCN")
    sTL = [em.sem(f"sTL{i}") for i in range(3)]
    sST = [em.sem(f"sST{i}") for i in range(3)]
    sX = em.sem("sX"); sQ = em.sem("sQ")
    sKV = [em.sem("sKV0"), em.sem("sKV1"), em.sem("sKV2"), em.sem("sKV3")]
    sW = [em.sem("sWg0"), em.sem("sWg1")]
    sD = [em.sem("sWd0"), em.sem("sWd1")]
    sPRS = [em.sem("sPRS0"), em.sem("sPRS1")]
    sDQ = em.sem("sDQ"); sDQ2 = em.sem("sDQ2"); sSC = em.sem("sSC"); sAW = em.sem("sAW")
    sWO = em.sem("sWO"); sSW = em.sem("sSW"); sSW2 = em.sem("sSW2"); sVT = em.sem("sVT")
    sQKV = em.sem("sQKV"); sWS = em.sem("sWS"); sWS2 = em.sem("sWS2"); sLNS = em.sem("sLNS")
    sOUT = em.sem("sOUT")

    rg = [list(range(NC))]

    def collective(kind, op, ins, outs):
        v = sCC.bump(1)
        def f(g, k=kind, o=op, i=ins, u=outs):
            g.collective_compute(k, o, replica_groups=rg, ins=i, outs=u).then_inc(sCC.h, 1)
        em.emit("gpsimd", f)
        return v

    em.dma("sync", ident_s[:, :], ident.ap(), sCN)
    for l in range(4):
        em.dma("sync", tri_s[:, l * 512:(l + 1) * 512],
               tri.ap()[128 * l:128 * (l + 1), :], sCN)
    em.dma("sync", iota_s[:, :], iota88.ap(), sCN)
    em.dma("sync", iotac_s[:, :], iota88c.ap(), sCN)
    em.dma("sync", bq_s[:, :], bq4.ap(), sCN)
    em.dma("sync", bk_s[:, :], bk4.ap(), sCN)
    em.dma("sync", bv_s[:, :], bv4.ap(), sCN)
    em.dma("sync", bo_s[:, :], bo32.ap(), sCN)
    for e in ("vector", "scalar", "tensor", "gpsimd"):
        em.waitf(e, sCN)
    em.vec("memset", ones1[:, :], 1.0)
    em.vec("memset", eps_c[:, :], EPS)
    em.vec("memset", tb0[:, 0:1024], 0.0)
    em.vec("memset", onesc_b[:, :], 1.0, sem=sDVE)
    consts_v = sDVE.n
    em.wait("tensor", sDVE, consts_v)
    em.wait("scalar", sDVE, consts_v)
    em.wait("gpsimd", sDVE, consts_v)
    for shd, w in ((wsg_sh, NTIL * GBLK), (wsu_sh, NTIL * UBLK),
                   (wsd_sh, DMT * DBLK)):
        w1 = (w // 1024) * 1024
        em.dma("gpsimd", shd.ap()[:, 0:w1].rearrange("p (r c) -> p r c", c=1024),
               tb0[:, 0:1024].unsqueeze(1).broadcast_to([128, w1 // 1024, 1024]),
               sWS)
        if w1 < w:
            em.dma("gpsimd", shd.ap()[:, w1:w], tb0[:, 0:w - w1], sWS)
    em.waitf("gpsimd", sWS)
    em.waitf("scalar", sWS)
    em.waitf("vector", sWS)

    def emit_merge_ln_chunk(cc, src_x, src_red, dst_x, dst_lnloc, bias_cols):
        col = cc * 512
        em.waitf("vector", sLNS)
        em.wait("vector", sPE, sPE.n)
        em.wait("sync", sDVE, sDVE.n)
        em.wait("sync", sACT, sACT.n)
        em.waitf("sync", sOUT)
        em.waitf("sync", sPRS[0])
        em.waitf("sync", sPRS[1])
        slot_act = {}
        slot_dve = {}
        prev_stat = 0
        for ft in range(DMT):
            r = ft % 3
            a = t0[:, r * 512:(r + 1) * 512]
            if ft >= 3:
                em.waitf("sync", sST[r])
                em.wait("sync", sACT, slot_act.get(ft - 3, 0))
                em.wait("sync", sDVE, slot_dve.get(ft - 3, 0))
            em.dma("sync", a, src_x[128 * ft:128 * (ft + 1), col:col + 512], sTL[r])
            if src_red is not None:
                b = t1[:, r * 512:(r + 1) * 512]
                em.dma("sync", b, src_red[128 * ft:128 * (ft + 1), 0:512],
                       sTL[r])
                em.waitf("vector", sTL[r])
                if bias_cols is not None:
                    em.vec("tensor_scalar", b, b, bias_cols[:, ft:ft + 1], None,
                           ALU.add)
                em.vec("tensor_add", a, a, b, sem=sDVE)
                mv = sDVE.n
                slot_dve[ft] = mv
                em.wait("gpsimd", sDVE, mv)
                em.dma("gpsimd", dst_x[128 * ft:128 * (ft + 1), col:col + 512], a,
                       sST[r])
                em.wait("scalar", sDVE, mv)
            else:
                em.waitf("scalar", sTL[r])
            xb = tb0[:, 0:512]
            sq = tb0[:, 512:1024]
            em.wait("scalar", sPE, prev_stat)
            em.actv(xb, a, AF.Copy)
            em.actv(sq, a, AF.Square, sem=sACT)
            av = sACT.n
            slot_act[ft] = av
            em.wait("tensor", sACT, av)
            em.mm(PA[0:1, 0:512], onesc_b[:, :], xb, start=(ft == 0),
                  stop=(ft == DMT - 1))
            prev_stat = em.mm(PA[0:1, 512:1024], onesc_b[:, :], sq, start=(ft == 0),
                              stop=(ft == DMT - 1), sem=sPE)
        pe_done = sPE.n
        em.wait("vector", sPE, pe_done)
        em.vec("tensor_scalar", rows[0:1, 0:512], PA[0:1, 0:512], 1.0 / DM, None,
               ALU.mult)
        em.vec("tensor_scalar", rows[0:1, 512:1024], PA[0:1, 512:1024], 1.0 / DM,
               None, ALU.mult)
        em.vec("tensor_mul", rows[0:1, 1024:1536], rows[0:1, 0:512],
               rows[0:1, 0:512])
        em.vec("tensor_sub", rows[0:1, 512:1024], rows[0:1, 512:1024],
               rows[0:1, 1024:1536], sem=sDVE)
        vv = sDVE.n
        em.wait("scalar", sDVE, vv)
        em.actv(rows[0:1, 1024:1536], rows[0:1, 512:1024], AF.Ln,
                bias=eps_c[0:1, :])
        em.actv(rows[0:1, 512:1024], rows[0:1, 1024:1536], AF.Exp, scale=-0.5,
                sem=sACT)
        av = sACT.n
        em.wait("vector", sACT, av)
        em.vec("tensor_mul", rows[0:1, 1024:1536], rows[0:1, 0:512],
               rows[0:1, 512:1024], sem=sDVE)
        mrv = sDVE.n
        em.wait("tensor", sDVE, mrv)
        em.mm(PB[:, 0:512], ones1[:, :], rows[0:1, 512:1024], start=True, stop=True)
        em.mm(PB[:, 512:1024], ones1[:, :], rows[0:1, 1024:1536], start=True,
              stop=True, sem=sPE)
        bcv = sPE.n
        rdsrc = dst_x if (src_red is not None and dst_x is not None) else src_x
        for i in range(3):
            em.waitf("sync", sST[i])
        em.wait("sync", sDVE, sDVE.n)
        em.wait("sync", sACT, sACT.n)
        slot_dve2 = {}
        for ft in range(DMT):
            r = ft % 3
            a = t0[:, r * 512:(r + 1) * 512]
            if ft >= 3:
                em.wait("sync", sDVE, slot_dve2.get(ft - 3, 0))
            em.dma("sync", a, rdsrc[128 * ft:128 * (ft + 1), col:col + 512], sTL[r])
            em.waitf("vector", sTL[r])
            em.wait("vector", sPE, bcv)
            b = t1[:, r * 512:(r + 1) * 512]
            em.vec("tensor_mul", b, a, PB[:, 0:512])
            ln_t = x_sb[:, ft * 512:(ft + 1) * 512]
            em.vec("tensor_sub", ln_t, b, PB[:, 512:1024], sem=sDVE)
            lv = sDVE.n
            slot_dve2[ft] = lv
            if dst_lnloc is not None:
                em.wait("gpsimd", sDVE, lv)
                em.dma("gpsimd", dst_lnloc[128 * ft:128 * (ft + 1), 0:512],
                       ln_t, sLNS)

    ag_half = {}

    def mk_ag_ln(h):
        return collective(
            "AllGather", ALU.bypass,
            [ln_loc.ap()[h * DM:(h + 1) * DM, :].opt()],
            [ln_full.ap()[h * NC * DM:(h + 1) * NC * DM, :].opt()])

    for cc in range(NCH):
        emit_merge_ln_chunk(cc, x0T.ap(), None, None,
                            ln_loc.ap()[cc * DM:(cc + 1) * DM, :], None)
        em.waitf("gpsimd", sLNS)
        ag_half[cc] = mk_ag_ln(cc)
    ag1 = ag_half[1]

    if stop_after == "ln1":
        _finish(em, tap_pairs, sOUT, sCC)
        em.replay_into(nc)
        return nc

    def blocked(dst_dram, blk):
        return dst_dram.ap().rearrange("p (b c) -> p b c", c=blk)

    def emit_dequant_gen(qb_ap, sc_ap, qz_ap, gidx_ap, nkt, ncols, G, nblk, blk,
                         dst_dram, want_ohh, gidx_oh_ap, wofs):
        em.wait("sync", sDVE, sDVE.n)
        em.wait("gpsimd", sPE, sPE.n)
        em.dma("gpsimd", sc_bf[0:G, 0:ncols], sc_ap, sSC)
        em.waitf("tensor", sSC)
        dst = blocked(dst_dram, blk)
        dbuf = ncols <= 2048
        back = 2 if dbuf else 1
        unp0, mm0 = sDVE.n, sPE.n
        st0 = {0: sWS.n, 1: sWS2.n}
        unp_mark = {}
        mm_mark = {}
        store_mark = {}
        for kt in range(nkt):
            p = (kt % 2) if dbuf else 0
            u8v = dq_u8[:, p * 2048:p * 2048 + ncols]
            fv = dq_f[:, p * 2048:p * 2048 + ncols]
            bv = dq_b[:, p * 2048:p * 2048 + ncols]
            giv = dq_gi[0:G, p * 128:p * 128 + 128]
            gfv = dq_gf[0:G, p * 128:p * 128 + 128]
            ohv = dq_ohT[0:G, p * 128:p * 128 + 128]
            em.wait("sync", sDVE, unp_mark.get(kt - back, unp0))
            bsrc = qb_ap[64 * kt:64 * (kt + 1), :].unsqueeze(0) \
                .broadcast_to([2, 64, ncols])
            dqsem = sDQ2 if (dbuf and p == 1) else sDQ
            em.dma("sync", u8v, bsrc, dqsem)
            gsrc = gidx_ap[kt * 128:(kt + 1) * 128, 0:1] \
                .rearrange("k x -> x k").broadcast_to([G, 128])
            em.dma("sync", giv, gsrc, dqsem)
            em.waitf("vector", dqsem)
            wsem = sWS2 if (dbuf and p == 1) else sWS
            em.wait("vector", wsem, store_mark.get(kt - back, st0[p]))
            em.wait("vector", sPE, mm_mark.get(kt - back, mm0))
            em.vec("tensor_copy", gfv,
                   giv.rearrange("g (k2 o) -> g k2 o", o=2)
                   .transpose([0, 2, 1]))
            em.vec("tensor_scalar", ohv, gfv, iotac_s[0:G, :],
                   None, ALU.is_equal)
            em.vec("tensor_scalar", u8v[0:64, :], u8v[0:64, :],
                   15, None, ALU.bitwise_and)
            em.vec("tensor_scalar", u8v[64:128, :], u8v[64:128, :],
                   4, 15, ALU.logical_shift_right, ALU.bitwise_and)
            em.vec("tensor_copy", fv, u8v, sem=sDVE)
            dv = sDVE.n
            unp_mark[kt] = dv
            nchk = (ncols + 511) // 512
            for ch in range(nchk):
                a0 = ch * 512
                a1 = min(ncols, a0 + 512)
                em.wait("tensor", sDVE, dv if ch == 0 else sDVE.n)
                em.mm(PC[:, 0:a1 - a0], ohv, sc_bf[0:G, a0:a1],
                      start=True, stop=True, sem=sPE)
                em.wait("vector", sPE, sPE.n)
                em.vec("tensor_mul", bv[:, a0:a1], fv[:, a0:a1],
                       PC[:, 0:a1 - a0], sem=sDVE)
            mm_mark[kt] = sPE.n
            em.wait("gpsimd", sDVE, sDVE.n)
            dstp = dst_dram.ap().rearrange("(k2 o) q -> k2 o q", o=2) \
                .transpose([1, 0, 2]) \
                .rearrange("o k2 (b c) -> o k2 b c", c=blk)[
                    :, :, :, kt * 128:(kt + 1) * 128]
            cpb = ncols // nblk
            bstep2 = max(1, 1536 // 64)
            for o in range(2):
                for b0 in range(0, nblk, bstep2):
                    b1 = min(nblk, b0 + bstep2)
                    em.dma("gpsimd", dstp[o][:, b0:b1, :],
                           bv[64 * o:64 * (o + 1), b0 * cpb:b1 * cpb]
                           .rearrange("p (b c) -> p b c", b=b1 - b0), wsem)
            store_mark[kt] = wsem.n
            yield
        unpack_mark = unp_mark.get(nkt - 1, 0)
        nq = ncols // 8
        em.wait("sync", sDVE, unpack_mark)
        em.dma("sync", zz_i[0:G, 0:nq], qz_ap, sDQ)
        em.dma("sync", zz_n[0:G, 0:ncols], sc_ap, sDQ)
        em.waitf("vector", sDQ)
        em.waitf("vector", sWS)
        em.waitf("vector", sWS2)
        for sh in range(8):
            em.vec("tensor_scalar",
                   zz_i[0:G, 512:512 + ncols].rearrange("g (c e) -> g c e", e=8)[
                       :, :, sh:sh + 1],
                   zz_i[0:G, 0:nq], 4 * sh, 15,
                   ALU.logical_shift_right, ALU.bitwise_and)
        em.vec("tensor_copy", zz_f[0:G, 0:ncols], zz_i[0:G, 512:512 + ncols])
        em.vec("tensor_scalar", zz_n[0:G, 0:ncols], zz_n[0:G, 0:ncols], -1.0, None,
               ALU.mult)
        em.vec("scalar_tensor_tensor", dq_b[0:G, 0:ncols], zz_f[0:G, 0:ncols], 1.0,
               zz_n[0:G, 0:ncols], ALU.add, ALU.mult, sem=sDVE)
        em.wait("gpsimd", sDVE, sDVE.n)
        bstep = max(1, 1536 // G)
        for b0 in range(0, nblk, bstep):
            b1 = min(nblk, b0 + bstep)
            em.dma("gpsimd", dst[0:G, b0:b1, wofs:wofs + 128],
                   dq_b[0:G, b0 * (ncols // nblk):b1 * (ncols // nblk)]
                   .rearrange("g (b c) -> g b c", b=b1 - b0), sWS)
        yield
        if want_ohh:
            for bl in range(nblk):
                em.wait("sync", sDVE, sDVE.n)
                em.dma("sync", gcol_i[:, :],
                       gidx_oh_ap[bl * 128:(bl + 1) * 128, 0:1], sDQ)
                em.waitf("vector", sDQ)
                em.waitf("vector", sWS)
                em.waitf("vector", sWS2)
                em.vec("tensor_copy", gcol_f[:, :], gcol_i[:, :])
                em.vec("tensor_scalar", oh1[:, :], iota_s[:, 0:88], gcol_f[:, :],
                       None, ALU.is_equal, sem=sDVE)
                em.wait("gpsimd", sDVE, sDVE.n)
                em.dma("gpsimd", dst[:, bl:bl + 1, wofs + 128:wofs + 216],
                       oh1[:, :].unsqueeze(1), sWS)
                yield

    agd_d = {}

    def emit_ohx_gen(gidx_ap, dst):
        for kt in range(DMT):
            em.wait("sync", sDVE, sDVE.n)
            em.dma("sync", gcol_i[:, :], gidx_ap[kt * 128:(kt + 1) * 128, 0:1], sDQ)
            em.waitf("vector", sDQ)
            em.vec("tensor_copy", gcol_f[:, :], gcol_i[:, :])
            em.vec("tensor_scalar", dst[:, kt * GG:(kt + 1) * GG], iota_s[:, 0:GG],
                   gcol_f[:, :], None, ALU.is_equal, sem=sDVE)
            yield

    def deq_master():
        yield from emit_dequant_gen(qbg.ap(), scg_sh.ap(), qzg_sh.ap(), gig.ap(),
                                    DMT, SHN, GG, NTIL, GBLK, wsg_sh, True,
                                    gidsh.ap(), DM)
        em.waitf("gpsimd", sWS)
        em.waitf("gpsimd", sWS2)
        agd_d["agg"] = collective("AllGather", ALU.bypass, [wsg_sh.ap().opt()],
                                  [wsg.ap().opt()])
        yield
        yield from emit_dequant_gen(qbu.ap(), scu_sh.ap(), qzu_sh.ap(), giu.ap(),
                                    DMT, SHN, GG, NTIL, UBLK, wsu_sh, False,
                                    None, DM)
        em.waitf("gpsimd", sWS)
        em.waitf("gpsimd", sWS2)
        agd_d["agu"] = collective("AllGather", ALU.bypass, [wsu_sh.ap().opt()],
                                  [wsu.ap().opt()])
        yield
        yield from emit_dequant_gen(qbd.ap(), scd.ap(), qzd.ap(), gidsh.ap(),
                                    NTIL, DM, GD, DMT, DBLK, wsd_sh, False,
                                    None, SHN)
        em.waitf("gpsimd", sWS)
        em.waitf("gpsimd", sWS2)
        agd_d["agd"] = collective("AllGather", ALU.bypass, [wsd_sh.ap().opt()],
                                  [wsd.ap().opt()])
        yield
        yield from emit_ohx_gen(gig.ap(), ohxg_s)
        yield from emit_ohx_gen(giu.ap(), ohxu_s)

    deq = deq_master()
    if stop_after == "dequant":
        for _ in deq:
            pass
        _finish(em, tap_pairs, sOUT, sCC)
        em.replay_into(nc)
        return nc

    def emit_attn(cfg):
        nkt = cfg["kw"] // 128
        WREG = cfg["wreg"]
        XBUF = cfg["xbuf"]
        STP = cfg["statps"]
        if cfg["per_head_kv"]:
            passes = [[("q", 0), ("q", 1), ("q", 2), ("q", 3), ("k", 0), ("k", 1)],
                      [("k", 2), ("k", 3), ("v", 0), ("v", 1), ("v", 2), ("v", 3)]]
        else:
            passes = [[("q", 0), ("q", 1), ("q", 2), ("q", 3), ("k", 0), ("v", 0)]]
        dram_of = {"q": cfg["qT"], "k": cfg["kT"], "v": cfg["vT"]}
        wsrc_of = {"q": cfg["wq"], "k": cfg["wk"], "v": cfg["wv"]}
        for pi, plist in enumerate(passes):
            em.wait("gpsimd", sPE, sPE.n)
            em.wait("gpsimd", sDVE, sDVE.n)
            em.waitf("gpsimd", sLNS)
            em.waitf("sync", sLNS)
            for wi, (nm, nt) in enumerate(plist):
                src = wsrc_of[nm][:, nt * 128:(nt + 1) * 128] \
                    .rearrange("(dt p) n -> p dt n", p=128)
                em.dma("gpsimd", WREG[:, wi * DM:(wi + 1) * DM]
                       .rearrange("p (dt n) -> p dt n", dt=DMT), src, sAW)
            em.waitf("tensor", sAW)
            last_evac = {}
            for sc_i in (list(range(0, NMAC, 2)) + list(range(1, NMAC, 2))):
                em.wait("sync", sCC, cfg["wait_cc_half"][sc_i % 2])
                em.wait("sync", sPE, sPE.n)
                em.wait("sync", sDVE, sDVE.n)
                base = (sc_i % 2) * NC * DM + (sc_i // 2) * DM
                xsrc = ln_full.ap()[base:base + DM, 0:512] \
                    .rearrange("(dt p) c -> p dt c", p=128)
                em.dma("sync", XBUF[:, :].rearrange("p (dt c) -> p dt c", dt=DMT),
                       xsrc, sX)
                if cfg["rope"]:
                    em.dma("sync", rope_c,
                           cosR.ap()[:, sc_i * 512:(sc_i + 1) * 512], sX)
                    em.dma("sync", rope_s,
                           sinS.ap()[:, sc_i * 512:(sc_i + 1) * 512], sX)
                em.waitf("tensor", sX)
                em.waitf("vector", sX)
                for wi, (nm, nt) in enumerate(plist):
                    half = wi % 2
                    acc = PA[:, half * 512:(half + 1) * 512]
                    prevev = last_evac.get(half)
                    if prevev is not None:
                        em.wait("tensor", prevev[0], prevev[1])
                    for dt_i in range(DMT):
                        em.mm(acc, WREG[:, wi * DM + dt_i * 128:
                                        wi * DM + dt_i * 128 + 128],
                              XBUF[:, dt_i * 512:(dt_i + 1) * 512],
                              start=(dt_i == 0), stop=(dt_i == DMT - 1),
                              sem=(sPE if dt_i == DMT - 1 else None))
                    mmv = sPE.n
                    em.wait("scalar", sPE, mmv)
                    if nm == "v":
                        vt = tb0[:, 0:512]
                        em.wait("vector", sPE, sPE.n)
                        if cfg["biased"]:
                            em.vec("tensor_scalar", vt, acc, bv_s[:, nt:nt + 1],
                                   None, ALU.add, sem=sDVE)
                        else:
                            em.vec("tensor_copy", vt, acc, sem=sDVE)
                        av = sDVE.n
                        last_evac[half] = (sDVE, av)
                        em.wait("tensor", sDVE, av)
                        for sub in range(4):
                            em.trans(PT[:, sub * 128:(sub + 1) * 128],
                                     vt[:, sub * 128:(sub + 1) * 128],
                                     ident_s[:, :],
                                     sem=(sPE if sub == 3 else None))
                        tv = sPE.n
                        em.wait("vector", sPE, tv)
                        em.waitf("vector", sVT)
                        em.vec("tensor_copy", vtr[:, :], PT[:, :], sem=sDVE)
                        em.wait("sync", sDVE, sDVE.n)
                        em.dma("sync",
                               dram_of["v"][sc_i * 512:(sc_i + 1) * 512,
                                            nt * 128:(nt + 1) * 128]
                               .rearrange("(sub s) d -> sub s d", sub=4)
                               .transpose([1, 0, 2]),
                               vtr[:, :].rearrange("p (sub d) -> p sub d", sub=4),
                               sVT)
                        continue
                    if cfg["rope"]:
                        a = t0[:, 0:512]
                        em.waitf("scalar", sSW)
                        em.wait("scalar", sDVE, sDVE.n)
                        em.actv(a, acc, AF.Copy, sem=sACT)
                        av = sACT.n
                        last_evac[half] = (sACT, av)
                        em.wait("sync", sACT, av)
                        em.wait("sync", sDVE, sDVE.n)
                        em.dma("sync", t1[0:64, 0:512], a[64:128, 0:512], sSW)
                        em.dma("sync", t1[64:128, 0:512], a[0:64, 0:512], sSW)
                        em.waitf("vector", sSW)
                        em.vec("tensor_mul", t0[:, 512:1024], a, rope_c)
                        em.vec("tensor_mul", t1[:, 512:1024], t1[:, 0:512], rope_s)
                        qk_t = tb0[:, 512:1024]
                        em.waitf("vector", sQKV)
                        em.vec("tensor_add", qk_t, t0[:, 512:1024],
                               t1[:, 512:1024], sem=sDVE)
                        dv = sDVE.n
                        em.wait("sync", sDVE, dv)
                        em.wait("scalar", sDVE, dv)
                    else:
                        bias_ap = (bq_s if nm == "q" else bk_s)[:, nt:nt + 1]
                        qk_t = tb0[:, 512:1024]
                        em.waitf("vector", sQKV)
                        em.wait("vector", sPE, sPE.n)
                        em.wait("vector", sACT, sACT.n)
                        em.vec("tensor_scalar", qk_t, acc, bias_ap, None, ALU.add,
                               sem=sDVE)
                        av = sDVE.n
                        last_evac[half] = (sDVE, av)
                        em.wait("sync", sDVE, av)
                    em.dma("sync", dram_of[nm][nt * 128:(nt + 1) * 128,
                                                 sc_i * 512:(sc_i + 1) * 512],
                           qk_t, sQKV)
                    em.wait("scalar", sPE, sPE.n)
                    em.wait("scalar", sDVE, sDVE.n)
                    em.actv(tb0[:, 1024:1536], qk_t, AF.Square, sem=sACT)
                    av = sACT.n
                    em.wait("tensor", sACT, av)
                    em.wait("tensor", sDVE, sDVE.n)
                    em.mm(STP[0:1, 0:512], onesc_b[:, :], tb0[:, 1024:1536],
                          start=True, stop=True, sem=sPE)
                    em.wait("vector", sPE, sPE.n)
                    dst_row = qn_s if nm == "q" else kn_s
                    em.vec("tensor_reduce",
                           dst_row[0:1, nt * 16 + sc_i:nt * 16 + sc_i + 1],
                           STP[0:1, 0:512], AXX, ALU.max, sem=sDVE)
                    if wi == 2 and cfg.get("interleave"):
                        cfg["interleave"]()
                if cfg.get("interleave"):
                    cfg["interleave"]()

        em.wait("vector", sDVE, sDVE.n)
        for j in range(nkt):
            em.vec("tensor_reduce", kn_s[0:1, 64 + j:65 + j],
                   kn_s[0:1, j * 16:(j + 1) * 16], AXX, ALU.max)
        if nkt == 1:
            em.vec("tensor_mul", cneg_s[0:1, :], qn_s[0:1, :],
                   kn_s[0:1, 64:65].broadcast_to([1, 64]), sem=sDVE)
        else:
            em.vec("tensor_mul",
                   cneg_s[0:1, :].rearrange("o (h m) -> o h m", h=4),
                   qn_s[0:1, :].rearrange("o (h m) -> o h m", h=4),
                   kn_s[0:1, 64:68].unsqueeze(2).broadcast_to([1, 4, 16]),
                   sem=sDVE)
        dv = sDVE.n
        em.wait("scalar", sDVE, dv)
        em.actv(rows[0:1, 0:64], cneg_s[0:1, :], AF.Ln)
        em.actv(rows[0:1, 64:128], rows[0:1, 0:64], AF.Exp, scale=0.5, sem=sACT)
        av = sACT.n
        em.wait("vector", sACT, av)
        em.vec("tensor_scalar", cneg_s[0:1, :], rows[0:1, 64:128], -SCALE, None,
               ALU.mult, sem=sDVE)
        dv = sDVE.n
        em.wait("tensor", sDVE, dv)
        em.mm(PC[:, 0:64], ones1[:, :], cneg_s[0:1, :], start=True, stop=True,
              sem=sPE)
        em.wait("vector", sPE, sPE.n)
        em.vec("tensor_copy", bias_all[:, :], PC[:, 0:64], sem=sDVE)
        em.wait("tensor", sDVE, sDVE.n)

        em.wait("gpsimd", sPE, sPE.n)
        wo_src = cfg["wo"].rearrange("(et p) f -> p et f", p=128)
        em.dma("gpsimd", wo_sb[:, :].rearrange("p (et f) -> p et f", et=4),
               wo_src, sWO)

        ones33 = R3[:, 16384:16640].bitcast(F32)
        em.vec("memset", ones33[32:33, 0:128], 1.0, sem=sDVE)
        em.waitf("sync", sQKV)
        em.waitf("sync", sVT)
        exp_done = {0: 0, 1: 0}
        pslot_pv = [0, 0, 0, 0]
        bctr = 0
        pend = [None]
        kv_done = {}

        def flush_pend():
            if pend[0] is None:
                return
            (fh, fh2, fkc, fb, fslot, fmi, fuse_sem, fuse_val, fpar2) = pend[0]
            pend[0] = None
            em.wait("tensor", fuse_sem, fuse_val)
            pq = p_sb[:, fslot * 1024: fslot * 1024 + 1024]
            first = (fkc == 0)
            last = (fkc == fmi)
            for j in range(2):
                let = 2 * fb + j
                em.mm(PC[32 * fh2:32 * fh2 + 1, 0:512], onesc_b[:, :],
                      pq[:, j * 512:(j + 1) * 512],
                      start=(first and let == 0), stop=(last and let == 3),
                      tile_position=(0, 32 * fh2))
            v = None
            for j in range(2):
                let = 2 * fb + j
                if cfg["per_head_kv"]:
                    vl = kv_v[:, fpar2 * 2048 + let * 512 + fh * 128:
                              fpar2 * 2048 + let * 512 + fh * 128 + 128]
                else:
                    vl = kv_v[:, fpar2 * 512 + let * 128:
                              fpar2 * 512 + let * 128 + 128]
                v = em.mm(PA[:, fh2 * 512:(fh2 + 1) * 512], vl,
                          pq[:, j * 512:(j + 1) * 512],
                          start=(first and let == 0), stop=(last and let == 3),
                          sem=sPE)
            pslot_pv[fslot] = v
            if fh2 == 1 and fb == 1:
                kv_done[fkc] = v

        macro_order = list(range(0, NMAC, 2)) + list(range(1, NMAC, 2))
        for mseq, mi in enumerate(macro_order):
            em.wait("sync", sPE, sPE.n)
            em.dma("sync", q_mac[:, :].rearrange("p (h c) -> p h c", h=4),
                   cfg["qT"][0:512, mi * 512:(mi + 1) * 512]
                   .rearrange("(h p) c -> p h c", p=128), sQ)
            phk = cfg["per_head_kv"]
            nslot = 2 if phk else 4
            swid = 2048 if phk else 512
            pre = 1 if phk else 2
            for hp in range(2):
                kv_done.clear()
                kv_mark = {}
                em.wait("sync", sPE, sPE.n)

                def issue_kv(kc, mi=mi):
                    sl = kc % nslot
                    g = kv_done.get(kc - nslot)
                    if g:
                        em.wait("sync", sPE, g)
                    base = sl * swid
                    if phk:
                        for hf in range(2):
                            ks = cfg["kT"][hf * 256:(hf + 1) * 256,
                                           kc * 512:(kc + 1) * 512] \
                                .rearrange("(h p) c -> p h c", p=128)
                            em.dma("sync",
                                   kv_k[:, base + hf * 1024:base + hf * 1024 + 1024]
                                   .rearrange("p (h c) -> p h c", h=2), ks,
                                   sKV[sl])
                            vs = cfg["vT"][kc * 512 + hf * 256:
                                           kc * 512 + (hf + 1) * 256, 0:512] \
                                .rearrange("(l p) d -> p l d", p=128)
                            em.dma("sync",
                                   kv_v[:, base + hf * 1024:base + hf * 1024 + 1024]
                                   .rearrange("p (l d) -> p l d", l=2), vs,
                                   sKV[sl])
                    else:
                        ks = cfg["kT"][0:128, kc * 512:(kc + 1) * 512]
                        vs = cfg["vT"][kc * 512:(kc + 1) * 512, 0:128] \
                            .rearrange("(l p) d -> p l d", p=128)
                        em.dma("sync", kv_k[:, base:base + 512], ks, sKV[sl])
                        em.dma("sync", kv_v[:, base:base + 512]
                               .rearrange("p (l d) -> p l d", l=4), vs, sKV[sl])
                    kv_mark[kc] = sKV[sl].n

                for kp in range(min(pre, mi + 1)):
                    issue_kv(kp)
                for kc in range(mi + 1):
                    par2 = kc % nslot
                    em.wait("tensor", sKV[par2], kv_mark[kc])
                    nxt = kc + pre
                    if nxt <= mi and nxt not in kv_mark:
                        if phk:
                            flush_pend()
                        issue_kv(nxt)
                    em.waitf("tensor", sQ)
                    for h2 in range(2):
                        h = hp * 2 + h2
                        for b in range(2):
                            par = bctr % 2
                            slot = bctr % 4
                            bctr += 1
                            buf = PB if par == 0 else PB2
                            if exp_done[par]:
                                em.wait("tensor", sACT, exp_done[par])
                            qk = None
                            for j in range(2):
                                let = 2 * b + j
                                if cfg["per_head_kv"]:
                                    lt = kv_k[:, par2 * 2048 + h * 512 + let * 128:
                                              par2 * 2048 + h * 512 + let * 128 + 128]
                                else:
                                    lt = kv_k[:, par2 * 512 + let * 128:
                                              par2 * 512 + let * 128 + 128]
                                qk = em.mm(buf[:, j * 512:(j + 1) * 512], lt,
                                           q_mac[:, h * 512:(h + 1) * 512],
                                           start=True, stop=True, sem=sPE)
                            flush_pend()
                            em.wait("scalar", sPE, max(qk, pslot_pv[slot]))
                            pq = p_sb[:, slot * 1024: slot * 1024 + 1024]
                            em.actv(pq, buf[:, :], AF.Exp,
                                    bias=bias_all[:, h * 16 + mi:h * 16 + mi + 1],
                                    scale=SCALE, sem=sACT)
                            ev = sACT.n
                            exp_done[par] = ev
                            use_sem, use_val = sACT, ev
                            if kc == mi:
                                em.wait("vector", sACT, ev)
                                em.vec("tensor_mul", pq, pq,
                                       tri_s[:, b * 1024:(b + 1) * 1024], sem=sDVE)
                                use_sem, use_val = sDVE, sDVE.n
                            pend[0] = (h, h2, kc, b, slot, mi, use_sem, use_val,
                                       par2)
                flush_pend()
                pv_all = sPE.n
                em.wait("vector", sPE, pv_all)
                em.waitf("vector", sPRS[0])
                em.waitf("vector", sPRS[1])
                em.waitf("vector", sSW)
                em.vec("tensor_copy", den_s[0:1, :], PC[0:1, :])
                em.vec("tensor_copy", den_s[32:33, :], PC[32:33, :])
                em.vec("reciprocal", den_s[0:1, :], den_s[0:1, :])
                em.vec("reciprocal", den_s[32:33, :], den_s[32:33, :], sem=sDVE)
                rv = sDVE.n
                em.wait("tensor", sDVE, rv)
                for h2 in range(2):
                    h = hp * 2 + h2
                    dsrc = den_s[0:1, :] if h2 == 0 else den_s[32:33, :]
                    lsrc = ones1[:, :] if h2 == 0 else ones33[32:33, :]
                    em.wait("tensor", sDVE, sDVE.n)
                    em.mm(PB[:, 0:512], lsrc, dsrc,
                          start=True, stop=True, sem=sPE)
                    bv_ = sPE.n
                    em.wait("scalar", sPE, bv_)
                    em.wait("scalar", sDVE, sDVE.n)
                    em.actv(t0[:, 0:512], PA[:, h2 * 512:(h2 + 1) * 512], AF.Copy,
                            sem=sACT)
                    av = sACT.n
                    em.wait("vector", sACT, av)
                    em.wait("vector", sPE, bv_)
                    em.vec("tensor_mul", att_sb[:, h * 512:(h + 1) * 512],
                           t0[:, 0:512], PB[:, 0:512], sem=sDVE)
                em.wait("tensor", sDVE, sDVE.n)
                exp_done[0] = 0
                exp_done[1] = 0
                em.wait("tensor", sACT, sACT.n)
            em.waitf("tensor", sWO)
            op_evac = {}
            for ft in range(DMT):
                half = ft % 2
                bank = PB2[:, half * 512:half * 512 + 512]
                if ft >= 2:
                    em.wait("tensor", sDVE, op_evac[ft - 2])
                for et in range(4):
                    em.mm(bank, wo_sb[:, (et * DMT + ft) * 128:
                                      (et * DMT + ft) * 128 + 128],
                          att_sb[:, et * 512:(et + 1) * 512],
                          start=(et == 0), stop=(et == 3),
                          sem=(sPE if et == 3 else None))
                pv = sPE.n
                em.wait("vector", sPE, pv)
                em.waitf("vector", sPRS[0])
                em.waitf("vector", sPRS[1])
                em.vec("tensor_copy", t1[:, half * 512:half * 512 + 512], bank,
                       sem=sDVE)
                op_evac[ft] = sDVE.n
                em.wait("gpsimd", sDVE, op_evac[ft])
                em.dma("gpsimd",
                       partial.ap()[(mi % 2) * NC * DM + (mi // 2) * DM + 128 * ft:
                                    (mi % 2) * NC * DM + (mi // 2) * DM + 128 * (ft + 1),
                                    0:512],
                       t1[:, half * 512:half * 512 + 512], sPRS[mi % 2])
            em.wait("tensor", sACT, sACT.n)
            em.wait("tensor", sDVE, sDVE.n)
            if mseq == NMAC // 2 - 1:
                em.waitf("gpsimd", sPRS[0])
                cfg["rs_out"].append(cfg["mk_rs"](0))
            elif mseq == NMAC - 1:
                em.waitf("gpsimd", sPRS[1])
                cfg["rs_out"].append(cfg["mk_rs"](1))

    def mk_rs(halfsel):
        return collective(
            "ReduceScatter", ALU.add,
            [partial.ap()[halfsel * NC * DM:(halfsel + 1) * NC * DM, :].opt()],
            [red.ap()[halfsel * DM:(halfsel + 1) * DM, :].opt()])

    def inter_deq():
        for _ in range(8):
            if next(deq, "DONE") == "DONE":
                break

    cfg_a = dict(qT=qaD.ap(), kT=kaD.ap(), vT=vaD.ap(), per_head_kv=False,
                 rope=True, biased=False, wq=waqT.ap(), wk=wakT.ap(),
                 wv=wavT.ap(), wo=waoT.ap(), kw=128,
                 wait_cc_half=(ag_half[0], ag_half[1]),
                 wreg=R2, xbuf=R3[:, 0:16384], statps=PB2,
                 interleave=inter_deq, mk_rs=mk_rs, rs_out=[])
    emit_attn(cfg_a)
    for _ in deq:
        pass
    rs1 = cfg_a["rs_out"][1]

    if stop_after == "attn_a":
        _finish(em, tap_pairs, sOUT, sCC)
        em.replay_into(nc)
        return nc

    def emit_ffn(cc, src_x, dst_x, dst_lnloc, is_last):
        col = cc * 512
        em.wait("sync", sPE, sPE.n)
        em.wait("tensor", sDVE, sDVE.n)
        for kt in range(DMT):
            em.mm(PC[0:GG, 0:512], ohxg_s[:, kt * GG:(kt + 1) * GG],
                  x_sb[:, kt * 512:(kt + 1) * 512], start=(kt == 0),
                  stop=(kt == DMT - 1))
        for kt in range(DMT):
            em.mm(PB2[0:GG, 0:512], ohxu_s[:, kt * GG:(kt + 1) * GG],
                  x_sb[:, kt * 512:(kt + 1) * 512], start=(kt == 0),
                  stop=(kt == DMT - 1), sem=(sPE if kt == DMT - 1 else None))
        pv = sPE.n
        em.wait("vector", sPE, pv)
        em.vec("tensor_copy", xg_s[:, 0:512], PC[0:GG, 0:512])
        em.vec("tensor_copy", xg_s[:, 512:1024], PB2[0:GG, 0:512], sem=sDVE)
        em.wait("tensor", sDVE, sDVE.n)

        wpar_done = {}
        hpar_done = {}
        for nt in range(NTI):
            parw = nt % 2
            r, ntl = divmod(nt, NTIL)
            woff = parw * (GBLK + UBLK)
            if nt >= 2:
                em.wait("sync", sPE, wpar_done[nt - 2])
            em.dma("sync", R3[:, woff:woff + GBLK],
                   wsg.ap()[r * 128:(r + 1) * 128, ntl * GBLK:(ntl + 1) * GBLK],
                   sW[parw])
            em.dma("sync", R3[:, woff + GBLK:woff + GBLK + UBLK],
                   wsu.ap()[r * 128:(r + 1) * 128, ntl * UBLK:(ntl + 1) * UBLK],
                   sW[parw])
            em.waitf("tensor", sW[parw])
            half = nt % 2
            if nt >= 2:
                em.wait("tensor", sACT, hpar_done[nt - 2][0])
                em.wait("tensor", sDVE, hpar_done[nt - 2][1])
            for gi_, base in ((0, woff), (1, woff + GBLK)):
                acc = (PA if gi_ == 0 else PB)[:, half * 512:(half + 1) * 512]
                for kt in range(DMT):
                    em.mm(acc, R3[:, base + kt * 128:base + kt * 128 + 128],
                          x_sb[:, kt * 512:(kt + 1) * 512], start=(kt == 0),
                          stop=False)
                em.mm(acc, R3[0:GG, base + DM:base + DM + 128],
                      xg_s[0:GG, gi_ * 512:gi_ * 512 + 512],
                      start=False, stop=True, sem=sPE)
            gu = sPE.n
            em.wait("scalar", sPE, gu)
            sil = tb0[:, half * 512:half * 512 + 512]
            em.actv(sil, PA[:, half * 512:(half + 1) * 512], AF.Silu, sem=sACT)
            av = sACT.n
            em.wait("vector", sACT, av)
            em.vec("tensor_mul", h_sb[:, nt * 512:(nt + 1) * 512], sil,
                   PB[:, half * 512:(half + 1) * 512], sem=sDVE)
            hv = sDVE.n
            hpar_done[nt] = (av, hv)
            em.wait("tensor", sDVE, hv)
            wpar_done[nt] = em.mm(
                PC[0:88, 0:512], R3[:, woff + DM + 128:woff + DM + 216],
                h_sb[:, nt * 512:(nt + 1) * 512], start=(nt == 0),
                stop=(nt == NTI - 1), sem=sPE)
        hgv = sPE.n
        em.wait("vector", sPE, hgv)
        em.vec("tensor_copy", hg_s[:, :], PC[0:88, 0:512], sem=sDVE)
        em.wait("tensor", sDVE, sDVE.n)

        em.wait("sync", sPE, sPE.n)
        em.wait("sync", sDVE, sDVE.n)
        em.wait("scalar", sDVE, sDVE.n)
        prev_stat2 = 0
        wd_done = {}
        dn_evac = {}
        slot_dve = {}
        for ft in range(DMT):
            parw = ft % 2
            wdoff = parw * (NC * DBLK)
            if ft >= 2:
                em.wait("sync", sPE, wd_done[ft - 2])
            for r in range(NC):
                em.dma("sync", R2[:, wdoff + r * DBLK: wdoff + (r + 1) * DBLK],
                       wsd.ap()[r * 128:(r + 1) * 128,
                                ft * DBLK:(ft + 1) * DBLK], sD[parw])
            em.waitf("tensor", sD[parw])
            bank = PB2[:, parw * 512:parw * 512 + 512]
            if ft >= 2:
                em.wait("tensor", sACT, dn_evac[ft - 2])
            for kt in range(NTI):
                r2, kl = divmod(kt, NTIL)
                em.mm(bank, R2[:, wdoff + r2 * DBLK + kl * 128:
                               wdoff + r2 * DBLK + kl * 128 + 128],
                      h_sb[:, kt * 512:(kt + 1) * 512], start=(kt == 0),
                      stop=False)
            em.mm(bank, R2[0:GD, wdoff + SHN: wdoff + SHN + 128],
                  hg_s[0:GD, 0:512], start=False, stop=True, sem=sPE)
            dv_ = sPE.n
            wd_done[ft] = dv_
            em.wait("scalar", sPE, dv_)
            rsl = ft % 3
            a = t0[:, rsl * 512:(rsl + 1) * 512]
            if ft >= 3:
                em.waitf("scalar", sST[rsl])
                if is_last:
                    em.waitf("scalar", sOUT)
                em.wait("sync", sDVE, slot_dve.get(ft - 3, 0))
            em.actv(a, bank, AF.Copy, sem=sACT)
            dn_evac[ft] = sACT.n
            b = t1[:, rsl * 512:(rsl + 1) * 512]
            em.dma("sync", b, src_x[128 * ft:128 * (ft + 1), col:col + 512],
                   sTL[rsl])
            em.waitf("vector", sTL[rsl])
            em.wait("vector", sACT, dn_evac[ft])
            em.vec("tensor_add", a, a, b, sem=sDVE)
            mv = sDVE.n
            slot_dve[ft] = mv
            em.wait("gpsimd", sDVE, mv)
            if is_last:
                em.dma("gpsimd", out_ext.ap()[128 * ft:128 * (ft + 1),
                                              col:col + 512], a, sOUT)
            else:
                em.dma("gpsimd", dst_x[128 * ft:128 * (ft + 1), col:col + 512], a,
                       sST[rsl])
            if dst_lnloc is not None:
                xb = tb0[:, 1024:1536]
                em.wait("scalar", sPE, prev_stat2)
                em.wait("scalar", sDVE, mv)
                em.actv(xb, a, AF.Copy, sem=sACT)
                av2 = sACT.n
                em.wait("tensor", sACT, av2)
                em.mm(PA[0:1, 0:512], onesc_b[:, :], xb, start=(ft == 0),
                      stop=(ft == DMT - 1), sem=sPE)
                em.wait("scalar", sPE, sPE.n)
                em.actv(xb, a, AF.Square, sem=sACT)
                av3 = sACT.n
                em.wait("tensor", sACT, av3)
                prev_stat2 = em.mm(PA[0:1, 512:1024], onesc_b[:, :], xb,
                                   start=(ft == 0), stop=(ft == DMT - 1), sem=sPE)
        if dst_lnloc is None:
            return
        pe_done = sPE.n
        em.wait("vector", sPE, pe_done)
        em.vec("tensor_scalar", rows[0:1, 0:512], PA[0:1, 0:512], 1.0 / DM, None,
               ALU.mult)
        em.vec("tensor_scalar", rows[0:1, 512:1024], PA[0:1, 512:1024], 1.0 / DM,
               None, ALU.mult)
        em.vec("tensor_mul", rows[0:1, 1024:1536], rows[0:1, 0:512],
               rows[0:1, 0:512])
        em.vec("tensor_sub", rows[0:1, 512:1024], rows[0:1, 512:1024],
               rows[0:1, 1024:1536], sem=sDVE)
        vv = sDVE.n
        em.wait("scalar", sDVE, vv)
        em.actv(rows[0:1, 1024:1536], rows[0:1, 512:1024], AF.Ln,
                bias=eps_c[0:1, :])
        em.actv(rows[0:1, 512:1024], rows[0:1, 1024:1536], AF.Exp, scale=-0.5,
                sem=sACT)
        av = sACT.n
        em.wait("vector", sACT, av)
        em.vec("tensor_mul", rows[0:1, 1024:1536], rows[0:1, 0:512],
               rows[0:1, 512:1024], sem=sDVE)
        mrv = sDVE.n
        em.wait("tensor", sDVE, mrv)
        em.mm(PB[:, 0:512], ones1[:, :], rows[0:1, 512:1024], start=True, stop=True)
        em.mm(PB[:, 512:1024], ones1[:, :], rows[0:1, 1024:1536], start=True,
              stop=True, sem=sPE)
        bcv = sPE.n
        for i in range(3):
            em.waitf("sync", sST[i])
        em.waitf("vector", sLNS)
        em.wait("sync", sDVE, sDVE.n)
        em.wait("sync", sACT, sACT.n)
        slot_dve2 = {}
        for ft in range(DMT):
            rsl = ft % 3
            a = t0[:, rsl * 512:(rsl + 1) * 512]
            if ft >= 3:
                em.wait("sync", sDVE, slot_dve2.get(ft - 3, 0))
            em.dma("sync", a, dst_x[128 * ft:128 * (ft + 1), col:col + 512],
                   sTL[rsl])
            em.waitf("vector", sTL[rsl])
            em.wait("vector", sPE, bcv)
            b = t1[:, rsl * 512:(rsl + 1) * 512]
            em.vec("tensor_mul", b, a, PB[:, 0:512])
            ln_t = x_sb[:, ft * 512:(ft + 1) * 512]
            em.vec("tensor_sub", ln_t, b, PB[:, 512:1024], sem=sDVE)
            slot_dve2[ft] = sDVE.n
            em.wait("gpsimd", sDVE, sDVE.n)
            em.dma("gpsimd", dst_lnloc[128 * ft:128 * (ft + 1), 0:512],
                   ln_t, sLNS)

    em.wait("tensor", sCC, agd_d["agu"])
    em.wait("sync", sCC, agd_d["agd"])
    ag2_half = {}
    for cc in range(NCH):
        rsw = cfg_a["rs_out"][cc]
        em.wait("sync", sCC, rsw)
        em.wait("vector", sCC, rsw)
        emit_merge_ln_chunk(cc, x0T.ap(),
                            red.ap()[cc * DM:(cc + 1) * DM, :], xA.ap(), None,
                            None)
        em.wait("gpsimd", sCC, ag_half[1])
        emit_ffn(cc, xA.ap(), xB.ap(),
                 ln_loc.ap()[cc * DM:(cc + 1) * DM, :], is_last=False)
        em.waitf("gpsimd", sLNS)
        ag2_half[cc] = mk_ag_ln(cc)
    ag2a, ag2b = ag2_half[0], ag2_half[1]

    if stop_after == "ffn1":
        _finish(em, tap_pairs, sOUT, sCC)
        em.replay_into(nc)
        return nc

    cfg_b = dict(qT=qbD.ap(), kT=kbD.ap(), vT=vbD.ap(), per_head_kv=True,
                 rope=False, biased=True, wq=wbqT.ap(), wk=wbkT.ap(),
                 wv=wbvT.ap(), wo=wboT.ap(), kw=512,
                 wait_cc_half=(ag2a, ag2b),
                 wreg=R1, xbuf=x_sb, statps=PC,
                 interleave=None, mk_rs=mk_rs, rs_out=[])
    emit_attn(cfg_b)
    rs2 = cfg_b["rs_out"][1]

    for cc in range(NCH):
        rsw = cfg_b["rs_out"][cc]
        em.wait("sync", sCC, rsw)
        em.wait("vector", sCC, rsw)
        emit_merge_ln_chunk(cc, xB.ap(),
                            red.ap()[cc * DM:(cc + 1) * DM, :], xA.ap(), None,
                            bo_s)
        emit_ffn(cc, xA.ap(), None, None, is_last=True)

    _finish(em, tap_pairs, sOUT, sCC)
    em.replay_into(nc)
    return nc


def _finish(em, tap_pairs, sOUT, sCC):
    if sCC.n:
        em.wait("gpsimd", sCC, sCC.n)
    for tout, tsrc in tap_pairs:
        em.dma("gpsimd", tout.ap(), tsrc.ap(), sOUT)
    em.waitf("gpsimd", sOUT)
    em.waitf("sync", sOUT)


def kernel(**inputs):
    in_maps = _host_prep(inputs)
    if "nc" not in _NC_CACHE:
        _NC_CACHE["nc"] = build_nc()
    nc = _NC_CACHE["nc"]
    res = run_bass_kernel_spmd(nc, in_maps, core_ids=list(range(NC)))
    outs = [res.results[c]["out"] for c in range(NC)]
    full = np.concatenate([o.T for o in outs], axis=0)
    return full[None, :, :].astype(np.float32, copy=False)


# version 09
# speedup vs baseline: 1.1086x; 1.0031x over previous
import numpy as np
import ml_dtypes
from contextlib import ExitStack

import concourse.bass as bass
import concourse.mybir as mybir
from concourse.bass_utils import run_bass_kernel_spmd

BF16 = mybir.dt.bfloat16
F32 = mybir.dt.float32
U8 = mybir.dt.uint8
I32 = mybir.dt.int32
AF = mybir.ActivationFunctionType
ALU = mybir.AluOpType
AXX = mybir.AxisListType.X
NBF = ml_dtypes.bfloat16

NC = 8
S = 8192
DM = 4096
LS = S // NC
NCH = LS // 512
INTER_P = 11264
NTI = INTER_P // 128
NTIL = NTI // NC
SHN = INTER_P // NC
GG = 32
GD = 86
SCALE = 1.0 / float(np.sqrt(128.0))
EPS = 1e-5
NMAC = S // 512
DMT = DM // 128

GBLK = DM + 128 + 88 + 40
UBLK = DM + 128
DBLK = SHN + 128

ENGS = ("sync", "gpsimd", "vector", "scalar", "tensor")

_NC_CACHE = {}


def _bytes_view(qw):
    kk8, n = qw.shape
    b = qw.view('<u4').view(np.uint8).reshape(kk8, n, 4).transpose(0, 2, 1)
    return np.ascontiguousarray(b.reshape(kk8 * 4, n))


def _host_prep(inp):
    x = np.asarray(inp["L0_attn_a_x"])[0]
    xT = np.ascontiguousarray(x.T)

    w_inT = np.ascontiguousarray(np.asarray(inp["L0_attn_a_w_in"]).T)
    w_outT = np.ascontiguousarray(np.asarray(inp["L0_attn_a_w_out"]).T)
    WqT = np.ascontiguousarray(np.asarray(inp["L0_attn_b_Wq"]).T)
    WkT = np.ascontiguousarray(np.asarray(inp["L0_attn_b_Wk"]).T)
    WvT = np.ascontiguousarray(np.asarray(inp["L0_attn_b_Wv"]).T)
    WoT = np.ascontiguousarray(np.asarray(inp["L0_attn_b_Wo"]).T)
    bq = np.asarray(inp["L0_attn_b_bq"]); bk = np.asarray(inp["L0_attn_b_bk"])
    bv = np.asarray(inp["L0_attn_b_bv"]); bo = np.asarray(inp["L0_attn_b_bo"])

    qb_g = _bytes_view(np.asarray(inp["L0_ffn_gate_qweight"]))
    qb_u = _bytes_view(np.asarray(inp["L0_ffn_up_qweight"]))
    qb_d = _bytes_view(np.asarray(inp["L0_ffn_down_qweight"]))
    sc_g = np.asarray(inp["L0_ffn_gate_scales"]).astype(np.float32, copy=False)
    sc_u = np.asarray(inp["L0_ffn_up_scales"]).astype(np.float32, copy=False)
    sc_d = np.asarray(inp["L0_ffn_down_scales"]).astype(np.float32, copy=False)
    qz_g = np.asarray(inp["L0_ffn_gate_qzeros"]).astype(np.int32, copy=False)
    qz_u = np.asarray(inp["L0_ffn_up_qzeros"]).astype(np.int32, copy=False)
    qz_d = np.asarray(inp["L0_ffn_down_qzeros"]).astype(np.int32, copy=False)
    gi_g = np.asarray(inp["L0_ffn_gate_g_idx"]).astype(np.int32)
    gi_u = np.asarray(inp["L0_ffn_up_g_idx"]).astype(np.int32)
    gi_d = np.asarray(inp["L0_ffn_down_g_idx"]).astype(np.int32)

    gi_d_p = np.full(INTER_P, -1, np.int32)
    for r in range(NC):
        gi_d_p[r * SHN: r * SHN + 1376] = gi_d[r * 1376:(r + 1) * 1376]

    inv_freq = 1.0 / (10000.0 ** (np.arange(0, 128, 2, dtype=np.float32) / 128.0))
    ang = np.arange(S, dtype=np.float32)[:, None] * inv_freq[None, :]
    cos = np.cos(ang).astype(np.float32).T
    sin = np.sin(ang).astype(np.float32).T
    cosR = np.ascontiguousarray(np.concatenate([cos, cos], axis=0))
    sinS = np.ascontiguousarray(np.concatenate([-sin, sin], axis=0))

    tri = np.ascontiguousarray(
        (np.arange(512)[:, None] <= np.arange(512)[None, :]).astype(NBF))
    iota128x88 = np.ascontiguousarray(
        np.broadcast_to(np.arange(88, dtype=np.float32), (128, 88)))
    iota88c = np.ascontiguousarray(np.arange(88, dtype=np.float32)[:, None])
    ident = np.ascontiguousarray(np.eye(128, dtype=NBF))

    def col4(v):
        return np.ascontiguousarray(v.reshape(4, 128).T)

    in_maps = []
    for c in range(NC):
        qr = slice(512 * c, 512 * (c + 1))
        m = {
            "x0T": np.ascontiguousarray(xT[:, LS * c: LS * (c + 1)]),
            "waqT": np.ascontiguousarray(w_inT[:, qr]),
            "wakT": np.ascontiguousarray(w_inT[:, 4096 + 128 * c: 4096 + 128 * (c + 1)]),
            "wavT": np.ascontiguousarray(w_inT[:, 5120 + 128 * c: 5120 + 128 * (c + 1)]),
            "waoT": np.ascontiguousarray(w_outT[qr, :]),
            "wbqT": np.ascontiguousarray(WqT[:, qr]),
            "wbkT": np.ascontiguousarray(WkT[:, qr]),
            "wbvT": np.ascontiguousarray(WvT[:, qr]),
            "wboT": np.ascontiguousarray(WoT[qr, :]),
            "bq4": col4(bq[qr]), "bk4": col4(bk[qr]), "bv4": col4(bv[qr]),
            "bo32": np.ascontiguousarray(bo.reshape(DMT, 128).T),
            "qbg": np.ascontiguousarray(
                np.pad(qb_g[:, 1376 * c:1376 * (c + 1)], ((0, 0), (0, 32)))),
            "qbu": np.ascontiguousarray(
                np.pad(qb_u[:, 1376 * c:1376 * (c + 1)], ((0, 0), (0, 32)))),
            "qbd": np.ascontiguousarray(
                np.pad(qb_d[688 * c:688 * (c + 1), :], ((0, 16), (0, 0)))),
            "scg_sh": np.ascontiguousarray(
                np.pad(sc_g[:, 1376 * c:1376 * (c + 1)], ((0, 0), (0, 32)))),
            "scu_sh": np.ascontiguousarray(
                np.pad(sc_u[:, 1376 * c:1376 * (c + 1)], ((0, 0), (0, 32)))),
            "scd": sc_d,
            "qzg_sh": np.ascontiguousarray(
                np.pad(qz_g[:, 172 * c:172 * (c + 1)], ((0, 0), (0, 4)))),
            "qzu_sh": np.ascontiguousarray(
                np.pad(qz_u[:, 172 * c:172 * (c + 1)], ((0, 0), (0, 4)))),
            "qzd": qz_d,
            "gig": np.ascontiguousarray(gi_g[:, None]),
            "giu": np.ascontiguousarray(gi_u[:, None]),
            "gidp": np.ascontiguousarray(gi_d_p[:, None]),
            "gidsh": np.ascontiguousarray(gi_d_p[SHN * c: SHN * (c + 1)])[:, None],
            "cosR": cosR, "sinS": sinS, "tri": tri,
            "iota88": iota128x88, "iota88c": iota88c,
            "ident": ident,
        }
        in_maps.append(m)
    return in_maps


class Sem:
    __slots__ = ("h", "n")

    def __init__(self, handle):
        self.h = handle
        self.n = 0

    def bump(self, k):
        self.n += k
        return self.n


class Em:
    def __init__(self, nc):
        self.nc = nc
        self.q = {e: [] for e in ENGS}
        self.ctx = ExitStack()
        self.auto = {}

    def sem(self, name):
        return Sem(self.ctx.enter_context(self.nc.semaphore(name)))

    def sb(self, name, shape, dtype):
        return self.ctx.enter_context(self.nc.sbuf_tensor(name, shape, dtype))

    def psum(self, name, shape, dtype=F32):
        return self.ctx.enter_context(self.nc.psum_tensor(name, shape, dtype))

    def emit(self, eng, fn):
        self.q[eng].append(fn)

    def dma(self, eng, out, in_, sem, **kw):
        v = sem.bump(16)
        self.emit(eng, lambda e, o=out, i=in_, s=sem.h, kw=kw:
                  e.dma_start(out=o, in_=i, **kw).then_inc(s, 16))
        return v

    def wait(self, eng, sem, val):
        assert val <= sem.n
        if val > 0:
            self.emit(eng, lambda e, s=sem.h, vv=val: e.wait_ge(s, vv))

    def waitf(self, eng, sem):
        v = sem.n
        if v > 0:
            self.emit(eng, lambda e, s=sem.h, vv=v: e.wait_ge(s, vv))

    def mm(self, out, lhsT, rhs, start, stop, sem=None, tile_position=None):
        v = sem.bump(1) if sem else None
        import inspect
        fr = inspect.currentframe().f_back
        tag = f"mm@{fr.f_lineno}"
        def f(t, o=out, l=lhsT, r=rhs, st=start, sp=stop, s=sem, tp=tile_position,
              tag=tag):
            i = t.matmul(o, l, r, start=st, stop=sp, tile_position=tp)
            i.annotate(tag)
            if s is not None:
                i.then_inc(s.h, 1)
        self.emit("tensor", f)
        return v

    def trans(self, out, in_, ident, sem=None):
        v = sem.bump(1) if sem else None
        def f(t, o=out, i=in_, d=ident, s=sem):
            ins = t.transpose(o, i, d)
            if s is not None:
                ins.then_inc(s.h, 1)
        self.emit("tensor", f)
        return v

    def actv(self, out, in_, func, sem=None, bias=0.0, scale=1.0):
        s = self.auto["scalar"]
        pre = s.n
        v = s.bump(1)
        def f(a, o=out, i=in_, fu=func, b=bias, sc=scale, pre=pre, s=s):
            if pre > 0:
                a.wait_ge(s.h, pre)
            a.activation(o, i, fu, bias=b, scale=sc).then_inc(s.h, 1)
        self.emit("scalar", f)
        return v

    def vec(self, fn_name, *args, sem=None, **kw):
        s = self.auto["vector"]
        pre = s.n
        v = s.bump(1)
        def f(e, a=args, kw=kw, fn=fn_name, pre=pre, s=s):
            if pre > 0:
                e.wait_ge(s.h, pre)
            getattr(e, fn)(*a, **kw).then_inc(s.h, 1)
        self.emit("vector", f)
        return v

    def replay_into(self, nc):
        with nc.Block() as block:
            for name in ENGS:
                ops = self.q[name]
                def runner(e, ops=ops):
                    for op in ops:
                        op(e)
                getattr(block, name)(runner)


def build_nc(taps=(), stop_after=None):
    nc = bass.Bass("TRN2", target_bir_lowering=False, debug=False)

    def din(name, shape, dtype):
        return nc.dram_tensor(name, shape, dtype, kind="ExternalInput")

    x0T = din("x0T", [DM, LS], F32)
    waqT = din("waqT", [DM, 512], F32); wakT = din("wakT", [DM, 128], F32)
    wavT = din("wavT", [DM, 128], F32); waoT = din("waoT", [512, DM], F32)
    wbqT = din("wbqT", [DM, 512], F32); wbkT = din("wbkT", [DM, 512], F32)
    wbvT = din("wbvT", [DM, 512], F32); wboT = din("wboT", [512, DM], F32)
    bq4 = din("bq4", [128, 4], F32); bk4 = din("bk4", [128, 4], F32)
    bv4 = din("bv4", [128, 4], F32); bo32 = din("bo32", [128, DMT], F32)
    qbg = din("qbg", [2048, SHN], U8); qbu = din("qbu", [2048, SHN], U8)
    qbd = din("qbd", [704, DM], U8)
    scg_sh = din("scg_sh", [GG, SHN], F32); scu_sh = din("scu_sh", [GG, SHN], F32)
    scd = din("scd", [GD, DM], F32)
    qzg_sh = din("qzg_sh", [GG, SHN // 8], I32)
    qzu_sh = din("qzu_sh", [GG, SHN // 8], I32)
    qzd = din("qzd", [GD, 512], I32)
    gig = din("gig", [DM, 1], I32); giu = din("giu", [DM, 1], I32)
    gidp = din("gidp", [INTER_P, 1], I32); gidsh = din("gidsh", [SHN, 1], I32)
    cosR = din("cosR", [128, S], F32); sinS = din("sinS", [128, S], F32)
    tri = din("tri", [512, 512], BF16)
    iota88 = din("iota88", [128, 88], F32); iota88c = din("iota88c", [88, 1], F32)
    ident = din("ident", [128, 128], BF16)

    out_ext = nc.dram_tensor("out", [DM, LS], F32, kind="ExternalOutput")

    ln_loc = nc.dram_tensor("ln_loc", [2 * DM, 512], BF16)
    ln_full = nc.dram_tensor("ln_full", [2 * NC * DM, 512], BF16)
    partial = nc.dram_tensor("partial", [2 * NC * DM, 512], F32)
    red = nc.dram_tensor("red", [2 * DM, 512], F32)
    xA = nc.dram_tensor("xA", [DM, LS], F32)
    xB = nc.dram_tensor("xB", [DM, LS], F32)
    qaD = nc.dram_tensor("qaD", [512, S], BF16)
    kaD = nc.dram_tensor("kaD", [128, S], BF16)
    vaD = nc.dram_tensor("vaD", [S, 128], BF16)
    qbD = nc.dram_tensor("qbD", [512, S], BF16)
    kbD = nc.dram_tensor("kbD", [512, S], BF16)
    vbD = nc.dram_tensor("vbD", [S, 512], BF16)
    wsg_sh = nc.dram_tensor("wsg_sh", [128, NTIL * GBLK], BF16)
    wsu_sh = nc.dram_tensor("wsu_sh", [128, NTIL * UBLK], BF16)
    wsd_sh = nc.dram_tensor("wsd_sh", [128, DMT * DBLK], BF16)
    wsg = nc.dram_tensor("wsg", [NC * 128, NTIL * GBLK], BF16)
    wsu = nc.dram_tensor("wsu", [NC * 128, NTIL * UBLK], BF16)
    wsd = nc.dram_tensor("wsd", [NC * 128, DMT * DBLK], BF16)

    internal = {"ln_loc": ln_loc, "ln_full": ln_full, "partial": partial,
                "red": red, "xA": xA, "xB": xB, "qaD": qaD, "kaD": kaD,
                "vaD": vaD, "qbD": qbD, "kbD": kbD, "vbD": vbD,
                "wsg_sh": wsg_sh, "wsu_sh": wsu_sh, "wsd_sh": wsd_sh,
                "wsg": wsg, "wsu": wsu, "wsd": wsd}
    tap_pairs = []
    for tname in taps:
        t = internal[tname]
        tap_pairs.append((nc.dram_tensor("tap_" + tname, list(t.ap().shape),
                                         t.ap().dtype, kind="ExternalOutput"), t))

    em = Em(nc)

    R1 = em.sb("R1", [128, 45056], BF16)
    R2 = em.sb("R2", [128, 24576], BF16)
    R3 = em.sb("R3", [128, 17152], BF16)

    h_sb = R1[:, 0:45056]
    x_sb = R2[:, 0:16384]
    wo_sb = R2[:, 0:16384]

    dq_u8 = R1[:, 0:2048].bitcast(U8)
    dq_f = R1[:, 2048:10240].bitcast(F32)
    dq_b = R1[:, 10240:14336]
    zz_i = R1[:, 14336:23552].bitcast(I32)
    zz_f = R1[:, 23552:31744].bitcast(F32)
    zz_n = R1[:, 31744:39936].bitcast(F32)
    sc_bf = R1[:, 39936:44032]

    kv_k = R3[:, 0:4096]
    kv_v = R3[:, 4096:8192]
    q_mac = R3[:, 8192:10240]
    p_sb = R3[:, 10240:14336]
    att_sb = R3[:, 14336:16384]

    ones1 = em.sb("ones1", [1, 128], F32)
    onesc_b = em.sb("onesc_b", [128, 1], BF16)
    ident_s = em.sb("ident_s", [128, 128], BF16)
    tri_s = em.sb("tri_s", [128, 2048], BF16)
    iota_s = em.sb("iota_s", [128, 88], F32)
    iotac_s = em.sb("iotac_s", [88, 1], F32)
    ohxg_s = em.sb("ohxg_s", [128, DMT * GG], BF16)
    ohxu_s = em.sb("ohxu_s", [128, DMT * GG], BF16)
    bias_all = em.sb("bias_all", [128, 64], F32)
    bq_s = em.sb("bq_s", [128, 4], F32)
    bk_s = em.sb("bk_s", [128, 4], F32)
    bv_s = em.sb("bv_s", [128, 4], F32)
    bo_s = em.sb("bo_s", [128, DMT], F32)
    xg_s = em.sb("xg_s", [GG, 1024], BF16)
    hg_s = em.sb("hg_s", [88, 512], BF16)
    dq_gi = em.sb("dq_gi", [88, 256], I32)
    dq_gf = em.sb("dq_gf", [88, 256], F32)
    dq_ohT = em.sb("dq_ohT", [88, 256], BF16)
    gcol_i = em.sb("gcol_i", [128, 1], I32)
    gcol_f = em.sb("gcol_f", [128, 1], F32)
    oh1 = em.sb("oh1", [128, 88], BF16)

    t0 = em.sb("t0", [128, 1536], F32)
    t1 = em.sb("t1", [128, 1536], F32)
    tb0 = em.sb("tb0", [128, 1536], BF16)
    vtr = em.sb("vtr", [128, 512], BF16)
    rope_c = t0[:, 1024:1536]
    rope_s = t1[:, 1024:1536]
    den_s = t1[:, 0:512]
    rows = em.sb("rows", [1, 1536], F32)
    qn_s = em.sb("qn_s", [1, 64], F32)
    eps_c = em.sb("eps_c", [1, 1], F32)
    kn_s = em.sb("kn_s", [1, 80], F32)
    cneg_s = em.sb("cneg_s", [1, 64], F32)

    PA = em.psum("PA", [128, 1024])
    PB = em.psum("PB", [128, 1024])
    PB2 = em.psum("PB2", [128, 1024])
    PC = em.psum("PC", [128, 512])
    PT = em.psum("PT", [128, 512], BF16)

    sPE = em.sem("sPE"); sACT = em.sem("sACT"); sDVE = em.sem("sDVE")
    em.auto = {"vector": sDVE, "scalar": sACT}
    sCC = em.sem("sCC")
    sCN = em.sem("sCN")
    sTL = [em.sem(f"sTL{i}") for i in range(3)]
    sST = [em.sem(f"sST{i}") for i in range(3)]
    sX = em.sem("sX"); sQ = em.sem("sQ")
    sKV = [em.sem("sKV0"), em.sem("sKV1"), em.sem("sKV2"), em.sem("sKV3")]
    sW = [em.sem("sWg0"), em.sem("sWg1")]
    sD = [em.sem("sWd0"), em.sem("sWd1")]
    sPRS = [em.sem("sPRS0"), em.sem("sPRS1")]
    sDQ = em.sem("sDQ"); sDQ2 = em.sem("sDQ2"); sSC = em.sem("sSC"); sAW = em.sem("sAW")
    sWO = em.sem("sWO"); sSW = em.sem("sSW"); sSW2 = em.sem("sSW2"); sVT = em.sem("sVT")
    sQKV = em.sem("sQKV"); sWS = em.sem("sWS"); sWS2 = em.sem("sWS2"); sLNS = em.sem("sLNS")
    sOUT = em.sem("sOUT")

    rg = [list(range(NC))]

    def collective(kind, op, ins, outs):
        v = sCC.bump(1)
        def f(g, k=kind, o=op, i=ins, u=outs):
            g.collective_compute(k, o, replica_groups=rg, ins=i, outs=u).then_inc(sCC.h, 1)
        em.emit("gpsimd", f)
        return v

    em.dma("sync", ident_s[:, :], ident.ap(), sCN)
    for l in range(4):
        em.dma("sync", tri_s[:, l * 512:(l + 1) * 512],
               tri.ap()[128 * l:128 * (l + 1), :], sCN)
    em.dma("sync", iota_s[:, :], iota88.ap(), sCN)
    em.dma("sync", iotac_s[:, :], iota88c.ap(), sCN)
    em.dma("sync", bq_s[:, :], bq4.ap(), sCN)
    em.dma("sync", bk_s[:, :], bk4.ap(), sCN)
    em.dma("sync", bv_s[:, :], bv4.ap(), sCN)
    em.dma("sync", bo_s[:, :], bo32.ap(), sCN)
    for e in ("vector", "scalar", "tensor", "gpsimd"):
        em.waitf(e, sCN)
    em.vec("memset", ones1[:, :], 1.0)
    em.vec("memset", eps_c[:, :], EPS)
    em.vec("memset", tb0[:, 0:1024], 0.0)
    em.vec("memset", onesc_b[:, :], 1.0, sem=sDVE)
    consts_v = sDVE.n
    em.wait("tensor", sDVE, consts_v)
    em.wait("scalar", sDVE, consts_v)
    em.wait("gpsimd", sDVE, consts_v)
    for shd, w in ((wsg_sh, NTIL * GBLK), (wsu_sh, NTIL * UBLK),
                   (wsd_sh, DMT * DBLK)):
        w1 = (w // 1024) * 1024
        em.dma("gpsimd", shd.ap()[:, 0:w1].rearrange("p (r c) -> p r c", c=1024),
               tb0[:, 0:1024].unsqueeze(1).broadcast_to([128, w1 // 1024, 1024]),
               sWS)
        if w1 < w:
            em.dma("gpsimd", shd.ap()[:, w1:w], tb0[:, 0:w - w1], sWS)
    em.waitf("gpsimd", sWS)
    em.waitf("scalar", sWS)
    em.waitf("vector", sWS)

    def emit_merge_ln_chunk(cc, src_x, src_red, dst_x, dst_lnloc, bias_cols):
        col = cc * 512
        em.waitf("vector", sLNS)
        em.wait("vector", sPE, sPE.n)
        em.wait("sync", sDVE, sDVE.n)
        em.wait("sync", sACT, sACT.n)
        em.waitf("sync", sOUT)
        em.waitf("sync", sPRS[0])
        em.waitf("sync", sPRS[1])
        slot_act = {}
        slot_dve = {}
        prev_stat = 0
        for ft in range(DMT):
            r = ft % 3
            a = t0[:, r * 512:(r + 1) * 512]
            if ft >= 3:
                em.waitf("sync", sST[r])
                em.wait("sync", sACT, slot_act.get(ft - 3, 0))
                em.wait("sync", sDVE, slot_dve.get(ft - 3, 0))
            em.dma("sync", a, src_x[128 * ft:128 * (ft + 1), col:col + 512], sTL[r])
            if src_red is not None:
                b = t1[:, r * 512:(r + 1) * 512]
                em.dma("sync", b, src_red[128 * ft:128 * (ft + 1), 0:512],
                       sTL[r])
                em.waitf("vector", sTL[r])
                if bias_cols is not None:
                    em.vec("tensor_scalar", b, b, bias_cols[:, ft:ft + 1], None,
                           ALU.add)
                em.vec("tensor_add", a, a, b, sem=sDVE)
                mv = sDVE.n
                slot_dve[ft] = mv
                em.wait("gpsimd", sDVE, mv)
                em.dma("gpsimd", dst_x[128 * ft:128 * (ft + 1), col:col + 512], a,
                       sST[r])
                em.wait("scalar", sDVE, mv)
            else:
                em.waitf("scalar", sTL[r])
            xb = tb0[:, 0:512]
            sq = tb0[:, 512:1024]
            em.wait("scalar", sPE, prev_stat)
            em.actv(xb, a, AF.Copy)
            em.actv(sq, a, AF.Square, sem=sACT)
            av = sACT.n
            slot_act[ft] = av
            em.wait("tensor", sACT, av)
            em.mm(PA[0:1, 0:512], onesc_b[:, :], xb, start=(ft == 0),
                  stop=(ft == DMT - 1))
            prev_stat = em.mm(PA[0:1, 512:1024], onesc_b[:, :], sq, start=(ft == 0),
                              stop=(ft == DMT - 1), sem=sPE)
        pe_done = sPE.n
        em.wait("vector", sPE, pe_done)
        em.vec("tensor_scalar", rows[0:1, 0:512], PA[0:1, 0:512], 1.0 / DM, None,
               ALU.mult)
        em.vec("tensor_scalar", rows[0:1, 512:1024], PA[0:1, 512:1024], 1.0 / DM,
               None, ALU.mult)
        em.vec("tensor_mul", rows[0:1, 1024:1536], rows[0:1, 0:512],
               rows[0:1, 0:512])
        em.vec("tensor_sub", rows[0:1, 512:1024], rows[0:1, 512:1024],
               rows[0:1, 1024:1536], sem=sDVE)
        vv = sDVE.n
        em.wait("scalar", sDVE, vv)
        em.actv(rows[0:1, 1024:1536], rows[0:1, 512:1024], AF.Ln,
                bias=eps_c[0:1, :])
        em.actv(rows[0:1, 512:1024], rows[0:1, 1024:1536], AF.Exp, scale=-0.5,
                sem=sACT)
        av = sACT.n
        em.wait("vector", sACT, av)
        em.vec("tensor_mul", rows[0:1, 1024:1536], rows[0:1, 0:512],
               rows[0:1, 512:1024], sem=sDVE)
        mrv = sDVE.n
        em.wait("tensor", sDVE, mrv)
        em.mm(PB[:, 0:512], ones1[:, :], rows[0:1, 512:1024], start=True, stop=True)
        em.mm(PB[:, 512:1024], ones1[:, :], rows[0:1, 1024:1536], start=True,
              stop=True, sem=sPE)
        bcv = sPE.n
        rdsrc = dst_x if (src_red is not None and dst_x is not None) else src_x
        for i in range(3):
            em.waitf("sync", sST[i])
        em.wait("sync", sDVE, sDVE.n)
        em.wait("sync", sACT, sACT.n)
        slot_dve2 = {}
        for ft in range(DMT):
            r = ft % 3
            a = t0[:, r * 512:(r + 1) * 512]
            if ft >= 3:
                em.wait("sync", sDVE, slot_dve2.get(ft - 3, 0))
            em.dma("sync", a, rdsrc[128 * ft:128 * (ft + 1), col:col + 512], sTL[r])
            em.waitf("vector", sTL[r])
            em.wait("vector", sPE, bcv)
            b = t1[:, r * 512:(r + 1) * 512]
            em.vec("tensor_mul", b, a, PB[:, 0:512])
            ln_t = x_sb[:, ft * 512:(ft + 1) * 512]
            em.vec("tensor_sub", ln_t, b, PB[:, 512:1024], sem=sDVE)
            lv = sDVE.n
            slot_dve2[ft] = lv
            if dst_lnloc is not None:
                em.wait("gpsimd", sDVE, lv)
                em.dma("gpsimd", dst_lnloc[128 * ft:128 * (ft + 1), 0:512],
                       ln_t, sLNS)

    ag_half = {}

    def mk_ag_ln(h):
        return collective(
            "AllGather", ALU.bypass,
            [ln_loc.ap()[h * DM:(h + 1) * DM, :].opt()],
            [ln_full.ap()[h * NC * DM:(h + 1) * NC * DM, :].opt()])

    for cc in range(NCH):
        emit_merge_ln_chunk(cc, x0T.ap(), None, None,
                            ln_loc.ap()[cc * DM:(cc + 1) * DM, :], None)
        em.waitf("gpsimd", sLNS)
        ag_half[cc] = mk_ag_ln(cc)
    ag1 = ag_half[1]

    if stop_after == "ln1":
        _finish(em, tap_pairs, sOUT, sCC)
        em.replay_into(nc)
        return nc

    def blocked(dst_dram, blk):
        return dst_dram.ap().rearrange("p (b c) -> p b c", c=blk)

    def emit_dequant_gen(qb_ap, sc_ap, qz_ap, gidx_ap, nkt, ncols, G, nblk, blk,
                         dst_dram, want_ohh, gidx_oh_ap, wofs):
        em.wait("sync", sDVE, sDVE.n)
        em.wait("gpsimd", sPE, sPE.n)
        em.dma("gpsimd", sc_bf[0:G, 0:ncols], sc_ap, sSC)
        em.waitf("tensor", sSC)
        dst = blocked(dst_dram, blk)
        dbuf = ncols <= 2048
        back = 2 if dbuf else 1
        unp0, mm0 = sDVE.n, sPE.n
        st0 = {0: sWS.n, 1: sWS2.n}
        unp_mark = {}
        mm_mark = {}
        store_mark = {}
        for kt in range(nkt):
            p = (kt % 2) if dbuf else 0
            u8v = dq_u8[:, p * 2048:p * 2048 + ncols]
            fv = dq_f[:, p * 2048:p * 2048 + ncols]
            bv = dq_b[:, p * 2048:p * 2048 + ncols]
            giv = dq_gi[0:G, p * 128:p * 128 + 128]
            gfv = dq_gf[0:G, p * 128:p * 128 + 128]
            ohv = dq_ohT[0:G, p * 128:p * 128 + 128]
            em.wait("sync", sDVE, unp_mark.get(kt - back, unp0))
            bsrc = qb_ap[64 * kt:64 * (kt + 1), :].unsqueeze(0) \
                .broadcast_to([2, 64, ncols])
            dqsem = sDQ2 if (dbuf and p == 1) else sDQ
            em.dma("sync", u8v, bsrc, dqsem)
            gsrc = gidx_ap[kt * 128:(kt + 1) * 128, 0:1] \
                .rearrange("k x -> x k").broadcast_to([G, 128])
            em.dma("sync", giv, gsrc, dqsem)
            em.waitf("vector", dqsem)
            wsem = sWS2 if (dbuf and p == 1) else sWS
            em.wait("vector", wsem, store_mark.get(kt - back, st0[p]))
            em.wait("vector", sPE, mm_mark.get(kt - back, mm0))
            em.vec("tensor_copy", gfv,
                   giv.rearrange("g (k2 o) -> g k2 o", o=2)
                   .transpose([0, 2, 1]))
            em.vec("tensor_scalar", ohv, gfv, iotac_s[0:G, :],
                   None, ALU.is_equal)
            em.vec("tensor_scalar", u8v[0:64, :], u8v[0:64, :],
                   15, None, ALU.bitwise_and)
            em.vec("tensor_scalar", u8v[64:128, :], u8v[64:128, :],
                   4, 15, ALU.logical_shift_right, ALU.bitwise_and)
            em.vec("tensor_copy", fv, u8v, sem=sDVE)
            dv = sDVE.n
            unp_mark[kt] = dv
            nchk = (ncols + 511) // 512
            for ch in range(nchk):
                a0 = ch * 512
                a1 = min(ncols, a0 + 512)
                em.wait("tensor", sDVE, dv if ch == 0 else sDVE.n)
                em.mm(PC[:, 0:a1 - a0], ohv, sc_bf[0:G, a0:a1],
                      start=True, stop=True, sem=sPE)
                em.wait("vector", sPE, sPE.n)
                em.vec("tensor_mul", bv[:, a0:a1], fv[:, a0:a1],
                       PC[:, 0:a1 - a0], sem=sDVE)
            mm_mark[kt] = sPE.n
            em.wait("gpsimd", sDVE, sDVE.n)
            dstp = dst_dram.ap().rearrange("(k2 o) q -> k2 o q", o=2) \
                .transpose([1, 0, 2]) \
                .rearrange("o k2 (b c) -> o k2 b c", c=blk)[
                    :, :, :, kt * 128:(kt + 1) * 128]
            cpb = ncols // nblk
            bstep2 = max(1, 1536 // 64)
            for o in range(2):
                for b0 in range(0, nblk, bstep2):
                    b1 = min(nblk, b0 + bstep2)
                    em.dma("gpsimd", dstp[o][:, b0:b1, :],
                           bv[64 * o:64 * (o + 1), b0 * cpb:b1 * cpb]
                           .rearrange("p (b c) -> p b c", b=b1 - b0), wsem)
            store_mark[kt] = wsem.n
            yield
        unpack_mark = unp_mark.get(nkt - 1, 0)
        nq = ncols // 8
        em.wait("sync", sDVE, unpack_mark)
        em.dma("sync", zz_i[0:G, 0:nq], qz_ap, sDQ)
        em.dma("sync", zz_n[0:G, 0:ncols], sc_ap, sDQ)
        em.waitf("vector", sDQ)
        em.waitf("vector", sWS)
        em.waitf("vector", sWS2)
        for sh in range(8):
            em.vec("tensor_scalar",
                   zz_i[0:G, 512:512 + ncols].rearrange("g (c e) -> g c e", e=8)[
                       :, :, sh:sh + 1],
                   zz_i[0:G, 0:nq], 4 * sh, 15,
                   ALU.logical_shift_right, ALU.bitwise_and)
        em.vec("tensor_copy", zz_f[0:G, 0:ncols], zz_i[0:G, 512:512 + ncols])
        em.vec("tensor_scalar", zz_n[0:G, 0:ncols], zz_n[0:G, 0:ncols], -1.0, None,
               ALU.mult)
        em.vec("scalar_tensor_tensor", dq_b[0:G, 0:ncols], zz_f[0:G, 0:ncols], 1.0,
               zz_n[0:G, 0:ncols], ALU.add, ALU.mult, sem=sDVE)
        em.wait("gpsimd", sDVE, sDVE.n)
        bstep = max(1, 1536 // G)
        for b0 in range(0, nblk, bstep):
            b1 = min(nblk, b0 + bstep)
            em.dma("gpsimd", dst[0:G, b0:b1, wofs:wofs + 128],
                   dq_b[0:G, b0 * (ncols // nblk):b1 * (ncols // nblk)]
                   .rearrange("g (b c) -> g b c", b=b1 - b0), sWS)
        yield
        if want_ohh:
            for bl in range(nblk):
                em.wait("sync", sDVE, sDVE.n)
                em.dma("sync", gcol_i[:, :],
                       gidx_oh_ap[bl * 128:(bl + 1) * 128, 0:1], sDQ)
                em.waitf("vector", sDQ)
                em.waitf("vector", sWS)
                em.waitf("vector", sWS2)
                em.vec("tensor_copy", gcol_f[:, :], gcol_i[:, :])
                em.vec("tensor_scalar", oh1[:, :], iota_s[:, 0:88], gcol_f[:, :],
                       None, ALU.is_equal, sem=sDVE)
                em.wait("gpsimd", sDVE, sDVE.n)
                em.dma("gpsimd", dst[:, bl:bl + 1, wofs + 128:wofs + 216],
                       oh1[:, :].unsqueeze(1), sWS)
                yield

    agd_d = {}

    def emit_ohx_gen(gidx_ap, dst):
        for kt in range(DMT):
            em.wait("sync", sDVE, sDVE.n)
            em.dma("sync", gcol_i[:, :], gidx_ap[kt * 128:(kt + 1) * 128, 0:1], sDQ)
            em.waitf("vector", sDQ)
            em.vec("tensor_copy", gcol_f[:, :], gcol_i[:, :])
            em.vec("tensor_scalar", dst[:, kt * GG:(kt + 1) * GG], iota_s[:, 0:GG],
                   gcol_f[:, :], None, ALU.is_equal, sem=sDVE)
            yield

    def deq_master():
        yield from emit_dequant_gen(qbg.ap(), scg_sh.ap(), qzg_sh.ap(), gig.ap(),
                                    DMT, SHN, GG, NTIL, GBLK, wsg_sh, True,
                                    gidsh.ap(), DM)
        em.waitf("gpsimd", sWS)
        em.waitf("gpsimd", sWS2)
        agd_d["agg"] = collective("AllGather", ALU.bypass, [wsg_sh.ap().opt()],
                                  [wsg.ap().opt()])
        yield
        yield from emit_dequant_gen(qbu.ap(), scu_sh.ap(), qzu_sh.ap(), giu.ap(),
                                    DMT, SHN, GG, NTIL, UBLK, wsu_sh, False,
                                    None, DM)
        em.waitf("gpsimd", sWS)
        em.waitf("gpsimd", sWS2)
        agd_d["agu"] = collective("AllGather", ALU.bypass, [wsu_sh.ap().opt()],
                                  [wsu.ap().opt()])
        yield
        yield from emit_dequant_gen(qbd.ap(), scd.ap(), qzd.ap(), gidsh.ap(),
                                    NTIL, DM, GD, DMT, DBLK, wsd_sh, False,
                                    None, SHN)
        em.waitf("gpsimd", sWS)
        em.waitf("gpsimd", sWS2)
        agd_d["agd"] = collective("AllGather", ALU.bypass, [wsd_sh.ap().opt()],
                                  [wsd.ap().opt()])
        yield
        yield from emit_ohx_gen(gig.ap(), ohxg_s)
        yield from emit_ohx_gen(giu.ap(), ohxu_s)

    deq = deq_master()
    if stop_after == "dequant":
        for _ in deq:
            pass
        _finish(em, tap_pairs, sOUT, sCC)
        em.replay_into(nc)
        return nc

    def emit_attn(cfg):
        nkt = cfg["kw"] // 128
        WREG = cfg["wreg"]
        XBUF = cfg["xbuf"]
        STP = cfg["statps"]
        if cfg["per_head_kv"]:
            passes = [[("q", 0), ("q", 1), ("q", 2), ("q", 3), ("k", 0), ("k", 1),
                       ("k", 2), ("k", 3), ("v", 0), ("v", 1), ("v", 2), ("v", 3)]]
        else:
            passes = [[("q", 0), ("q", 1), ("q", 2), ("q", 3), ("k", 0), ("v", 0)]]

        def wslot(wi):
            if wi < 11:
                return WREG[:, wi * DM:(wi + 1) * DM]
            return R3[:, 0:DM]
        dram_of = {"q": cfg["qT"], "k": cfg["kT"], "v": cfg["vT"]}
        wsrc_of = {"q": cfg["wq"], "k": cfg["wk"], "v": cfg["wv"]}
        for pi, plist in enumerate(passes):
            em.wait("gpsimd", sPE, sPE.n)
            em.wait("gpsimd", sDVE, sDVE.n)
            em.waitf("gpsimd", sLNS)
            em.waitf("sync", sLNS)
            for wi, (nm, nt) in enumerate(plist):
                src = wsrc_of[nm][:, nt * 128:(nt + 1) * 128] \
                    .rearrange("(dt p) n -> p dt n", p=128)
                em.dma("gpsimd", wslot(wi)
                       .rearrange("p (dt n) -> p dt n", dt=DMT), src, sAW)
            em.waitf("tensor", sAW)
            last_evac = {}
            for sc_i in (list(range(0, NMAC, 2)) + list(range(1, NMAC, 2))):
                em.wait("sync", sCC, cfg["wait_cc_half"][sc_i % 2])
                em.wait("sync", sPE, sPE.n)
                em.wait("sync", sDVE, sDVE.n)
                base = (sc_i % 2) * NC * DM + (sc_i // 2) * DM
                xsrc = ln_full.ap()[base:base + DM, 0:512] \
                    .rearrange("(dt p) c -> p dt c", p=128)
                em.dma("sync", XBUF[:, :].rearrange("p (dt c) -> p dt c", dt=DMT),
                       xsrc, sX)
                if cfg["rope"]:
                    em.dma("sync", rope_c,
                           cosR.ap()[:, sc_i * 512:(sc_i + 1) * 512], sX)
                    em.dma("sync", rope_s,
                           sinS.ap()[:, sc_i * 512:(sc_i + 1) * 512], sX)
                em.waitf("tensor", sX)
                em.waitf("vector", sX)
                for wi, (nm, nt) in enumerate(plist):
                    half = wi % 2
                    acc = PA[:, half * 512:(half + 1) * 512]
                    prevev = last_evac.get(half)
                    if prevev is not None:
                        em.wait("tensor", prevev[0], prevev[1])
                    wsl = wslot(wi)
                    for dt_i in range(DMT):
                        em.mm(acc, wsl[:, dt_i * 128:dt_i * 128 + 128],
                              XBUF[:, dt_i * 512:(dt_i + 1) * 512],
                              start=(dt_i == 0), stop=(dt_i == DMT - 1),
                              sem=(sPE if dt_i == DMT - 1 else None))
                    mmv = sPE.n
                    em.wait("scalar", sPE, mmv)
                    if nm == "v":
                        vt = tb0[:, 0:512]
                        em.wait("vector", sPE, sPE.n)
                        if cfg["biased"]:
                            em.vec("tensor_scalar", vt, acc, bv_s[:, nt:nt + 1],
                                   None, ALU.add, sem=sDVE)
                        else:
                            em.vec("tensor_copy", vt, acc, sem=sDVE)
                        av = sDVE.n
                        last_evac[half] = (sDVE, av)
                        em.wait("tensor", sDVE, av)
                        for sub in range(4):
                            em.trans(PT[:, sub * 128:(sub + 1) * 128],
                                     vt[:, sub * 128:(sub + 1) * 128],
                                     ident_s[:, :],
                                     sem=(sPE if sub == 3 else None))
                        tv = sPE.n
                        em.wait("vector", sPE, tv)
                        em.waitf("vector", sVT)
                        em.vec("tensor_copy", vtr[:, :], PT[:, :], sem=sDVE)
                        em.wait("sync", sDVE, sDVE.n)
                        em.dma("sync",
                               dram_of["v"][sc_i * 512:(sc_i + 1) * 512,
                                            nt * 128:(nt + 1) * 128]
                               .rearrange("(sub s) d -> sub s d", sub=4)
                               .transpose([1, 0, 2]),
                               vtr[:, :].rearrange("p (sub d) -> p sub d", sub=4),
                               sVT)
                        continue
                    if cfg["rope"]:
                        a = t0[:, 0:512]
                        em.waitf("scalar", sSW)
                        em.wait("scalar", sDVE, sDVE.n)
                        em.actv(a, acc, AF.Copy, sem=sACT)
                        av = sACT.n
                        last_evac[half] = (sACT, av)
                        em.wait("sync", sACT, av)
                        em.wait("sync", sDVE, sDVE.n)
                        em.dma("sync", t1[0:64, 0:512], a[64:128, 0:512], sSW)
                        em.dma("sync", t1[64:128, 0:512], a[0:64, 0:512], sSW)
                        em.waitf("vector", sSW)
                        em.vec("tensor_mul", t0[:, 512:1024], a, rope_c)
                        em.vec("tensor_mul", t1[:, 512:1024], t1[:, 0:512], rope_s)
                        qk_t = tb0[:, 512:1024]
                        em.waitf("vector", sQKV)
                        em.vec("tensor_add", qk_t, t0[:, 512:1024],
                               t1[:, 512:1024], sem=sDVE)
                        dv = sDVE.n
                        em.wait("sync", sDVE, dv)
                        em.wait("scalar", sDVE, dv)
                    else:
                        bias_ap = (bq_s if nm == "q" else bk_s)[:, nt:nt + 1]
                        qk_t = tb0[:, 512:1024]
                        em.waitf("vector", sQKV)
                        em.wait("vector", sPE, sPE.n)
                        em.wait("vector", sACT, sACT.n)
                        em.vec("tensor_scalar", qk_t, acc, bias_ap, None, ALU.add,
                               sem=sDVE)
                        av = sDVE.n
                        last_evac[half] = (sDVE, av)
                        em.wait("sync", sDVE, av)
                    em.dma("sync", dram_of[nm][nt * 128:(nt + 1) * 128,
                                                 sc_i * 512:(sc_i + 1) * 512],
                           qk_t, sQKV)
                    em.wait("scalar", sPE, sPE.n)
                    em.wait("scalar", sDVE, sDVE.n)
                    em.actv(tb0[:, 1024:1536], qk_t, AF.Square, sem=sACT)
                    av = sACT.n
                    em.wait("tensor", sACT, av)
                    em.wait("tensor", sDVE, sDVE.n)
                    em.mm(STP[0:1, 0:512], onesc_b[:, :], tb0[:, 1024:1536],
                          start=True, stop=True, sem=sPE)
                    em.wait("vector", sPE, sPE.n)
                    dst_row = qn_s if nm == "q" else kn_s
                    em.vec("tensor_reduce",
                           dst_row[0:1, nt * 16 + sc_i:nt * 16 + sc_i + 1],
                           STP[0:1, 0:512], AXX, ALU.max, sem=sDVE)
                    if wi == 2 and cfg.get("interleave"):
                        cfg["interleave"]()
                if cfg.get("interleave"):
                    cfg["interleave"]()

        em.wait("vector", sDVE, sDVE.n)
        for j in range(nkt):
            em.vec("tensor_reduce", kn_s[0:1, 64 + j:65 + j],
                   kn_s[0:1, j * 16:(j + 1) * 16], AXX, ALU.max)
        if nkt == 1:
            em.vec("tensor_mul", cneg_s[0:1, :], qn_s[0:1, :],
                   kn_s[0:1, 64:65].broadcast_to([1, 64]), sem=sDVE)
        else:
            em.vec("tensor_mul",
                   cneg_s[0:1, :].rearrange("o (h m) -> o h m", h=4),
                   qn_s[0:1, :].rearrange("o (h m) -> o h m", h=4),
                   kn_s[0:1, 64:68].unsqueeze(2).broadcast_to([1, 4, 16]),
                   sem=sDVE)
        dv = sDVE.n
        em.wait("scalar", sDVE, dv)
        em.actv(rows[0:1, 0:64], cneg_s[0:1, :], AF.Ln)
        em.actv(rows[0:1, 64:128], rows[0:1, 0:64], AF.Exp, scale=0.5, sem=sACT)
        av = sACT.n
        em.wait("vector", sACT, av)
        em.vec("tensor_scalar", cneg_s[0:1, :], rows[0:1, 64:128], -SCALE, None,
               ALU.mult, sem=sDVE)
        dv = sDVE.n
        em.wait("tensor", sDVE, dv)
        em.mm(PC[:, 0:64], ones1[:, :], cneg_s[0:1, :], start=True, stop=True,
              sem=sPE)
        em.wait("vector", sPE, sPE.n)
        em.vec("tensor_copy", bias_all[:, :], PC[:, 0:64], sem=sDVE)
        em.wait("tensor", sDVE, sDVE.n)

        em.wait("gpsimd", sPE, sPE.n)
        wo_src = cfg["wo"].rearrange("(et p) f -> p et f", p=128)
        em.dma("gpsimd", wo_sb[:, :].rearrange("p (et f) -> p et f", et=4),
               wo_src, sWO)

        ones33 = R3[:, 16384:16640].bitcast(F32)
        em.vec("memset", ones33[32:33, 0:128], 1.0, sem=sDVE)
        em.waitf("sync", sQKV)
        em.waitf("sync", sVT)
        exp_done = {0: 0, 1: 0}
        pslot_pv = [0, 0, 0, 0]
        bctr = 0
        pend = [None]
        kv_done = {}

        def flush_pend():
            if pend[0] is None:
                return
            (fh, fh2, fkc, fb, fslot, fmi, fuse_sem, fuse_val, fpar2) = pend[0]
            pend[0] = None
            em.wait("tensor", fuse_sem, fuse_val)
            pq = p_sb[:, fslot * 1024: fslot * 1024 + 1024]
            first = (fkc == 0)
            last = (fkc == fmi)
            for j in range(2):
                let = 2 * fb + j
                em.mm(PC[32 * fh2:32 * fh2 + 1, 0:512], onesc_b[:, :],
                      pq[:, j * 512:(j + 1) * 512],
                      start=(first and let == 0), stop=(last and let == 3),
                      tile_position=(0, 32 * fh2))
            v = None
            for j in range(2):
                let = 2 * fb + j
                if cfg["per_head_kv"]:
                    vl = kv_v[:, fpar2 * 2048 + let * 512 + fh * 128:
                              fpar2 * 2048 + let * 512 + fh * 128 + 128]
                else:
                    vl = kv_v[:, fpar2 * 512 + let * 128:
                              fpar2 * 512 + let * 128 + 128]
                v = em.mm(PA[:, fh2 * 512:(fh2 + 1) * 512], vl,
                          pq[:, j * 512:(j + 1) * 512],
                          start=(first and let == 0), stop=(last and let == 3),
                          sem=sPE)
            pslot_pv[fslot] = v
            if fh2 == 1 and fb == 1:
                kv_done[fkc] = v

        macro_order = list(range(0, NMAC, 2)) + list(range(1, NMAC, 2))
        for mseq, mi in enumerate(macro_order):
            em.wait("sync", sPE, sPE.n)
            em.dma("sync", q_mac[:, :].rearrange("p (h c) -> p h c", h=4),
                   cfg["qT"][0:512, mi * 512:(mi + 1) * 512]
                   .rearrange("(h p) c -> p h c", p=128), sQ)
            phk = cfg["per_head_kv"]
            nslot = 2 if phk else 4
            swid = 2048 if phk else 512
            pre = 1 if phk else 2
            for hp in range(2):
                kv_done.clear()
                kv_mark = {}
                em.wait("sync", sPE, sPE.n)

                def issue_kv(kc, mi=mi):
                    sl = kc % nslot
                    g = kv_done.get(kc - nslot)
                    if g:
                        em.wait("sync", sPE, g)
                    base = sl * swid
                    if phk:
                        for hf in range(2):
                            ks = cfg["kT"][hf * 256:(hf + 1) * 256,
                                           kc * 512:(kc + 1) * 512] \
                                .rearrange("(h p) c -> p h c", p=128)
                            em.dma("sync",
                                   kv_k[:, base + hf * 1024:base + hf * 1024 + 1024]
                                   .rearrange("p (h c) -> p h c", h=2), ks,
                                   sKV[sl])
                            vs = cfg["vT"][kc * 512 + hf * 256:
                                           kc * 512 + (hf + 1) * 256, 0:512] \
                                .rearrange("(l p) d -> p l d", p=128)
                            em.dma("sync",
                                   kv_v[:, base + hf * 1024:base + hf * 1024 + 1024]
                                   .rearrange("p (l d) -> p l d", l=2), vs,
                                   sKV[sl])
                    else:
                        ks = cfg["kT"][0:128, kc * 512:(kc + 1) * 512]
                        vs = cfg["vT"][kc * 512:(kc + 1) * 512, 0:128] \
                            .rearrange("(l p) d -> p l d", p=128)
                        em.dma("sync", kv_k[:, base:base + 512], ks, sKV[sl])
                        em.dma("sync", kv_v[:, base:base + 512]
                               .rearrange("p (l d) -> p l d", l=4), vs, sKV[sl])
                    kv_mark[kc] = sKV[sl].n

                for kp in range(min(pre, mi + 1)):
                    issue_kv(kp)
                for kc in range(mi + 1):
                    par2 = kc % nslot
                    em.wait("tensor", sKV[par2], kv_mark[kc])
                    nxt = kc + pre
                    if nxt <= mi and nxt not in kv_mark:
                        if phk:
                            flush_pend()
                        issue_kv(nxt)
                    em.waitf("tensor", sQ)
                    for h2 in range(2):
                        h = hp * 2 + h2
                        for b in range(2):
                            par = bctr % 2
                            slot = bctr % 4
                            bctr += 1
                            buf = PB if par == 0 else PB2
                            if exp_done[par]:
                                em.wait("tensor", sACT, exp_done[par])
                            qk = None
                            for j in range(2):
                                let = 2 * b + j
                                if cfg["per_head_kv"]:
                                    lt = kv_k[:, par2 * 2048 + h * 512 + let * 128:
                                              par2 * 2048 + h * 512 + let * 128 + 128]
                                else:
                                    lt = kv_k[:, par2 * 512 + let * 128:
                                              par2 * 512 + let * 128 + 128]
                                qk = em.mm(buf[:, j * 512:(j + 1) * 512], lt,
                                           q_mac[:, h * 512:(h + 1) * 512],
                                           start=True, stop=True, sem=sPE)
                            flush_pend()
                            em.wait("scalar", sPE, max(qk, pslot_pv[slot]))
                            pq = p_sb[:, slot * 1024: slot * 1024 + 1024]
                            em.actv(pq, buf[:, :], AF.Exp,
                                    bias=bias_all[:, h * 16 + mi:h * 16 + mi + 1],
                                    scale=SCALE, sem=sACT)
                            ev = sACT.n
                            exp_done[par] = ev
                            use_sem, use_val = sACT, ev
                            if kc == mi:
                                em.wait("vector", sACT, ev)
                                em.vec("tensor_mul", pq, pq,
                                       tri_s[:, b * 1024:(b + 1) * 1024], sem=sDVE)
                                use_sem, use_val = sDVE, sDVE.n
                            pend[0] = (h, h2, kc, b, slot, mi, use_sem, use_val,
                                       par2)
                flush_pend()
                pv_all = sPE.n
                em.wait("vector", sPE, pv_all)
                em.waitf("vector", sPRS[0])
                em.waitf("vector", sPRS[1])
                em.waitf("vector", sSW)
                em.vec("tensor_copy", den_s[0:1, :], PC[0:1, :])
                em.vec("tensor_copy", den_s[32:33, :], PC[32:33, :])
                em.vec("reciprocal", den_s[0:1, :], den_s[0:1, :])
                em.vec("reciprocal", den_s[32:33, :], den_s[32:33, :], sem=sDVE)
                rv = sDVE.n
                em.wait("tensor", sDVE, rv)
                for h2 in range(2):
                    h = hp * 2 + h2
                    dsrc = den_s[0:1, :] if h2 == 0 else den_s[32:33, :]
                    lsrc = ones1[:, :] if h2 == 0 else ones33[32:33, :]
                    em.wait("tensor", sDVE, sDVE.n)
                    em.mm(PB[:, 0:512], lsrc, dsrc,
                          start=True, stop=True, sem=sPE)
                    bv_ = sPE.n
                    em.wait("scalar", sPE, bv_)
                    em.wait("scalar", sDVE, sDVE.n)
                    em.actv(t0[:, 0:512], PA[:, h2 * 512:(h2 + 1) * 512], AF.Copy,
                            sem=sACT)
                    av = sACT.n
                    em.wait("vector", sACT, av)
                    em.wait("vector", sPE, bv_)
                    em.vec("tensor_mul", att_sb[:, h * 512:(h + 1) * 512],
                           t0[:, 0:512], PB[:, 0:512], sem=sDVE)
                em.wait("tensor", sDVE, sDVE.n)
                exp_done[0] = 0
                exp_done[1] = 0
                em.wait("tensor", sACT, sACT.n)
            em.waitf("tensor", sWO)
            op_evac = {}
            for ft in range(DMT):
                half = ft % 2
                bank = PB2[:, half * 512:half * 512 + 512]
                if ft >= 2:
                    em.wait("tensor", sDVE, op_evac[ft - 2])
                for et in range(4):
                    em.mm(bank, wo_sb[:, (et * DMT + ft) * 128:
                                      (et * DMT + ft) * 128 + 128],
                          att_sb[:, et * 512:(et + 1) * 512],
                          start=(et == 0), stop=(et == 3),
                          sem=(sPE if et == 3 else None))
                pv = sPE.n
                em.wait("vector", sPE, pv)
                em.waitf("vector", sPRS[0])
                em.waitf("vector", sPRS[1])
                em.vec("tensor_copy", t1[:, half * 512:half * 512 + 512], bank,
                       sem=sDVE)
                op_evac[ft] = sDVE.n
                em.wait("gpsimd", sDVE, op_evac[ft])
                em.dma("gpsimd",
                       partial.ap()[(mi % 2) * NC * DM + (mi // 2) * DM + 128 * ft:
                                    (mi % 2) * NC * DM + (mi // 2) * DM + 128 * (ft + 1),
                                    0:512],
                       t1[:, half * 512:half * 512 + 512], sPRS[mi % 2])
            em.wait("tensor", sACT, sACT.n)
            em.wait("tensor", sDVE, sDVE.n)
            if mseq == NMAC // 2 - 1:
                em.waitf("gpsimd", sPRS[0])
                cfg["rs_out"].append(cfg["mk_rs"](0))
            elif mseq == NMAC - 1:
                em.waitf("gpsimd", sPRS[1])
                cfg["rs_out"].append(cfg["mk_rs"](1))

    def mk_rs(halfsel):
        return collective(
            "ReduceScatter", ALU.add,
            [partial.ap()[halfsel * NC * DM:(halfsel + 1) * NC * DM, :].opt()],
            [red.ap()[halfsel * DM:(halfsel + 1) * DM, :].opt()])

    def inter_deq():
        for _ in range(8):
            if next(deq, "DONE") == "DONE":
                break

    cfg_a = dict(qT=qaD.ap(), kT=kaD.ap(), vT=vaD.ap(), per_head_kv=False,
                 rope=True, biased=False, wq=waqT.ap(), wk=wakT.ap(),
                 wv=wavT.ap(), wo=waoT.ap(), kw=128,
                 wait_cc_half=(ag_half[0], ag_half[1]),
                 wreg=R2, xbuf=R3[:, 0:16384], statps=PB2,
                 interleave=inter_deq, mk_rs=mk_rs, rs_out=[])
    emit_attn(cfg_a)
    for _ in deq:
        pass
    rs1 = cfg_a["rs_out"][1]

    if stop_after == "attn_a":
        _finish(em, tap_pairs, sOUT, sCC)
        em.replay_into(nc)
        return nc

    def emit_ffn(cc, src_x, dst_x, dst_lnloc, is_last):
        col = cc * 512
        em.wait("sync", sPE, sPE.n)
        em.wait("tensor", sDVE, sDVE.n)
        for kt in range(DMT):
            em.mm(PC[0:GG, 0:512], ohxg_s[:, kt * GG:(kt + 1) * GG],
                  x_sb[:, kt * 512:(kt + 1) * 512], start=(kt == 0),
                  stop=(kt == DMT - 1))
        for kt in range(DMT):
            em.mm(PB2[0:GG, 0:512], ohxu_s[:, kt * GG:(kt + 1) * GG],
                  x_sb[:, kt * 512:(kt + 1) * 512], start=(kt == 0),
                  stop=(kt == DMT - 1), sem=(sPE if kt == DMT - 1 else None))
        pv = sPE.n
        em.wait("vector", sPE, pv)
        em.vec("tensor_copy", xg_s[:, 0:512], PC[0:GG, 0:512])
        em.vec("tensor_copy", xg_s[:, 512:1024], PB2[0:GG, 0:512], sem=sDVE)
        em.wait("tensor", sDVE, sDVE.n)

        wpar_done = {}
        hpar_done = {}
        for nt in range(NTI):
            parw = nt % 2
            r, ntl = divmod(nt, NTIL)
            woff = parw * (GBLK + UBLK)
            if nt >= 2:
                em.wait("sync", sPE, wpar_done[nt - 2])
            em.dma("sync", R3[:, woff:woff + GBLK],
                   wsg.ap()[r * 128:(r + 1) * 128, ntl * GBLK:(ntl + 1) * GBLK],
                   sW[parw])
            em.dma("sync", R3[:, woff + GBLK:woff + GBLK + UBLK],
                   wsu.ap()[r * 128:(r + 1) * 128, ntl * UBLK:(ntl + 1) * UBLK],
                   sW[parw])
            em.waitf("tensor", sW[parw])
            half = nt % 2
            if nt >= 2:
                em.wait("tensor", sACT, hpar_done[nt - 2][0])
                em.wait("tensor", sDVE, hpar_done[nt - 2][1])
            for gi_, base in ((0, woff), (1, woff + GBLK)):
                acc = (PA if gi_ == 0 else PB)[:, half * 512:(half + 1) * 512]
                for kt in range(DMT):
                    em.mm(acc, R3[:, base + kt * 128:base + kt * 128 + 128],
                          x_sb[:, kt * 512:(kt + 1) * 512], start=(kt == 0),
                          stop=False)
                em.mm(acc, R3[0:GG, base + DM:base + DM + 128],
                      xg_s[0:GG, gi_ * 512:gi_ * 512 + 512],
                      start=False, stop=True, sem=sPE)
            gu = sPE.n
            em.wait("scalar", sPE, gu)
            sil = tb0[:, half * 512:half * 512 + 512]
            em.actv(sil, PA[:, half * 512:(half + 1) * 512], AF.Silu, sem=sACT)
            av = sACT.n
            em.wait("vector", sACT, av)
            em.vec("tensor_mul", h_sb[:, nt * 512:(nt + 1) * 512], sil,
                   PB[:, half * 512:(half + 1) * 512], sem=sDVE)
            hv = sDVE.n
            hpar_done[nt] = (av, hv)
            em.wait("tensor", sDVE, hv)
            wpar_done[nt] = em.mm(
                PC[0:88, 0:512], R3[:, woff + DM + 128:woff + DM + 216],
                h_sb[:, nt * 512:(nt + 1) * 512], start=(nt == 0),
                stop=(nt == NTI - 1), sem=sPE)
        hgv = sPE.n
        em.wait("vector", sPE, hgv)
        em.vec("tensor_copy", hg_s[:, :], PC[0:88, 0:512], sem=sDVE)
        em.wait("tensor", sDVE, sDVE.n)

        em.wait("sync", sPE, sPE.n)
        em.wait("sync", sDVE, sDVE.n)
        em.wait("scalar", sDVE, sDVE.n)
        prev_stat2 = 0
        wd_done = {}
        dn_evac = {}
        slot_dve = {}
        for ft in range(DMT):
            parw = ft % 2
            wdoff = parw * (NC * DBLK)
            if ft >= 2:
                em.wait("sync", sPE, wd_done[ft - 2])
            for r in range(NC):
                em.dma("sync", R2[:, wdoff + r * DBLK: wdoff + (r + 1) * DBLK],
                       wsd.ap()[r * 128:(r + 1) * 128,
                                ft * DBLK:(ft + 1) * DBLK], sD[parw])
            em.waitf("tensor", sD[parw])
            bank = PB2[:, parw * 512:parw * 512 + 512]
            if ft >= 2:
                em.wait("tensor", sACT, dn_evac[ft - 2])
            for kt in range(NTI):
                r2, kl = divmod(kt, NTIL)
                em.mm(bank, R2[:, wdoff + r2 * DBLK + kl * 128:
                               wdoff + r2 * DBLK + kl * 128 + 128],
                      h_sb[:, kt * 512:(kt + 1) * 512], start=(kt == 0),
                      stop=False)
            em.mm(bank, R2[0:GD, wdoff + SHN: wdoff + SHN + 128],
                  hg_s[0:GD, 0:512], start=False, stop=True, sem=sPE)
            dv_ = sPE.n
            wd_done[ft] = dv_
            em.wait("scalar", sPE, dv_)
            rsl = ft % 3
            a = t0[:, rsl * 512:(rsl + 1) * 512]
            if ft >= 3:
                em.waitf("scalar", sST[rsl])
                if is_last:
                    em.waitf("scalar", sOUT)
                em.wait("sync", sDVE, slot_dve.get(ft - 3, 0))
            em.actv(a, bank, AF.Copy, sem=sACT)
            dn_evac[ft] = sACT.n
            b = t1[:, rsl * 512:(rsl + 1) * 512]
            em.dma("sync", b, src_x[128 * ft:128 * (ft + 1), col:col + 512],
                   sTL[rsl])
            em.waitf("vector", sTL[rsl])
            em.wait("vector", sACT, dn_evac[ft])
            em.vec("tensor_add", a, a, b, sem=sDVE)
            mv = sDVE.n
            slot_dve[ft] = mv
            em.wait("gpsimd", sDVE, mv)
            if is_last:
                em.dma("gpsimd", out_ext.ap()[128 * ft:128 * (ft + 1),
                                              col:col + 512], a, sOUT)
            else:
                em.dma("gpsimd", dst_x[128 * ft:128 * (ft + 1), col:col + 512], a,
                       sST[rsl])
            if dst_lnloc is not None:
                xb = tb0[:, 1024:1536]
                em.wait("scalar", sPE, prev_stat2)
                em.wait("scalar", sDVE, mv)
                em.actv(xb, a, AF.Copy, sem=sACT)
                av2 = sACT.n
                em.wait("tensor", sACT, av2)
                em.mm(PA[0:1, 0:512], onesc_b[:, :], xb, start=(ft == 0),
                      stop=(ft == DMT - 1), sem=sPE)
                em.wait("scalar", sPE, sPE.n)
                em.actv(xb, a, AF.Square, sem=sACT)
                av3 = sACT.n
                em.wait("tensor", sACT, av3)
                prev_stat2 = em.mm(PA[0:1, 512:1024], onesc_b[:, :], xb,
                                   start=(ft == 0), stop=(ft == DMT - 1), sem=sPE)
        if dst_lnloc is None:
            return
        pe_done = sPE.n
        em.wait("vector", sPE, pe_done)
        em.vec("tensor_scalar", rows[0:1, 0:512], PA[0:1, 0:512], 1.0 / DM, None,
               ALU.mult)
        em.vec("tensor_scalar", rows[0:1, 512:1024], PA[0:1, 512:1024], 1.0 / DM,
               None, ALU.mult)
        em.vec("tensor_mul", rows[0:1, 1024:1536], rows[0:1, 0:512],
               rows[0:1, 0:512])
        em.vec("tensor_sub", rows[0:1, 512:1024], rows[0:1, 512:1024],
               rows[0:1, 1024:1536], sem=sDVE)
        vv = sDVE.n
        em.wait("scalar", sDVE, vv)
        em.actv(rows[0:1, 1024:1536], rows[0:1, 512:1024], AF.Ln,
                bias=eps_c[0:1, :])
        em.actv(rows[0:1, 512:1024], rows[0:1, 1024:1536], AF.Exp, scale=-0.5,
                sem=sACT)
        av = sACT.n
        em.wait("vector", sACT, av)
        em.vec("tensor_mul", rows[0:1, 1024:1536], rows[0:1, 0:512],
               rows[0:1, 512:1024], sem=sDVE)
        mrv = sDVE.n
        em.wait("tensor", sDVE, mrv)
        em.mm(PB[:, 0:512], ones1[:, :], rows[0:1, 512:1024], start=True, stop=True)
        em.mm(PB[:, 512:1024], ones1[:, :], rows[0:1, 1024:1536], start=True,
              stop=True, sem=sPE)
        bcv = sPE.n
        for i in range(3):
            em.waitf("sync", sST[i])
        em.waitf("vector", sLNS)
        em.wait("sync", sDVE, sDVE.n)
        em.wait("sync", sACT, sACT.n)
        slot_dve2 = {}
        for ft in range(DMT):
            rsl = ft % 3
            a = t0[:, rsl * 512:(rsl + 1) * 512]
            if ft >= 3:
                em.wait("sync", sDVE, slot_dve2.get(ft - 3, 0))
            em.dma("sync", a, dst_x[128 * ft:128 * (ft + 1), col:col + 512],
                   sTL[rsl])
            em.waitf("vector", sTL[rsl])
            em.wait("vector", sPE, bcv)
            b = t1[:, rsl * 512:(rsl + 1) * 512]
            em.vec("tensor_mul", b, a, PB[:, 0:512])
            ln_t = x_sb[:, ft * 512:(ft + 1) * 512]
            em.vec("tensor_sub", ln_t, b, PB[:, 512:1024], sem=sDVE)
            slot_dve2[ft] = sDVE.n
            em.wait("gpsimd", sDVE, sDVE.n)
            em.dma("gpsimd", dst_lnloc[128 * ft:128 * (ft + 1), 0:512],
                   ln_t, sLNS)

    em.wait("tensor", sCC, agd_d["agu"])
    em.wait("sync", sCC, agd_d["agd"])
    ag2_half = {}
    for cc in range(NCH):
        rsw = cfg_a["rs_out"][cc]
        em.wait("sync", sCC, rsw)
        em.wait("vector", sCC, rsw)
        emit_merge_ln_chunk(cc, x0T.ap(),
                            red.ap()[cc * DM:(cc + 1) * DM, :], xA.ap(), None,
                            None)
        em.wait("gpsimd", sCC, ag_half[1])
        emit_ffn(cc, xA.ap(), xB.ap(),
                 ln_loc.ap()[cc * DM:(cc + 1) * DM, :], is_last=False)
        em.waitf("gpsimd", sLNS)
        ag2_half[cc] = mk_ag_ln(cc)
    ag2a, ag2b = ag2_half[0], ag2_half[1]

    if stop_after == "ffn1":
        _finish(em, tap_pairs, sOUT, sCC)
        em.replay_into(nc)
        return nc

    cfg_b = dict(qT=qbD.ap(), kT=kbD.ap(), vT=vbD.ap(), per_head_kv=True,
                 rope=False, biased=True, wq=wbqT.ap(), wk=wbkT.ap(),
                 wv=wbvT.ap(), wo=wboT.ap(), kw=512,
                 wait_cc_half=(ag2a, ag2b),
                 wreg=R1, xbuf=x_sb, statps=PC,
                 interleave=None, mk_rs=mk_rs, rs_out=[])
    emit_attn(cfg_b)
    rs2 = cfg_b["rs_out"][1]

    for cc in range(NCH):
        rsw = cfg_b["rs_out"][cc]
        em.wait("sync", sCC, rsw)
        em.wait("vector", sCC, rsw)
        emit_merge_ln_chunk(cc, xB.ap(),
                            red.ap()[cc * DM:(cc + 1) * DM, :], xA.ap(), None,
                            bo_s)
        emit_ffn(cc, xA.ap(), None, None, is_last=True)

    _finish(em, tap_pairs, sOUT, sCC)
    em.replay_into(nc)
    return nc


def _finish(em, tap_pairs, sOUT, sCC):
    if sCC.n:
        em.wait("gpsimd", sCC, sCC.n)
    for tout, tsrc in tap_pairs:
        em.dma("gpsimd", tout.ap(), tsrc.ap(), sOUT)
    em.waitf("gpsimd", sOUT)
    em.waitf("sync", sOUT)


def kernel(**inputs):
    in_maps = _host_prep(inputs)
    if "nc" not in _NC_CACHE:
        _NC_CACHE["nc"] = build_nc()
    nc = _NC_CACHE["nc"]
    res = run_bass_kernel_spmd(nc, in_maps, core_ids=list(range(NC)))
    outs = [res.results[c]["out"] for c in range(NC)]
    full = np.concatenate([o.T for o in outs], axis=0)
    return full[None, :, :].astype(np.float32, copy=False)


# version 10
# speedup vs baseline: 1.1130x; 1.0039x over previous
import numpy as np
import ml_dtypes
from contextlib import ExitStack

import concourse.bass as bass
import concourse.mybir as mybir
from concourse.bass_utils import run_bass_kernel_spmd

BF16 = mybir.dt.bfloat16
F32 = mybir.dt.float32
U8 = mybir.dt.uint8
I32 = mybir.dt.int32
AF = mybir.ActivationFunctionType
ALU = mybir.AluOpType
AXX = mybir.AxisListType.X
NBF = ml_dtypes.bfloat16

NC = 8
S = 8192
DM = 4096
LS = S // NC
NCH = LS // 512
INTER_P = 11264
NTI = INTER_P // 128
NTIL = NTI // NC
SHN = INTER_P // NC
GG = 32
GD = 86
SCALE = 1.0 / float(np.sqrt(128.0))
EPS = 1e-5
NMAC = S // 512
DMT = DM // 128

GBLK = DM + 128 + 88 + 40
UBLK = DM + 128
DBLK = SHN + 128

ENGS = ("sync", "gpsimd", "vector", "scalar", "tensor")

_NC_CACHE = {}


def _bytes_view(qw):
    kk8, n = qw.shape
    b = qw.view('<u4').view(np.uint8).reshape(kk8, n, 4).transpose(0, 2, 1)
    return np.ascontiguousarray(b.reshape(kk8 * 4, n))


def _host_prep(inp):
    x = np.asarray(inp["L0_attn_a_x"])[0]
    xT = np.ascontiguousarray(x.T)

    w_inT = np.ascontiguousarray(np.asarray(inp["L0_attn_a_w_in"]).T)
    w_outT = np.ascontiguousarray(np.asarray(inp["L0_attn_a_w_out"]).T)
    WqT = np.ascontiguousarray(np.asarray(inp["L0_attn_b_Wq"]).T)
    WkT = np.ascontiguousarray(np.asarray(inp["L0_attn_b_Wk"]).T)
    WvT = np.ascontiguousarray(np.asarray(inp["L0_attn_b_Wv"]).T)
    WoT = np.ascontiguousarray(np.asarray(inp["L0_attn_b_Wo"]).T)
    bq = np.asarray(inp["L0_attn_b_bq"]); bk = np.asarray(inp["L0_attn_b_bk"])
    bv = np.asarray(inp["L0_attn_b_bv"]); bo = np.asarray(inp["L0_attn_b_bo"])

    qb_g = _bytes_view(np.asarray(inp["L0_ffn_gate_qweight"]))
    qb_u = _bytes_view(np.asarray(inp["L0_ffn_up_qweight"]))
    qb_d = _bytes_view(np.asarray(inp["L0_ffn_down_qweight"]))
    sc_g = np.asarray(inp["L0_ffn_gate_scales"]).astype(np.float32, copy=False)
    sc_u = np.asarray(inp["L0_ffn_up_scales"]).astype(np.float32, copy=False)
    sc_d = np.asarray(inp["L0_ffn_down_scales"]).astype(np.float32, copy=False)
    qz_g = np.asarray(inp["L0_ffn_gate_qzeros"]).astype(np.int32, copy=False)
    qz_u = np.asarray(inp["L0_ffn_up_qzeros"]).astype(np.int32, copy=False)
    qz_d = np.asarray(inp["L0_ffn_down_qzeros"]).astype(np.int32, copy=False)
    gi_g = np.asarray(inp["L0_ffn_gate_g_idx"]).astype(np.int32)
    gi_u = np.asarray(inp["L0_ffn_up_g_idx"]).astype(np.int32)
    gi_d = np.asarray(inp["L0_ffn_down_g_idx"]).astype(np.int32)

    gi_d_p = np.full(INTER_P, -1, np.int32)
    for r in range(NC):
        gi_d_p[r * SHN: r * SHN + 1376] = gi_d[r * 1376:(r + 1) * 1376]

    inv_freq = 1.0 / (10000.0 ** (np.arange(0, 128, 2, dtype=np.float32) / 128.0))
    ang = np.arange(S, dtype=np.float32)[:, None] * inv_freq[None, :]
    cos = np.cos(ang).astype(np.float32).T
    sin = np.sin(ang).astype(np.float32).T
    cosR = np.ascontiguousarray(np.concatenate([cos, cos], axis=0))
    sinS = np.ascontiguousarray(np.concatenate([-sin, sin], axis=0))

    tri = np.ascontiguousarray(
        (np.arange(512)[:, None] <= np.arange(512)[None, :]).astype(NBF))
    iota128x88 = np.ascontiguousarray(
        np.broadcast_to(np.arange(88, dtype=np.float32), (128, 88)))
    iota88c = np.ascontiguousarray(np.arange(88, dtype=np.float32)[:, None])
    ident = np.ascontiguousarray(np.eye(128, dtype=NBF))

    def col4(v):
        return np.ascontiguousarray(v.reshape(4, 128).T)

    in_maps = []
    for c in range(NC):
        qr = slice(512 * c, 512 * (c + 1))
        m = {
            "x0T": np.ascontiguousarray(xT[:, LS * c: LS * (c + 1)]),
            "waqT": np.ascontiguousarray(w_inT[:, qr]),
            "wakT": np.ascontiguousarray(w_inT[:, 4096 + 128 * c: 4096 + 128 * (c + 1)]),
            "wavT": np.ascontiguousarray(w_inT[:, 5120 + 128 * c: 5120 + 128 * (c + 1)]),
            "waoT": np.ascontiguousarray(w_outT[qr, :]),
            "wbqT": np.ascontiguousarray(WqT[:, qr]),
            "wbkT": np.ascontiguousarray(WkT[:, qr]),
            "wbvT": np.ascontiguousarray(WvT[:, qr]),
            "wboT": np.ascontiguousarray(WoT[qr, :]),
            "bq4": col4(bq[qr]), "bk4": col4(bk[qr]), "bv4": col4(bv[qr]),
            "bo32": np.ascontiguousarray(bo.reshape(DMT, 128).T),
            "qbg": np.ascontiguousarray(
                np.pad(qb_g[:, 1376 * c:1376 * (c + 1)], ((0, 0), (0, 32)))),
            "qbu": np.ascontiguousarray(
                np.pad(qb_u[:, 1376 * c:1376 * (c + 1)], ((0, 0), (0, 32)))),
            "qbd": np.ascontiguousarray(
                np.pad(qb_d[688 * c:688 * (c + 1), :], ((0, 16), (0, 0)))),
            "scg_sh": np.ascontiguousarray(
                np.pad(sc_g[:, 1376 * c:1376 * (c + 1)], ((0, 0), (0, 32)))),
            "scu_sh": np.ascontiguousarray(
                np.pad(sc_u[:, 1376 * c:1376 * (c + 1)], ((0, 0), (0, 32)))),
            "scd": sc_d,
            "qzg_sh": np.ascontiguousarray(
                np.pad(qz_g[:, 172 * c:172 * (c + 1)], ((0, 0), (0, 4)))),
            "qzu_sh": np.ascontiguousarray(
                np.pad(qz_u[:, 172 * c:172 * (c + 1)], ((0, 0), (0, 4)))),
            "qzd": qz_d,
            "gig": np.ascontiguousarray(gi_g[:, None]),
            "giu": np.ascontiguousarray(gi_u[:, None]),
            "gidp": np.ascontiguousarray(gi_d_p[:, None]),
            "gidsh": np.ascontiguousarray(gi_d_p[SHN * c: SHN * (c + 1)])[:, None],
            "cosR": cosR, "sinS": sinS, "tri": tri,
            "iota88": iota128x88, "iota88c": iota88c,
            "ident": ident,
        }
        in_maps.append(m)
    return in_maps


class Sem:
    __slots__ = ("h", "n")

    def __init__(self, handle):
        self.h = handle
        self.n = 0

    def bump(self, k):
        self.n += k
        return self.n


class Em:
    def __init__(self, nc):
        self.nc = nc
        self.q = {e: [] for e in ENGS}
        self.ctx = ExitStack()
        self.auto = {}

    def sem(self, name):
        return Sem(self.ctx.enter_context(self.nc.semaphore(name)))

    def sb(self, name, shape, dtype):
        return self.ctx.enter_context(self.nc.sbuf_tensor(name, shape, dtype))

    def psum(self, name, shape, dtype=F32):
        return self.ctx.enter_context(self.nc.psum_tensor(name, shape, dtype))

    def emit(self, eng, fn):
        self.q[eng].append(fn)

    def dma(self, eng, out, in_, sem, **kw):
        v = sem.bump(16)
        self.emit(eng, lambda e, o=out, i=in_, s=sem.h, kw=kw:
                  e.dma_start(out=o, in_=i, **kw).then_inc(s, 16))
        return v

    def wait(self, eng, sem, val):
        assert val <= sem.n
        if val > 0:
            self.emit(eng, lambda e, s=sem.h, vv=val: e.wait_ge(s, vv))

    def waitf(self, eng, sem):
        v = sem.n
        if v > 0:
            self.emit(eng, lambda e, s=sem.h, vv=v: e.wait_ge(s, vv))

    def mm(self, out, lhsT, rhs, start, stop, sem=None, tile_position=None):
        v = sem.bump(1) if sem else None
        import inspect
        fr = inspect.currentframe().f_back
        tag = f"mm@{fr.f_lineno}"
        def f(t, o=out, l=lhsT, r=rhs, st=start, sp=stop, s=sem, tp=tile_position,
              tag=tag):
            i = t.matmul(o, l, r, start=st, stop=sp, tile_position=tp)
            i.annotate(tag)
            if s is not None:
                i.then_inc(s.h, 1)
        self.emit("tensor", f)
        return v

    def trans(self, out, in_, ident, sem=None):
        v = sem.bump(1) if sem else None
        def f(t, o=out, i=in_, d=ident, s=sem):
            ins = t.transpose(o, i, d)
            if s is not None:
                ins.then_inc(s.h, 1)
        self.emit("tensor", f)
        return v

    def actv(self, out, in_, func, sem=None, bias=0.0, scale=1.0):
        s = self.auto["scalar"]
        pre = s.n
        v = s.bump(1)
        def f(a, o=out, i=in_, fu=func, b=bias, sc=scale, pre=pre, s=s):
            if pre > 0:
                a.wait_ge(s.h, pre)
            a.activation(o, i, fu, bias=b, scale=sc).then_inc(s.h, 1)
        self.emit("scalar", f)
        return v

    def vec(self, fn_name, *args, sem=None, **kw):
        s = self.auto["vector"]
        pre = s.n
        v = s.bump(1)
        def f(e, a=args, kw=kw, fn=fn_name, pre=pre, s=s):
            if pre > 0:
                e.wait_ge(s.h, pre)
            getattr(e, fn)(*a, **kw).then_inc(s.h, 1)
        self.emit("vector", f)
        return v

    def replay_into(self, nc):
        with nc.Block() as block:
            for name in ENGS:
                ops = self.q[name]
                def runner(e, ops=ops):
                    for op in ops:
                        op(e)
                getattr(block, name)(runner)


def build_nc(taps=(), stop_after=None):
    nc = bass.Bass("TRN2", target_bir_lowering=False, debug=False)

    def din(name, shape, dtype):
        return nc.dram_tensor(name, shape, dtype, kind="ExternalInput")

    x0T = din("x0T", [DM, LS], F32)
    waqT = din("waqT", [DM, 512], F32); wakT = din("wakT", [DM, 128], F32)
    wavT = din("wavT", [DM, 128], F32); waoT = din("waoT", [512, DM], F32)
    wbqT = din("wbqT", [DM, 512], F32); wbkT = din("wbkT", [DM, 512], F32)
    wbvT = din("wbvT", [DM, 512], F32); wboT = din("wboT", [512, DM], F32)
    bq4 = din("bq4", [128, 4], F32); bk4 = din("bk4", [128, 4], F32)
    bv4 = din("bv4", [128, 4], F32); bo32 = din("bo32", [128, DMT], F32)
    qbg = din("qbg", [2048, SHN], U8); qbu = din("qbu", [2048, SHN], U8)
    qbd = din("qbd", [704, DM], U8)
    scg_sh = din("scg_sh", [GG, SHN], F32); scu_sh = din("scu_sh", [GG, SHN], F32)
    scd = din("scd", [GD, DM], F32)
    qzg_sh = din("qzg_sh", [GG, SHN // 8], I32)
    qzu_sh = din("qzu_sh", [GG, SHN // 8], I32)
    qzd = din("qzd", [GD, 512], I32)
    gig = din("gig", [DM, 1], I32); giu = din("giu", [DM, 1], I32)
    gidp = din("gidp", [INTER_P, 1], I32); gidsh = din("gidsh", [SHN, 1], I32)
    cosR = din("cosR", [128, S], F32); sinS = din("sinS", [128, S], F32)
    tri = din("tri", [512, 512], BF16)
    iota88 = din("iota88", [128, 88], F32); iota88c = din("iota88c", [88, 1], F32)
    ident = din("ident", [128, 128], BF16)

    out_ext = nc.dram_tensor("out", [DM, LS], F32, kind="ExternalOutput")

    ln_loc = nc.dram_tensor("ln_loc", [2 * DM, 512], BF16)
    ln_full = nc.dram_tensor("ln_full", [2 * NC * DM, 512], BF16)
    partial = nc.dram_tensor("partial", [2 * NC * DM, 512], F32)
    red = nc.dram_tensor("red", [2 * DM, 512], F32)
    xA = nc.dram_tensor("xA", [DM, LS], F32)
    xB = nc.dram_tensor("xB", [DM, LS], F32)
    qaD = nc.dram_tensor("qaD", [512, S], BF16)
    kaD = nc.dram_tensor("kaD", [128, S], BF16)
    vaD = nc.dram_tensor("vaD", [S, 128], BF16)
    qbD = nc.dram_tensor("qbD", [512, S], BF16)
    kbD = nc.dram_tensor("kbD", [512, S], BF16)
    vbD = nc.dram_tensor("vbD", [S, 512], BF16)
    wsg_sh = nc.dram_tensor("wsg_sh", [128, NTIL * GBLK], BF16)
    wsu_sh = nc.dram_tensor("wsu_sh", [128, NTIL * UBLK], BF16)
    wsd_sh = nc.dram_tensor("wsd_sh", [128, DMT * DBLK], BF16)
    wsg = nc.dram_tensor("wsg", [NC * 128, NTIL * GBLK], BF16)
    wsu = nc.dram_tensor("wsu", [NC * 128, NTIL * UBLK], BF16)
    wsd = nc.dram_tensor("wsd", [NC * 128, DMT * DBLK], BF16)

    internal = {"ln_loc": ln_loc, "ln_full": ln_full, "partial": partial,
                "red": red, "xA": xA, "xB": xB, "qaD": qaD, "kaD": kaD,
                "vaD": vaD, "qbD": qbD, "kbD": kbD, "vbD": vbD,
                "wsg_sh": wsg_sh, "wsu_sh": wsu_sh, "wsd_sh": wsd_sh,
                "wsg": wsg, "wsu": wsu, "wsd": wsd}
    tap_pairs = []
    for tname in taps:
        t = internal[tname]
        tap_pairs.append((nc.dram_tensor("tap_" + tname, list(t.ap().shape),
                                         t.ap().dtype, kind="ExternalOutput"), t))

    em = Em(nc)

    R1 = em.sb("R1", [128, 45056], BF16)
    R2 = em.sb("R2", [128, 24576], BF16)
    R3 = em.sb("R3", [128, 17152], BF16)

    h_sb = R1[:, 0:45056]
    x_sb = R2[:, 0:16384]
    wo_sb = R2[:, 0:16384]

    dq_u8 = R1[:, 0:2048].bitcast(U8)
    dq_f = R1[:, 2048:10240].bitcast(F32)
    dq_b = R1[:, 10240:14336]
    zz_i = R1[:, 14336:23552].bitcast(I32)
    zz_f = R1[:, 23552:31744].bitcast(F32)
    zz_n = R1[:, 31744:39936].bitcast(F32)
    sc_bf = R1[:, 39936:44032]

    kv_k = R3[:, 0:4096]
    kv_v = R3[:, 4096:8192]
    q_mac = R3[:, 8192:10240]
    p_sb = R3[:, 10240:14336]
    att_sb = R3[:, 14336:16384]

    ones1 = em.sb("ones1", [1, 128], F32)
    onesc_b = em.sb("onesc_b", [128, 1], BF16)
    ident_s = em.sb("ident_s", [128, 128], BF16)
    tri_s = em.sb("tri_s", [128, 2048], BF16)
    iota_s = em.sb("iota_s", [128, 88], F32)
    iotac_s = em.sb("iotac_s", [88, 1], F32)
    ohxg_s = em.sb("ohxg_s", [128, DMT * GG], BF16)
    ohxu_s = em.sb("ohxu_s", [128, DMT * GG], BF16)
    bias_all = em.sb("bias_all", [128, 64], F32)
    bq_s = em.sb("bq_s", [128, 4], F32)
    bk_s = em.sb("bk_s", [128, 4], F32)
    bv_s = em.sb("bv_s", [128, 4], F32)
    bo_s = em.sb("bo_s", [128, DMT], F32)
    xg_s = em.sb("xg_s", [GG, 1024], BF16)
    hg_s = em.sb("hg_s", [88, 512], BF16)
    dq_gi = em.sb("dq_gi", [88, 256], I32)
    dq_gf = em.sb("dq_gf", [88, 256], F32)
    dq_ohT = em.sb("dq_ohT", [88, 256], BF16)
    gcol_i = em.sb("gcol_i", [128, 1], I32)
    gcol_f = em.sb("gcol_f", [128, 1], F32)
    oh1 = em.sb("oh1", [128, 88], BF16)

    t0 = em.sb("t0", [128, 1536], F32)
    t1 = em.sb("t1", [128, 1536], F32)
    tb0 = em.sb("tb0", [128, 1536], BF16)
    vtr = em.sb("vtr", [128, 512], BF16)
    rope_c = t0[:, 1024:1536]
    rope_s = t1[:, 1024:1536]
    den_s = t1[:, 0:512]
    rows = em.sb("rows", [1, 1536], F32)
    qn_s = em.sb("qn_s", [1, 64], F32)
    eps_c = em.sb("eps_c", [1, 1], F32)
    kn_s = em.sb("kn_s", [1, 80], F32)
    cneg_s = em.sb("cneg_s", [1, 64], F32)

    PA = em.psum("PA", [128, 1024])
    PB = em.psum("PB", [128, 1024])
    PB2 = em.psum("PB2", [128, 1024])
    PC = em.psum("PC", [128, 512])
    PT = em.psum("PT", [128, 512], BF16)

    sPE = em.sem("sPE"); sACT = em.sem("sACT"); sDVE = em.sem("sDVE")
    em.auto = {"vector": sDVE, "scalar": sACT}
    sCC = em.sem("sCC")
    sCN = em.sem("sCN")
    sTL = [em.sem(f"sTL{i}") for i in range(3)]
    sST = [em.sem(f"sST{i}") for i in range(3)]
    sX = em.sem("sX"); sQ = em.sem("sQ")
    sKV = [em.sem("sKV0"), em.sem("sKV1"), em.sem("sKV2"), em.sem("sKV3")]
    sW = [em.sem("sWg0"), em.sem("sWg1")]
    sD = [em.sem("sWd0"), em.sem("sWd1")]
    sPRS = [em.sem("sPRS0"), em.sem("sPRS1")]
    sDQ = em.sem("sDQ"); sDQ2 = em.sem("sDQ2"); sSC = em.sem("sSC"); sAW = em.sem("sAW")
    sWO = em.sem("sWO"); sSW = em.sem("sSW"); sSW2 = em.sem("sSW2"); sVT = em.sem("sVT")
    sQKV = em.sem("sQKV"); sWS = em.sem("sWS"); sWS2 = em.sem("sWS2"); sLNS = em.sem("sLNS")
    sOUT = em.sem("sOUT")

    rg = [list(range(NC))]

    def collective(kind, op, ins, outs):
        v = sCC.bump(1)
        def f(g, k=kind, o=op, i=ins, u=outs):
            g.collective_compute(k, o, replica_groups=rg, ins=i, outs=u).then_inc(sCC.h, 1)
        em.emit("gpsimd", f)
        return v

    em.dma("sync", ident_s[:, :], ident.ap(), sCN)
    for l in range(4):
        em.dma("sync", tri_s[:, l * 512:(l + 1) * 512],
               tri.ap()[128 * l:128 * (l + 1), :], sCN)
    em.dma("sync", iota_s[:, :], iota88.ap(), sCN)
    em.dma("sync", iotac_s[:, :], iota88c.ap(), sCN)
    em.dma("sync", bq_s[:, :], bq4.ap(), sCN)
    em.dma("sync", bk_s[:, :], bk4.ap(), sCN)
    em.dma("sync", bv_s[:, :], bv4.ap(), sCN)
    em.dma("sync", bo_s[:, :], bo32.ap(), sCN)
    for e in ("vector", "scalar", "tensor", "gpsimd"):
        em.waitf(e, sCN)
    em.vec("memset", ones1[:, :], 1.0)
    em.vec("memset", eps_c[:, :], EPS)
    em.vec("memset", tb0[:, 0:1024], 0.0)
    em.vec("memset", onesc_b[:, :], 1.0, sem=sDVE)
    consts_v = sDVE.n
    em.wait("tensor", sDVE, consts_v)
    em.wait("scalar", sDVE, consts_v)
    em.wait("gpsimd", sDVE, consts_v)
    for shd, w in ((wsg_sh, NTIL * GBLK), (wsu_sh, NTIL * UBLK),
                   (wsd_sh, DMT * DBLK)):
        w1 = (w // 1024) * 1024
        em.dma("gpsimd", shd.ap()[:, 0:w1].rearrange("p (r c) -> p r c", c=1024),
               tb0[:, 0:1024].unsqueeze(1).broadcast_to([128, w1 // 1024, 1024]),
               sWS)
        if w1 < w:
            em.dma("gpsimd", shd.ap()[:, w1:w], tb0[:, 0:w - w1], sWS)
    em.waitf("gpsimd", sWS)
    em.waitf("scalar", sWS)
    em.waitf("vector", sWS)

    def emit_merge_ln_chunk(cc, src_x, src_red, dst_x, dst_lnloc, bias_cols):
        col = cc * 512
        em.waitf("vector", sLNS)
        em.wait("vector", sPE, sPE.n)
        em.wait("sync", sDVE, sDVE.n)
        em.wait("sync", sACT, sACT.n)
        em.waitf("sync", sOUT)
        em.waitf("sync", sPRS[0])
        em.waitf("sync", sPRS[1])
        slot_act = {}
        slot_dve = {}
        prev_stat = 0
        for ft in range(DMT):
            r = ft % 3
            a = t0[:, r * 512:(r + 1) * 512]
            if ft >= 3:
                em.waitf("sync", sST[r])
                em.wait("sync", sACT, slot_act.get(ft - 3, 0))
                em.wait("sync", sDVE, slot_dve.get(ft - 3, 0))
            em.dma("sync", a, src_x[128 * ft:128 * (ft + 1), col:col + 512], sTL[r])
            if src_red is not None:
                b = t1[:, r * 512:(r + 1) * 512]
                em.dma("sync", b, src_red[128 * ft:128 * (ft + 1), 0:512],
                       sTL[r])
                em.waitf("vector", sTL[r])
                if bias_cols is not None:
                    em.vec("tensor_scalar", b, b, bias_cols[:, ft:ft + 1], None,
                           ALU.add)
                em.vec("tensor_add", a, a, b, sem=sDVE)
                mv = sDVE.n
                slot_dve[ft] = mv
                em.wait("gpsimd", sDVE, mv)
                em.dma("gpsimd", dst_x[128 * ft:128 * (ft + 1), col:col + 512], a,
                       sST[r])
                em.wait("scalar", sDVE, mv)
            else:
                em.waitf("scalar", sTL[r])
            xb = tb0[:, 0:512]
            sq = tb0[:, 512:1024]
            em.wait("scalar", sPE, prev_stat)
            em.actv(xb, a, AF.Copy)
            em.actv(sq, a, AF.Square, sem=sACT)
            av = sACT.n
            slot_act[ft] = av
            em.wait("tensor", sACT, av)
            em.mm(PA[0:1, 0:512], onesc_b[:, :], xb, start=(ft == 0),
                  stop=(ft == DMT - 1))
            prev_stat = em.mm(PA[0:1, 512:1024], onesc_b[:, :], sq, start=(ft == 0),
                              stop=(ft == DMT - 1), sem=sPE)
        pe_done = sPE.n
        em.wait("vector", sPE, pe_done)
        em.vec("tensor_scalar", rows[0:1, 0:512], PA[0:1, 0:512], 1.0 / DM, None,
               ALU.mult)
        em.vec("tensor_scalar", rows[0:1, 512:1024], PA[0:1, 512:1024], 1.0 / DM,
               None, ALU.mult)
        em.vec("tensor_mul", rows[0:1, 1024:1536], rows[0:1, 0:512],
               rows[0:1, 0:512])
        em.vec("tensor_sub", rows[0:1, 512:1024], rows[0:1, 512:1024],
               rows[0:1, 1024:1536], sem=sDVE)
        vv = sDVE.n
        em.wait("scalar", sDVE, vv)
        em.actv(rows[0:1, 1024:1536], rows[0:1, 512:1024], AF.Ln,
                bias=eps_c[0:1, :])
        em.actv(rows[0:1, 512:1024], rows[0:1, 1024:1536], AF.Exp, scale=-0.5,
                sem=sACT)
        av = sACT.n
        em.wait("vector", sACT, av)
        em.vec("tensor_mul", rows[0:1, 1024:1536], rows[0:1, 0:512],
               rows[0:1, 512:1024], sem=sDVE)
        mrv = sDVE.n
        em.wait("tensor", sDVE, mrv)
        em.mm(PB[:, 0:512], ones1[:, :], rows[0:1, 512:1024], start=True, stop=True)
        em.mm(PB[:, 512:1024], ones1[:, :], rows[0:1, 1024:1536], start=True,
              stop=True, sem=sPE)
        bcv = sPE.n
        rdsrc = dst_x if (src_red is not None and dst_x is not None) else src_x
        for i in range(3):
            em.waitf("sync", sST[i])
        em.wait("sync", sDVE, sDVE.n)
        em.wait("sync", sACT, sACT.n)
        slot_dve2 = {}
        for ft in range(DMT):
            r = ft % 3
            a = t0[:, r * 512:(r + 1) * 512]
            if ft >= 3:
                em.wait("sync", sDVE, slot_dve2.get(ft - 3, 0))
            em.dma("sync", a, rdsrc[128 * ft:128 * (ft + 1), col:col + 512], sTL[r])
            em.waitf("vector", sTL[r])
            em.wait("vector", sPE, bcv)
            b = t1[:, r * 512:(r + 1) * 512]
            em.vec("tensor_mul", b, a, PB[:, 0:512])
            ln_t = x_sb[:, ft * 512:(ft + 1) * 512]
            em.vec("tensor_sub", ln_t, b, PB[:, 512:1024], sem=sDVE)
            lv = sDVE.n
            slot_dve2[ft] = lv
            if dst_lnloc is not None:
                em.wait("gpsimd", sDVE, lv)
                em.dma("gpsimd", dst_lnloc[128 * ft:128 * (ft + 1), 0:512],
                       ln_t, sLNS)

    ag_half = {}

    def mk_ag_ln(h):
        return collective(
            "AllGather", ALU.bypass,
            [ln_loc.ap()[h * DM:(h + 1) * DM, :].opt()],
            [ln_full.ap()[h * NC * DM:(h + 1) * NC * DM, :].opt()])

    for cc in range(NCH):
        emit_merge_ln_chunk(cc, x0T.ap(), None, None,
                            ln_loc.ap()[cc * DM:(cc + 1) * DM, :], None)
        em.waitf("gpsimd", sLNS)
        ag_half[cc] = mk_ag_ln(cc)
    ag1 = ag_half[1]

    if stop_after == "ln1":
        _finish(em, tap_pairs, sOUT, sCC)
        em.replay_into(nc)
        return nc

    def blocked(dst_dram, blk):
        return dst_dram.ap().rearrange("p (b c) -> p b c", c=blk)

    def emit_dequant_gen(qb_ap, sc_ap, qz_ap, gidx_ap, nkt, ncols, G, nblk, blk,
                         dst_dram, want_ohh, gidx_oh_ap, wofs):
        em.wait("sync", sDVE, sDVE.n)
        em.wait("gpsimd", sPE, sPE.n)
        em.dma("gpsimd", sc_bf[0:G, 0:ncols], sc_ap, sSC)
        em.waitf("tensor", sSC)
        dst = blocked(dst_dram, blk)
        dbuf = ncols <= 2048
        back = 2 if dbuf else 1
        unp0, mm0 = sDVE.n, sPE.n
        st0 = {0: sWS.n, 1: sWS2.n}
        unp_mark = {}
        mm_mark = {}
        store_mark = {}
        for kt in range(nkt):
            p = (kt % 2) if dbuf else 0
            u8v = dq_u8[:, p * 2048:p * 2048 + ncols]
            fv = dq_f[:, p * 2048:p * 2048 + ncols]
            bv = dq_b[:, p * 2048:p * 2048 + ncols]
            giv = dq_gi[0:G, p * 128:p * 128 + 128]
            gfv = dq_gf[0:G, p * 128:p * 128 + 128]
            ohv = dq_ohT[0:G, p * 128:p * 128 + 128]
            em.wait("sync", sDVE, unp_mark.get(kt - back, unp0))
            bsrc = qb_ap[64 * kt:64 * (kt + 1), :].unsqueeze(0) \
                .broadcast_to([2, 64, ncols])
            dqsem = sDQ2 if (dbuf and p == 1) else sDQ
            em.dma("sync", u8v, bsrc, dqsem)
            gsrc = gidx_ap[kt * 128:(kt + 1) * 128, 0:1] \
                .rearrange("k x -> x k").broadcast_to([G, 128])
            em.dma("sync", giv, gsrc, dqsem)
            em.waitf("vector", dqsem)
            wsem = sWS2 if (dbuf and p == 1) else sWS
            em.wait("vector", wsem, store_mark.get(kt - back, st0[p]))
            em.wait("vector", sPE, mm_mark.get(kt - back, mm0))
            em.vec("tensor_copy", gfv,
                   giv.rearrange("g (k2 o) -> g k2 o", o=2)
                   .transpose([0, 2, 1]))
            em.vec("tensor_scalar", ohv, gfv, iotac_s[0:G, :],
                   None, ALU.is_equal)
            em.vec("tensor_scalar", u8v[0:64, :], u8v[0:64, :],
                   15, None, ALU.bitwise_and)
            em.vec("tensor_scalar", u8v[64:128, :], u8v[64:128, :],
                   4, 15, ALU.logical_shift_right, ALU.bitwise_and)
            em.vec("tensor_copy", fv, u8v, sem=sDVE)
            dv = sDVE.n
            unp_mark[kt] = dv
            nchk = (ncols + 511) // 512
            for ch in range(nchk):
                a0 = ch * 512
                a1 = min(ncols, a0 + 512)
                em.wait("tensor", sDVE, dv if ch == 0 else sDVE.n)
                em.mm(PC[:, 0:a1 - a0], ohv, sc_bf[0:G, a0:a1],
                      start=True, stop=True, sem=sPE)
                em.wait("vector", sPE, sPE.n)
                em.vec("tensor_mul", bv[:, a0:a1], fv[:, a0:a1],
                       PC[:, 0:a1 - a0], sem=sDVE)
            mm_mark[kt] = sPE.n
            em.wait("gpsimd", sDVE, sDVE.n)
            dstp = dst_dram.ap().rearrange("(k2 o) q -> k2 o q", o=2) \
                .transpose([1, 0, 2]) \
                .rearrange("o k2 (b c) -> o k2 b c", c=blk)[
                    :, :, :, kt * 128:(kt + 1) * 128]
            cpb = ncols // nblk
            bstep2 = max(1, 1536 // 64)
            for o in range(2):
                for b0 in range(0, nblk, bstep2):
                    b1 = min(nblk, b0 + bstep2)
                    em.dma("gpsimd", dstp[o][:, b0:b1, :],
                           bv[64 * o:64 * (o + 1), b0 * cpb:b1 * cpb]
                           .rearrange("p (b c) -> p b c", b=b1 - b0), wsem)
            store_mark[kt] = wsem.n
            yield
        unpack_mark = unp_mark.get(nkt - 1, 0)
        nq = ncols // 8
        em.wait("sync", sDVE, unpack_mark)
        em.dma("sync", zz_i[0:G, 0:nq], qz_ap, sDQ)
        em.dma("sync", zz_n[0:G, 0:ncols], sc_ap, sDQ)
        em.waitf("vector", sDQ)
        em.waitf("vector", sWS)
        em.waitf("vector", sWS2)
        for sh in range(8):
            em.vec("tensor_scalar",
                   zz_i[0:G, 512:512 + ncols].rearrange("g (c e) -> g c e", e=8)[
                       :, :, sh:sh + 1],
                   zz_i[0:G, 0:nq], 4 * sh, 15,
                   ALU.logical_shift_right, ALU.bitwise_and)
        em.vec("tensor_copy", zz_f[0:G, 0:ncols], zz_i[0:G, 512:512 + ncols])
        em.vec("tensor_scalar", zz_n[0:G, 0:ncols], zz_n[0:G, 0:ncols], -1.0, None,
               ALU.mult)
        em.vec("scalar_tensor_tensor", dq_b[0:G, 0:ncols], zz_f[0:G, 0:ncols], 1.0,
               zz_n[0:G, 0:ncols], ALU.add, ALU.mult, sem=sDVE)
        em.wait("gpsimd", sDVE, sDVE.n)
        bstep = max(1, 1536 // G)
        for b0 in range(0, nblk, bstep):
            b1 = min(nblk, b0 + bstep)
            em.dma("gpsimd", dst[0:G, b0:b1, wofs:wofs + 128],
                   dq_b[0:G, b0 * (ncols // nblk):b1 * (ncols // nblk)]
                   .rearrange("g (b c) -> g b c", b=b1 - b0), sWS)
        yield
        if want_ohh:
            for bl in range(nblk):
                em.wait("sync", sDVE, sDVE.n)
                em.dma("sync", gcol_i[:, :],
                       gidx_oh_ap[bl * 128:(bl + 1) * 128, 0:1], sDQ)
                em.waitf("vector", sDQ)
                em.waitf("vector", sWS)
                em.waitf("vector", sWS2)
                em.vec("tensor_copy", gcol_f[:, :], gcol_i[:, :])
                em.vec("tensor_scalar", oh1[:, :], iota_s[:, 0:88], gcol_f[:, :],
                       None, ALU.is_equal, sem=sDVE)
                em.wait("gpsimd", sDVE, sDVE.n)
                em.dma("gpsimd", dst[:, bl:bl + 1, wofs + 128:wofs + 216],
                       oh1[:, :].unsqueeze(1), sWS)
                yield

    agd_d = {}

    def emit_ohx_gen(gidx_ap, dst):
        for kt in range(DMT):
            em.wait("sync", sDVE, sDVE.n)
            em.dma("sync", gcol_i[:, :], gidx_ap[kt * 128:(kt + 1) * 128, 0:1], sDQ)
            em.waitf("vector", sDQ)
            em.vec("tensor_copy", gcol_f[:, :], gcol_i[:, :])
            em.vec("tensor_scalar", dst[:, kt * GG:(kt + 1) * GG], iota_s[:, 0:GG],
                   gcol_f[:, :], None, ALU.is_equal, sem=sDVE)
            yield

    def deq_master():
        yield from emit_dequant_gen(qbg.ap(), scg_sh.ap(), qzg_sh.ap(), gig.ap(),
                                    DMT, SHN, GG, NTIL, GBLK, wsg_sh, True,
                                    gidsh.ap(), DM)
        em.waitf("gpsimd", sWS)
        em.waitf("gpsimd", sWS2)
        agd_d["agg"] = collective("AllGather", ALU.bypass, [wsg_sh.ap().opt()],
                                  [wsg.ap().opt()])
        yield
        yield from emit_dequant_gen(qbu.ap(), scu_sh.ap(), qzu_sh.ap(), giu.ap(),
                                    DMT, SHN, GG, NTIL, UBLK, wsu_sh, False,
                                    None, DM)
        em.waitf("gpsimd", sWS)
        em.waitf("gpsimd", sWS2)
        agd_d["agu"] = collective("AllGather", ALU.bypass, [wsu_sh.ap().opt()],
                                  [wsu.ap().opt()])
        yield
        yield from emit_dequant_gen(qbd.ap(), scd.ap(), qzd.ap(), gidsh.ap(),
                                    NTIL, DM, GD, DMT, DBLK, wsd_sh, False,
                                    None, SHN)
        em.waitf("gpsimd", sWS)
        em.waitf("gpsimd", sWS2)
        agd_d["agd"] = collective("AllGather", ALU.bypass, [wsd_sh.ap().opt()],
                                  [wsd.ap().opt()])
        yield
        yield from emit_ohx_gen(gig.ap(), ohxg_s)
        yield from emit_ohx_gen(giu.ap(), ohxu_s)

    deq = deq_master()
    if stop_after == "dequant":
        for _ in deq:
            pass
        _finish(em, tap_pairs, sOUT, sCC)
        em.replay_into(nc)
        return nc

    def emit_attn(cfg):
        nkt = cfg["kw"] // 128
        WREG = cfg["wreg"]
        XBUF = cfg["xbuf"]
        STP = cfg["statps"]
        if cfg["per_head_kv"]:
            passes = [[("q", 0), ("q", 1), ("q", 2), ("q", 3), ("k", 0), ("k", 1),
                       ("k", 2), ("k", 3), ("v", 0), ("v", 1), ("v", 2), ("v", 3)]]
        else:
            passes = [[("q", 0), ("q", 1), ("q", 2), ("q", 3), ("k", 0), ("v", 0)]]

        def wslot(wi):
            if wi < 11:
                return WREG[:, wi * DM:(wi + 1) * DM]
            return R3[:, 0:DM]
        dram_of = {"q": cfg["qT"], "k": cfg["kT"], "v": cfg["vT"]}
        wsrc_of = {"q": cfg["wq"], "k": cfg["wk"], "v": cfg["wv"]}
        for pi, plist in enumerate(passes):
            em.wait("gpsimd", sPE, sPE.n)
            em.wait("gpsimd", sDVE, sDVE.n)
            em.waitf("gpsimd", sLNS)
            em.waitf("sync", sLNS)
            for wi, (nm, nt) in enumerate(plist):
                src = wsrc_of[nm][:, nt * 128:(nt + 1) * 128] \
                    .rearrange("(dt p) n -> p dt n", p=128)
                em.dma("gpsimd", wslot(wi)
                       .rearrange("p (dt n) -> p dt n", dt=DMT), src, sAW)
            em.waitf("tensor", sAW)
            last_evac = {}
            for sc_i in (list(range(0, NMAC, 2)) + list(range(1, NMAC, 2))):
                em.wait("sync", sCC, cfg["wait_cc_half"][sc_i % 2])
                em.wait("sync", sPE, sPE.n)
                em.wait("sync", sDVE, sDVE.n)
                base = (sc_i % 2) * NC * DM + (sc_i // 2) * DM
                xsrc = ln_full.ap()[base:base + DM, 0:512] \
                    .rearrange("(dt p) c -> p dt c", p=128)
                em.dma("sync", XBUF[:, :].rearrange("p (dt c) -> p dt c", dt=DMT),
                       xsrc, sX)
                if cfg["rope"]:
                    em.dma("sync", rope_c,
                           cosR.ap()[:, sc_i * 512:(sc_i + 1) * 512], sX)
                    em.dma("sync", rope_s,
                           sinS.ap()[:, sc_i * 512:(sc_i + 1) * 512], sX)
                em.waitf("tensor", sX)
                em.waitf("vector", sX)
                for wi, (nm, nt) in enumerate(plist):
                    half = wi % 2
                    acc = PA[:, half * 512:(half + 1) * 512]
                    prevev = last_evac.get(half)
                    if prevev is not None:
                        em.wait("tensor", prevev[0], prevev[1])
                    wsl = wslot(wi)
                    for dt_i in range(DMT):
                        em.mm(acc, wsl[:, dt_i * 128:dt_i * 128 + 128],
                              XBUF[:, dt_i * 512:(dt_i + 1) * 512],
                              start=(dt_i == 0), stop=(dt_i == DMT - 1),
                              sem=(sPE if dt_i == DMT - 1 else None))
                    mmv = sPE.n
                    em.wait("scalar", sPE, mmv)
                    if nm == "v":
                        vt = tb0[:, 0:512]
                        em.wait("vector", sPE, sPE.n)
                        if cfg["biased"]:
                            em.vec("tensor_scalar", vt, acc, bv_s[:, nt:nt + 1],
                                   None, ALU.add, sem=sDVE)
                        else:
                            em.vec("tensor_copy", vt, acc, sem=sDVE)
                        av = sDVE.n
                        last_evac[half] = (sDVE, av)
                        em.wait("tensor", sDVE, av)
                        for sub in range(4):
                            em.trans(PT[:, sub * 128:(sub + 1) * 128],
                                     vt[:, sub * 128:(sub + 1) * 128],
                                     ident_s[:, :],
                                     sem=(sPE if sub == 3 else None))
                        tv = sPE.n
                        em.wait("vector", sPE, tv)
                        em.waitf("vector", sVT)
                        em.vec("tensor_copy", vtr[:, :], PT[:, :], sem=sDVE)
                        em.wait("sync", sDVE, sDVE.n)
                        em.dma("sync",
                               dram_of["v"][sc_i * 512:(sc_i + 1) * 512,
                                            nt * 128:(nt + 1) * 128]
                               .rearrange("(sub s) d -> sub s d", sub=4)
                               .transpose([1, 0, 2]),
                               vtr[:, :].rearrange("p (sub d) -> p sub d", sub=4),
                               sVT)
                        continue
                    if cfg["rope"]:
                        a = t0[:, 0:512]
                        em.waitf("scalar", sSW)
                        em.wait("scalar", sDVE, sDVE.n)
                        em.actv(a, acc, AF.Copy, sem=sACT)
                        av = sACT.n
                        last_evac[half] = (sACT, av)
                        em.wait("sync", sACT, av)
                        em.wait("sync", sDVE, sDVE.n)
                        em.dma("sync", t1[0:64, 0:512], a[64:128, 0:512], sSW)
                        em.dma("sync", t1[64:128, 0:512], a[0:64, 0:512], sSW)
                        em.waitf("vector", sSW)
                        em.vec("tensor_mul", t0[:, 512:1024], a, rope_c)
                        em.vec("tensor_mul", t1[:, 512:1024], t1[:, 0:512], rope_s)
                        qk_t = tb0[:, 512:1024]
                        em.waitf("vector", sQKV)
                        em.vec("tensor_add", qk_t, t0[:, 512:1024],
                               t1[:, 512:1024], sem=sDVE)
                        dv = sDVE.n
                        em.wait("sync", sDVE, dv)
                        em.wait("scalar", sDVE, dv)
                    else:
                        bias_ap = (bq_s if nm == "q" else bk_s)[:, nt:nt + 1]
                        qk_t = tb0[:, 512:1024]
                        em.waitf("vector", sQKV)
                        em.wait("vector", sPE, sPE.n)
                        em.wait("vector", sACT, sACT.n)
                        em.vec("tensor_scalar", qk_t, acc, bias_ap, None, ALU.add,
                               sem=sDVE)
                        av = sDVE.n
                        last_evac[half] = (sDVE, av)
                        em.wait("sync", sDVE, av)
                    em.dma("sync", dram_of[nm][nt * 128:(nt + 1) * 128,
                                                 sc_i * 512:(sc_i + 1) * 512],
                           qk_t, sQKV)
                    em.wait("vector", sPE, sPE.n)
                    em.vec("tensor_mul", tb0[:, 1024:1536], qk_t, qk_t,
                           sem=sDVE)
                    em.wait("tensor", sDVE, sDVE.n)
                    em.mm(STP[0:1, 0:512], onesc_b[:, :], tb0[:, 1024:1536],
                          start=True, stop=True, sem=sPE)
                    em.wait("vector", sPE, sPE.n)
                    dst_row = qn_s if nm == "q" else kn_s
                    em.vec("tensor_reduce",
                           dst_row[0:1, nt * 16 + sc_i:nt * 16 + sc_i + 1],
                           STP[0:1, 0:512], AXX, ALU.max, sem=sDVE)
                    if wi == 2 and cfg.get("interleave"):
                        cfg["interleave"]()
                if cfg.get("interleave"):
                    cfg["interleave"]()

        em.wait("vector", sDVE, sDVE.n)
        for j in range(nkt):
            em.vec("tensor_reduce", kn_s[0:1, 64 + j:65 + j],
                   kn_s[0:1, j * 16:(j + 1) * 16], AXX, ALU.max)
        if nkt == 1:
            em.vec("tensor_mul", cneg_s[0:1, :], qn_s[0:1, :],
                   kn_s[0:1, 64:65].broadcast_to([1, 64]), sem=sDVE)
        else:
            em.vec("tensor_mul",
                   cneg_s[0:1, :].rearrange("o (h m) -> o h m", h=4),
                   qn_s[0:1, :].rearrange("o (h m) -> o h m", h=4),
                   kn_s[0:1, 64:68].unsqueeze(2).broadcast_to([1, 4, 16]),
                   sem=sDVE)
        dv = sDVE.n
        em.wait("scalar", sDVE, dv)
        em.actv(rows[0:1, 0:64], cneg_s[0:1, :], AF.Ln)
        em.actv(rows[0:1, 64:128], rows[0:1, 0:64], AF.Exp, scale=0.5, sem=sACT)
        av = sACT.n
        em.wait("vector", sACT, av)
        em.vec("tensor_scalar", cneg_s[0:1, :], rows[0:1, 64:128], -SCALE, None,
               ALU.mult, sem=sDVE)
        dv = sDVE.n
        em.wait("tensor", sDVE, dv)
        em.mm(PC[:, 0:64], ones1[:, :], cneg_s[0:1, :], start=True, stop=True,
              sem=sPE)
        em.wait("vector", sPE, sPE.n)
        em.vec("tensor_copy", bias_all[:, :], PC[:, 0:64], sem=sDVE)
        em.wait("tensor", sDVE, sDVE.n)

        em.wait("gpsimd", sPE, sPE.n)
        wo_src = cfg["wo"].rearrange("(et p) f -> p et f", p=128)
        em.dma("gpsimd", wo_sb[:, :].rearrange("p (et f) -> p et f", et=4),
               wo_src, sWO)

        ones33 = R3[:, 16384:16640].bitcast(F32)
        em.vec("memset", ones33[32:33, 0:128], 1.0, sem=sDVE)
        em.waitf("sync", sQKV)
        em.waitf("sync", sVT)
        exp_done = {0: 0, 1: 0}
        pslot_pv = [0, 0, 0, 0]
        bctr = 0
        pend = [None]
        kv_done = {}

        def flush_pend():
            if pend[0] is None:
                return
            (fh, fh2, fkc, fb, fslot, fmi, fuse_sem, fuse_val, fpar2) = pend[0]
            pend[0] = None
            em.wait("tensor", fuse_sem, fuse_val)
            pq = p_sb[:, fslot * 1024: fslot * 1024 + 1024]
            first = (fkc == 0)
            last = (fkc == fmi)
            for j in range(2):
                let = 2 * fb + j
                em.mm(PC[32 * fh2:32 * fh2 + 1, 0:512], onesc_b[:, :],
                      pq[:, j * 512:(j + 1) * 512],
                      start=(first and let == 0), stop=(last and let == 3),
                      tile_position=(0, 32 * fh2))
            v = None
            for j in range(2):
                let = 2 * fb + j
                if cfg["per_head_kv"]:
                    vl = kv_v[:, fpar2 * 2048 + let * 512 + fh * 128:
                              fpar2 * 2048 + let * 512 + fh * 128 + 128]
                else:
                    vl = kv_v[:, fpar2 * 512 + let * 128:
                              fpar2 * 512 + let * 128 + 128]
                v = em.mm(PA[:, fh2 * 512:(fh2 + 1) * 512], vl,
                          pq[:, j * 512:(j + 1) * 512],
                          start=(first and let == 0), stop=(last and let == 3),
                          sem=sPE)
            pslot_pv[fslot] = v
            if fh2 == 1 and fb == 1:
                kv_done[fkc] = v

        macro_order = list(range(0, NMAC, 2)) + list(range(1, NMAC, 2))
        for mseq, mi in enumerate(macro_order):
            em.wait("sync", sPE, sPE.n)
            em.dma("sync", q_mac[:, :].rearrange("p (h c) -> p h c", h=4),
                   cfg["qT"][0:512, mi * 512:(mi + 1) * 512]
                   .rearrange("(h p) c -> p h c", p=128), sQ)
            phk = cfg["per_head_kv"]
            nslot = 2 if phk else 4
            swid = 2048 if phk else 512
            pre = 1 if phk else 2
            for hp in range(2):
                kv_done.clear()
                kv_mark = {}
                em.wait("sync", sPE, sPE.n)

                def issue_kv(kc, mi=mi):
                    sl = kc % nslot
                    g = kv_done.get(kc - nslot)
                    if g:
                        em.wait("sync", sPE, g)
                    base = sl * swid
                    if phk:
                        for hf in range(2):
                            ks = cfg["kT"][hf * 256:(hf + 1) * 256,
                                           kc * 512:(kc + 1) * 512] \
                                .rearrange("(h p) c -> p h c", p=128)
                            em.dma("sync",
                                   kv_k[:, base + hf * 1024:base + hf * 1024 + 1024]
                                   .rearrange("p (h c) -> p h c", h=2), ks,
                                   sKV[sl])
                            vs = cfg["vT"][kc * 512 + hf * 256:
                                           kc * 512 + (hf + 1) * 256, 0:512] \
                                .rearrange("(l p) d -> p l d", p=128)
                            em.dma("sync",
                                   kv_v[:, base + hf * 1024:base + hf * 1024 + 1024]
                                   .rearrange("p (l d) -> p l d", l=2), vs,
                                   sKV[sl])
                    else:
                        ks = cfg["kT"][0:128, kc * 512:(kc + 1) * 512]
                        vs = cfg["vT"][kc * 512:(kc + 1) * 512, 0:128] \
                            .rearrange("(l p) d -> p l d", p=128)
                        em.dma("sync", kv_k[:, base:base + 512], ks, sKV[sl])
                        em.dma("sync", kv_v[:, base:base + 512]
                               .rearrange("p (l d) -> p l d", l=4), vs, sKV[sl])
                    kv_mark[kc] = sKV[sl].n

                for kp in range(min(pre, mi + 1)):
                    issue_kv(kp)
                for kc in range(mi + 1):
                    par2 = kc % nslot
                    em.wait("tensor", sKV[par2], kv_mark[kc])
                    nxt = kc + pre
                    if nxt <= mi and nxt not in kv_mark:
                        if phk:
                            flush_pend()
                        issue_kv(nxt)
                    em.waitf("tensor", sQ)
                    for h2 in range(2):
                        h = hp * 2 + h2
                        for b in range(2):
                            par = bctr % 2
                            slot = bctr % 4
                            bctr += 1
                            buf = PB if par == 0 else PB2
                            if exp_done[par]:
                                em.wait("tensor", sACT, exp_done[par])
                            qk = None
                            for j in range(2):
                                let = 2 * b + j
                                if cfg["per_head_kv"]:
                                    lt = kv_k[:, par2 * 2048 + h * 512 + let * 128:
                                              par2 * 2048 + h * 512 + let * 128 + 128]
                                else:
                                    lt = kv_k[:, par2 * 512 + let * 128:
                                              par2 * 512 + let * 128 + 128]
                                qk = em.mm(buf[:, j * 512:(j + 1) * 512], lt,
                                           q_mac[:, h * 512:(h + 1) * 512],
                                           start=True, stop=True, sem=sPE)
                            flush_pend()
                            em.wait("scalar", sPE, max(qk, pslot_pv[slot]))
                            pq = p_sb[:, slot * 1024: slot * 1024 + 1024]
                            em.actv(pq, buf[:, :], AF.Exp,
                                    bias=bias_all[:, h * 16 + mi:h * 16 + mi + 1],
                                    scale=SCALE, sem=sACT)
                            ev = sACT.n
                            exp_done[par] = ev
                            use_sem, use_val = sACT, ev
                            if kc == mi:
                                em.wait("vector", sACT, ev)
                                em.vec("tensor_mul", pq, pq,
                                       tri_s[:, b * 1024:(b + 1) * 1024], sem=sDVE)
                                use_sem, use_val = sDVE, sDVE.n
                            pend[0] = (h, h2, kc, b, slot, mi, use_sem, use_val,
                                       par2)
                flush_pend()
                pv_all = sPE.n
                em.wait("vector", sPE, pv_all)
                em.waitf("vector", sPRS[0])
                em.waitf("vector", sPRS[1])
                em.waitf("vector", sSW)
                em.vec("tensor_copy", den_s[0:1, :], PC[0:1, :])
                em.vec("tensor_copy", den_s[32:33, :], PC[32:33, :])
                em.vec("reciprocal", den_s[0:1, :], den_s[0:1, :])
                em.vec("reciprocal", den_s[32:33, :], den_s[32:33, :], sem=sDVE)
                rv = sDVE.n
                em.wait("tensor", sDVE, rv)
                for h2 in range(2):
                    h = hp * 2 + h2
                    dsrc = den_s[0:1, :] if h2 == 0 else den_s[32:33, :]
                    lsrc = ones1[:, :] if h2 == 0 else ones33[32:33, :]
                    em.wait("tensor", sDVE, sDVE.n)
                    em.mm(PB[:, 0:512], lsrc, dsrc,
                          start=True, stop=True, sem=sPE)
                    bv_ = sPE.n
                    em.wait("scalar", sPE, bv_)
                    em.wait("scalar", sDVE, sDVE.n)
                    em.actv(t0[:, 0:512], PA[:, h2 * 512:(h2 + 1) * 512], AF.Copy,
                            sem=sACT)
                    av = sACT.n
                    em.wait("vector", sACT, av)
                    em.wait("vector", sPE, bv_)
                    em.vec("tensor_mul", att_sb[:, h * 512:(h + 1) * 512],
                           t0[:, 0:512], PB[:, 0:512], sem=sDVE)
                em.wait("tensor", sDVE, sDVE.n)
                exp_done[0] = 0
                exp_done[1] = 0
                em.wait("tensor", sACT, sACT.n)
            em.waitf("tensor", sWO)
            op_evac = {}
            for ft in range(DMT):
                half = ft % 2
                bank = PB2[:, half * 512:half * 512 + 512]
                if ft >= 2:
                    em.wait("tensor", sDVE, op_evac[ft - 2])
                for et in range(4):
                    em.mm(bank, wo_sb[:, (et * DMT + ft) * 128:
                                      (et * DMT + ft) * 128 + 128],
                          att_sb[:, et * 512:(et + 1) * 512],
                          start=(et == 0), stop=(et == 3),
                          sem=(sPE if et == 3 else None))
                pv = sPE.n
                em.wait("vector", sPE, pv)
                em.waitf("vector", sPRS[0])
                em.waitf("vector", sPRS[1])
                em.vec("tensor_copy", t1[:, half * 512:half * 512 + 512], bank,
                       sem=sDVE)
                op_evac[ft] = sDVE.n
                em.wait("gpsimd", sDVE, op_evac[ft])
                em.dma("gpsimd",
                       partial.ap()[(mi % 2) * NC * DM + (mi // 2) * DM + 128 * ft:
                                    (mi % 2) * NC * DM + (mi // 2) * DM + 128 * (ft + 1),
                                    0:512],
                       t1[:, half * 512:half * 512 + 512], sPRS[mi % 2])
            em.wait("tensor", sACT, sACT.n)
            em.wait("tensor", sDVE, sDVE.n)
            if mseq == NMAC // 2 - 1:
                em.waitf("gpsimd", sPRS[0])
                cfg["rs_out"].append(cfg["mk_rs"](0))
            elif mseq == NMAC - 1:
                em.waitf("gpsimd", sPRS[1])
                cfg["rs_out"].append(cfg["mk_rs"](1))

    def mk_rs(halfsel):
        return collective(
            "ReduceScatter", ALU.add,
            [partial.ap()[halfsel * NC * DM:(halfsel + 1) * NC * DM, :].opt()],
            [red.ap()[halfsel * DM:(halfsel + 1) * DM, :].opt()])

    def inter_deq():
        for _ in range(8):
            if next(deq, "DONE") == "DONE":
                break

    cfg_a = dict(qT=qaD.ap(), kT=kaD.ap(), vT=vaD.ap(), per_head_kv=False,
                 rope=True, biased=False, wq=waqT.ap(), wk=wakT.ap(),
                 wv=wavT.ap(), wo=waoT.ap(), kw=128,
                 wait_cc_half=(ag_half[0], ag_half[1]),
                 wreg=R2, xbuf=R3[:, 0:16384], statps=PB2,
                 interleave=inter_deq, mk_rs=mk_rs, rs_out=[])
    emit_attn(cfg_a)
    for _ in deq:
        pass
    rs1 = cfg_a["rs_out"][1]

    if stop_after == "attn_a":
        _finish(em, tap_pairs, sOUT, sCC)
        em.replay_into(nc)
        return nc

    def emit_ffn(cc, src_x, dst_x, dst_lnloc, is_last):
        col = cc * 512
        em.wait("sync", sPE, sPE.n)
        em.wait("tensor", sDVE, sDVE.n)
        for kt in range(DMT):
            em.mm(PC[0:GG, 0:512], ohxg_s[:, kt * GG:(kt + 1) * GG],
                  x_sb[:, kt * 512:(kt + 1) * 512], start=(kt == 0),
                  stop=(kt == DMT - 1))
        for kt in range(DMT):
            em.mm(PB2[0:GG, 0:512], ohxu_s[:, kt * GG:(kt + 1) * GG],
                  x_sb[:, kt * 512:(kt + 1) * 512], start=(kt == 0),
                  stop=(kt == DMT - 1), sem=(sPE if kt == DMT - 1 else None))
        pv = sPE.n
        em.wait("vector", sPE, pv)
        em.vec("tensor_copy", xg_s[:, 0:512], PC[0:GG, 0:512])
        em.vec("tensor_copy", xg_s[:, 512:1024], PB2[0:GG, 0:512], sem=sDVE)
        em.wait("tensor", sDVE, sDVE.n)

        wpar_done = {}
        hpar_done = {}
        for nt in range(NTI):
            parw = nt % 2
            r, ntl = divmod(nt, NTIL)
            woff = parw * (GBLK + UBLK)
            if nt >= 2:
                em.wait("sync", sPE, wpar_done[nt - 2])
            em.dma("sync", R3[:, woff:woff + GBLK],
                   wsg.ap()[r * 128:(r + 1) * 128, ntl * GBLK:(ntl + 1) * GBLK],
                   sW[parw])
            em.dma("sync", R3[:, woff + GBLK:woff + GBLK + UBLK],
                   wsu.ap()[r * 128:(r + 1) * 128, ntl * UBLK:(ntl + 1) * UBLK],
                   sW[parw])
            em.waitf("tensor", sW[parw])
            half = nt % 2
            if nt >= 2:
                em.wait("tensor", sACT, hpar_done[nt - 2][0])
                em.wait("tensor", sDVE, hpar_done[nt - 2][1])
            for gi_, base in ((0, woff), (1, woff + GBLK)):
                acc = (PA if gi_ == 0 else PB)[:, half * 512:(half + 1) * 512]
                for kt in range(DMT):
                    em.mm(acc, R3[:, base + kt * 128:base + kt * 128 + 128],
                          x_sb[:, kt * 512:(kt + 1) * 512], start=(kt == 0),
                          stop=False)
                em.mm(acc, R3[0:GG, base + DM:base + DM + 128],
                      xg_s[0:GG, gi_ * 512:gi_ * 512 + 512],
                      start=False, stop=True, sem=sPE)
            gu = sPE.n
            em.wait("scalar", sPE, gu)
            sil = tb0[:, half * 512:half * 512 + 512]
            em.actv(sil, PA[:, half * 512:(half + 1) * 512], AF.Silu, sem=sACT)
            av = sACT.n
            em.wait("vector", sACT, av)
            em.vec("tensor_mul", h_sb[:, nt * 512:(nt + 1) * 512], sil,
                   PB[:, half * 512:(half + 1) * 512], sem=sDVE)
            hv = sDVE.n
            hpar_done[nt] = (av, hv)
            em.wait("tensor", sDVE, hv)
            wpar_done[nt] = em.mm(
                PC[0:88, 0:512], R3[:, woff + DM + 128:woff + DM + 216],
                h_sb[:, nt * 512:(nt + 1) * 512], start=(nt == 0),
                stop=(nt == NTI - 1), sem=sPE)
        hgv = sPE.n
        em.wait("vector", sPE, hgv)
        em.vec("tensor_copy", hg_s[:, :], PC[0:88, 0:512], sem=sDVE)
        em.wait("tensor", sDVE, sDVE.n)

        em.wait("sync", sPE, sPE.n)
        em.wait("sync", sDVE, sDVE.n)
        em.wait("scalar", sDVE, sDVE.n)
        prev_stat2 = 0
        wd_done = {}
        dn_evac = {}
        slot_dve = {}
        for ft in range(DMT):
            parw = ft % 2
            wdoff = parw * (NC * DBLK)
            if ft >= 2:
                em.wait("sync", sPE, wd_done[ft - 2])
            for r in range(NC):
                em.dma("sync", R2[:, wdoff + r * DBLK: wdoff + (r + 1) * DBLK],
                       wsd.ap()[r * 128:(r + 1) * 128,
                                ft * DBLK:(ft + 1) * DBLK], sD[parw])
            em.waitf("tensor", sD[parw])
            bank = PB2[:, parw * 512:parw * 512 + 512]
            if ft >= 2:
                em.wait("tensor", sACT, dn_evac[ft - 2])
            for kt in range(NTI):
                r2, kl = divmod(kt, NTIL)
                em.mm(bank, R2[:, wdoff + r2 * DBLK + kl * 128:
                               wdoff + r2 * DBLK + kl * 128 + 128],
                      h_sb[:, kt * 512:(kt + 1) * 512], start=(kt == 0),
                      stop=False)
            em.mm(bank, R2[0:GD, wdoff + SHN: wdoff + SHN + 128],
                  hg_s[0:GD, 0:512], start=False, stop=True, sem=sPE)
            dv_ = sPE.n
            wd_done[ft] = dv_
            em.wait("scalar", sPE, dv_)
            rsl = ft % 3
            a = t0[:, rsl * 512:(rsl + 1) * 512]
            if ft >= 3:
                em.waitf("scalar", sST[rsl])
                if is_last:
                    em.waitf("scalar", sOUT)
                em.wait("sync", sDVE, slot_dve.get(ft - 3, 0))
            em.actv(a, bank, AF.Copy, sem=sACT)
            dn_evac[ft] = sACT.n
            b = t1[:, rsl * 512:(rsl + 1) * 512]
            em.dma("sync", b, src_x[128 * ft:128 * (ft + 1), col:col + 512],
                   sTL[rsl])
            em.waitf("vector", sTL[rsl])
            em.wait("vector", sACT, dn_evac[ft])
            em.vec("tensor_add", a, a, b, sem=sDVE)
            mv = sDVE.n
            slot_dve[ft] = mv
            em.wait("gpsimd", sDVE, mv)
            if is_last:
                em.dma("gpsimd", out_ext.ap()[128 * ft:128 * (ft + 1),
                                              col:col + 512], a, sOUT)
            else:
                em.dma("gpsimd", dst_x[128 * ft:128 * (ft + 1), col:col + 512], a,
                       sST[rsl])
            if dst_lnloc is not None:
                xb = tb0[:, 1024:1536]
                em.wait("scalar", sPE, prev_stat2)
                em.wait("scalar", sDVE, mv)
                em.actv(xb, a, AF.Copy, sem=sACT)
                av2 = sACT.n
                em.wait("tensor", sACT, av2)
                em.mm(PA[0:1, 0:512], onesc_b[:, :], xb, start=(ft == 0),
                      stop=(ft == DMT - 1), sem=sPE)
                em.wait("scalar", sPE, sPE.n)
                em.actv(xb, a, AF.Square, sem=sACT)
                av3 = sACT.n
                em.wait("tensor", sACT, av3)
                prev_stat2 = em.mm(PA[0:1, 512:1024], onesc_b[:, :], xb,
                                   start=(ft == 0), stop=(ft == DMT - 1), sem=sPE)
        if dst_lnloc is None:
            return
        pe_done = sPE.n
        em.wait("vector", sPE, pe_done)
        em.vec("tensor_scalar", rows[0:1, 0:512], PA[0:1, 0:512], 1.0 / DM, None,
               ALU.mult)
        em.vec("tensor_scalar", rows[0:1, 512:1024], PA[0:1, 512:1024], 1.0 / DM,
               None, ALU.mult)
        em.vec("tensor_mul", rows[0:1, 1024:1536], rows[0:1, 0:512],
               rows[0:1, 0:512])
        em.vec("tensor_sub", rows[0:1, 512:1024], rows[0:1, 512:1024],
               rows[0:1, 1024:1536], sem=sDVE)
        vv = sDVE.n
        em.wait("scalar", sDVE, vv)
        em.actv(rows[0:1, 1024:1536], rows[0:1, 512:1024], AF.Ln,
                bias=eps_c[0:1, :])
        em.actv(rows[0:1, 512:1024], rows[0:1, 1024:1536], AF.Exp, scale=-0.5,
                sem=sACT)
        av = sACT.n
        em.wait("vector", sACT, av)
        em.vec("tensor_mul", rows[0:1, 1024:1536], rows[0:1, 0:512],
               rows[0:1, 512:1024], sem=sDVE)
        mrv = sDVE.n
        em.wait("tensor", sDVE, mrv)
        em.mm(PB[:, 0:512], ones1[:, :], rows[0:1, 512:1024], start=True, stop=True)
        em.mm(PB[:, 512:1024], ones1[:, :], rows[0:1, 1024:1536], start=True,
              stop=True, sem=sPE)
        bcv = sPE.n
        for i in range(3):
            em.waitf("sync", sST[i])
        em.waitf("vector", sLNS)
        em.wait("sync", sDVE, sDVE.n)
        em.wait("sync", sACT, sACT.n)
        slot_dve2 = {}
        for ft in range(DMT):
            rsl = ft % 3
            a = t0[:, rsl * 512:(rsl + 1) * 512]
            if ft >= 3:
                em.wait("sync", sDVE, slot_dve2.get(ft - 3, 0))
            em.dma("sync", a, dst_x[128 * ft:128 * (ft + 1), col:col + 512],
                   sTL[rsl])
            em.waitf("vector", sTL[rsl])
            em.wait("vector", sPE, bcv)
            b = t1[:, rsl * 512:(rsl + 1) * 512]
            em.vec("tensor_mul", b, a, PB[:, 0:512])
            ln_t = x_sb[:, ft * 512:(ft + 1) * 512]
            em.vec("tensor_sub", ln_t, b, PB[:, 512:1024], sem=sDVE)
            slot_dve2[ft] = sDVE.n
            em.wait("gpsimd", sDVE, sDVE.n)
            em.dma("gpsimd", dst_lnloc[128 * ft:128 * (ft + 1), 0:512],
                   ln_t, sLNS)

    em.wait("tensor", sCC, agd_d["agu"])
    em.wait("sync", sCC, agd_d["agd"])
    ag2_half = {}
    for cc in range(NCH):
        rsw = cfg_a["rs_out"][cc]
        em.wait("sync", sCC, rsw)
        em.wait("vector", sCC, rsw)
        emit_merge_ln_chunk(cc, x0T.ap(),
                            red.ap()[cc * DM:(cc + 1) * DM, :], xA.ap(), None,
                            None)
        em.wait("gpsimd", sCC, ag_half[1])
        emit_ffn(cc, xA.ap(), xB.ap(),
                 ln_loc.ap()[cc * DM:(cc + 1) * DM, :], is_last=False)
        em.waitf("gpsimd", sLNS)
        ag2_half[cc] = mk_ag_ln(cc)
    ag2a, ag2b = ag2_half[0], ag2_half[1]

    if stop_after == "ffn1":
        _finish(em, tap_pairs, sOUT, sCC)
        em.replay_into(nc)
        return nc

    cfg_b = dict(qT=qbD.ap(), kT=kbD.ap(), vT=vbD.ap(), per_head_kv=True,
                 rope=False, biased=True, wq=wbqT.ap(), wk=wbkT.ap(),
                 wv=wbvT.ap(), wo=wboT.ap(), kw=512,
                 wait_cc_half=(ag2a, ag2b),
                 wreg=R1, xbuf=x_sb, statps=PC,
                 interleave=None, mk_rs=mk_rs, rs_out=[])
    emit_attn(cfg_b)
    rs2 = cfg_b["rs_out"][1]

    for cc in range(NCH):
        rsw = cfg_b["rs_out"][cc]
        em.wait("sync", sCC, rsw)
        em.wait("vector", sCC, rsw)
        emit_merge_ln_chunk(cc, xB.ap(),
                            red.ap()[cc * DM:(cc + 1) * DM, :], xA.ap(), None,
                            bo_s)
        emit_ffn(cc, xA.ap(), None, None, is_last=True)

    _finish(em, tap_pairs, sOUT, sCC)
    em.replay_into(nc)
    return nc


def _finish(em, tap_pairs, sOUT, sCC):
    if sCC.n:
        em.wait("gpsimd", sCC, sCC.n)
    for tout, tsrc in tap_pairs:
        em.dma("gpsimd", tout.ap(), tsrc.ap(), sOUT)
    em.waitf("gpsimd", sOUT)
    em.waitf("sync", sOUT)


def kernel(**inputs):
    in_maps = _host_prep(inputs)
    if "nc" not in _NC_CACHE:
        _NC_CACHE["nc"] = build_nc()
    nc = _NC_CACHE["nc"]
    res = run_bass_kernel_spmd(nc, in_maps, core_ids=list(range(NC)))
    outs = [res.results[c]["out"] for c in range(NC)]
    full = np.concatenate([o.T for o in outs], axis=0)
    return full[None, :, :].astype(np.float32, copy=False)


# version 11 (final)
# speedup vs baseline: 1.1205x; 1.0067x over previous
import numpy as np
import ml_dtypes
from contextlib import ExitStack

import concourse.bass as bass
import concourse.mybir as mybir
from concourse.bass_utils import run_bass_kernel_spmd

BF16 = mybir.dt.bfloat16
F32 = mybir.dt.float32
U8 = mybir.dt.uint8
I32 = mybir.dt.int32
AF = mybir.ActivationFunctionType
ALU = mybir.AluOpType
AXX = mybir.AxisListType.X
NBF = ml_dtypes.bfloat16

NC = 8
S = 8192
DM = 4096
LS = S // NC
NCH = LS // 512
INTER_P = 11264
NTI = INTER_P // 128
NTIL = NTI // NC
SHN = INTER_P // NC
GG = 32
GD = 86
SCALE = 1.0 / float(np.sqrt(128.0))
EPS = 1e-5
NMAC = S // 512
DMT = DM // 128

GBLK = DM + 128 + 88 + 40
UBLK = DM + 128
DBLK = SHN + 128

ENGS = ("sync", "gpsimd", "vector", "scalar", "tensor")

_NC_CACHE = {}


def _bytes_view(qw):
    kk8, n = qw.shape
    b = qw.view('<u4').view(np.uint8).reshape(kk8, n, 4).transpose(0, 2, 1)
    return np.ascontiguousarray(b.reshape(kk8 * 4, n))


def _host_prep(inp):
    x = np.asarray(inp["L0_attn_a_x"])[0]
    xT = np.ascontiguousarray(x.T)

    w_inT = np.ascontiguousarray(np.asarray(inp["L0_attn_a_w_in"]).T)
    w_outT = np.ascontiguousarray(np.asarray(inp["L0_attn_a_w_out"]).T)
    WqT = np.ascontiguousarray(np.asarray(inp["L0_attn_b_Wq"]).T)
    WkT = np.ascontiguousarray(np.asarray(inp["L0_attn_b_Wk"]).T)
    WvT = np.ascontiguousarray(np.asarray(inp["L0_attn_b_Wv"]).T)
    WoT = np.ascontiguousarray(np.asarray(inp["L0_attn_b_Wo"]).T)
    bq = np.asarray(inp["L0_attn_b_bq"]); bk = np.asarray(inp["L0_attn_b_bk"])
    bv = np.asarray(inp["L0_attn_b_bv"]); bo = np.asarray(inp["L0_attn_b_bo"])

    qb_g = _bytes_view(np.asarray(inp["L0_ffn_gate_qweight"]))
    qb_u = _bytes_view(np.asarray(inp["L0_ffn_up_qweight"]))
    qb_d = _bytes_view(np.asarray(inp["L0_ffn_down_qweight"]))
    sc_g = np.asarray(inp["L0_ffn_gate_scales"]).astype(np.float32, copy=False)
    sc_u = np.asarray(inp["L0_ffn_up_scales"]).astype(np.float32, copy=False)
    sc_d = np.asarray(inp["L0_ffn_down_scales"]).astype(np.float32, copy=False)
    qz_g = np.asarray(inp["L0_ffn_gate_qzeros"]).astype(np.int32, copy=False)
    qz_u = np.asarray(inp["L0_ffn_up_qzeros"]).astype(np.int32, copy=False)
    qz_d = np.asarray(inp["L0_ffn_down_qzeros"]).astype(np.int32, copy=False)
    gi_g = np.asarray(inp["L0_ffn_gate_g_idx"]).astype(np.int32)
    gi_u = np.asarray(inp["L0_ffn_up_g_idx"]).astype(np.int32)
    gi_d = np.asarray(inp["L0_ffn_down_g_idx"]).astype(np.int32)

    gi_d_p = np.full(INTER_P, -1, np.int32)
    for r in range(NC):
        gi_d_p[r * SHN: r * SHN + 1376] = gi_d[r * 1376:(r + 1) * 1376]

    inv_freq = 1.0 / (10000.0 ** (np.arange(0, 128, 2, dtype=np.float32) / 128.0))
    ang = np.arange(S, dtype=np.float32)[:, None] * inv_freq[None, :]
    cos = np.cos(ang).astype(np.float32).T
    sin = np.sin(ang).astype(np.float32).T
    cosR = np.ascontiguousarray(np.concatenate([cos, cos], axis=0))
    sinS = np.ascontiguousarray(np.concatenate([-sin, sin], axis=0))

    tri = np.ascontiguousarray(
        (np.arange(512)[:, None] <= np.arange(512)[None, :]).astype(NBF))
    iota128x88 = np.ascontiguousarray(
        np.broadcast_to(np.arange(88, dtype=np.float32), (128, 88)))
    iota88c = np.ascontiguousarray(np.arange(88, dtype=np.float32)[:, None])
    ident = np.ascontiguousarray(np.eye(128, dtype=NBF))

    def col4(v):
        return np.ascontiguousarray(v.reshape(4, 128).T)

    in_maps = []
    for c in range(NC):
        qr = slice(512 * c, 512 * (c + 1))
        m = {
            "x0T": np.ascontiguousarray(xT[:, LS * c: LS * (c + 1)]),
            "waqT": np.ascontiguousarray(w_inT[:, qr]),
            "wakT": np.ascontiguousarray(w_inT[:, 4096 + 128 * c: 4096 + 128 * (c + 1)]),
            "wavT": np.ascontiguousarray(w_inT[:, 5120 + 128 * c: 5120 + 128 * (c + 1)]),
            "waoT": np.ascontiguousarray(w_outT[qr, :]),
            "wbqT": np.ascontiguousarray(WqT[:, qr]),
            "wbkT": np.ascontiguousarray(WkT[:, qr]),
            "wbvT": np.ascontiguousarray(WvT[:, qr]),
            "wboT": np.ascontiguousarray(WoT[qr, :]),
            "bq4": col4(bq[qr]), "bk4": col4(bk[qr]), "bv4": col4(bv[qr]),
            "bo32": np.ascontiguousarray(bo.reshape(DMT, 128).T),
            "qbg": np.ascontiguousarray(
                np.pad(qb_g[:, 1376 * c:1376 * (c + 1)], ((0, 0), (0, 32)))),
            "qbu": np.ascontiguousarray(
                np.pad(qb_u[:, 1376 * c:1376 * (c + 1)], ((0, 0), (0, 32)))),
            "qbd": np.ascontiguousarray(
                np.pad(qb_d[688 * c:688 * (c + 1), :], ((0, 16), (0, 0)))),
            "scg_sh": np.ascontiguousarray(
                np.pad(sc_g[:, 1376 * c:1376 * (c + 1)], ((0, 0), (0, 32)))),
            "scu_sh": np.ascontiguousarray(
                np.pad(sc_u[:, 1376 * c:1376 * (c + 1)], ((0, 0), (0, 32)))),
            "scd": sc_d,
            "qzg_sh": np.ascontiguousarray(
                np.pad(qz_g[:, 172 * c:172 * (c + 1)], ((0, 0), (0, 4)))),
            "qzu_sh": np.ascontiguousarray(
                np.pad(qz_u[:, 172 * c:172 * (c + 1)], ((0, 0), (0, 4)))),
            "qzd": qz_d,
            "gig": np.ascontiguousarray(gi_g[:, None]),
            "giu": np.ascontiguousarray(gi_u[:, None]),
            "gidp": np.ascontiguousarray(gi_d_p[:, None]),
            "gidsh": np.ascontiguousarray(gi_d_p[SHN * c: SHN * (c + 1)])[:, None],
            "cosR": cosR, "sinS": sinS, "tri": tri,
            "iota88": iota128x88, "iota88c": iota88c,
            "ident": ident,
        }
        in_maps.append(m)
    return in_maps


class Sem:
    __slots__ = ("h", "n")

    def __init__(self, handle):
        self.h = handle
        self.n = 0

    def bump(self, k):
        self.n += k
        return self.n


class Em:
    def __init__(self, nc):
        self.nc = nc
        self.q = {e: [] for e in ENGS}
        self.ctx = ExitStack()
        self.auto = {}

    def sem(self, name):
        return Sem(self.ctx.enter_context(self.nc.semaphore(name)))

    def sb(self, name, shape, dtype):
        return self.ctx.enter_context(self.nc.sbuf_tensor(name, shape, dtype))

    def psum(self, name, shape, dtype=F32):
        return self.ctx.enter_context(self.nc.psum_tensor(name, shape, dtype))

    def emit(self, eng, fn):
        self.q[eng].append(fn)

    def dma(self, eng, out, in_, sem, **kw):
        v = sem.bump(16)
        self.emit(eng, lambda e, o=out, i=in_, s=sem.h, kw=kw:
                  e.dma_start(out=o, in_=i, **kw).then_inc(s, 16))
        return v

    def wait(self, eng, sem, val):
        assert val <= sem.n
        if val > 0:
            self.emit(eng, lambda e, s=sem.h, vv=val: e.wait_ge(s, vv))

    def waitf(self, eng, sem):
        v = sem.n
        if v > 0:
            self.emit(eng, lambda e, s=sem.h, vv=v: e.wait_ge(s, vv))

    def mm(self, out, lhsT, rhs, start, stop, sem=None, tile_position=None):
        v = sem.bump(1) if sem else None
        import inspect
        fr = inspect.currentframe().f_back
        tag = f"mm@{fr.f_lineno}"
        def f(t, o=out, l=lhsT, r=rhs, st=start, sp=stop, s=sem, tp=tile_position,
              tag=tag):
            i = t.matmul(o, l, r, start=st, stop=sp, tile_position=tp)
            i.annotate(tag)
            if s is not None:
                i.then_inc(s.h, 1)
        self.emit("tensor", f)
        return v

    def trans(self, out, in_, ident, sem=None):
        v = sem.bump(1) if sem else None
        def f(t, o=out, i=in_, d=ident, s=sem):
            ins = t.transpose(o, i, d)
            if s is not None:
                ins.then_inc(s.h, 1)
        self.emit("tensor", f)
        return v

    def actv(self, out, in_, func, sem=None, bias=0.0, scale=1.0):
        s = self.auto["scalar"]
        pre = s.n
        v = s.bump(1)
        def f(a, o=out, i=in_, fu=func, b=bias, sc=scale, pre=pre, s=s):
            if pre > 0:
                a.wait_ge(s.h, pre)
            a.activation(o, i, fu, bias=b, scale=sc).then_inc(s.h, 1)
        self.emit("scalar", f)
        return v

    def vec(self, fn_name, *args, sem=None, **kw):
        s = self.auto["vector"]
        pre = s.n
        v = s.bump(1)
        def f(e, a=args, kw=kw, fn=fn_name, pre=pre, s=s):
            if pre > 0:
                e.wait_ge(s.h, pre)
            getattr(e, fn)(*a, **kw).then_inc(s.h, 1)
        self.emit("vector", f)
        return v

    def replay_into(self, nc):
        with nc.Block() as block:
            for name in ENGS:
                ops = self.q[name]
                def runner(e, ops=ops):
                    for op in ops:
                        op(e)
                getattr(block, name)(runner)


def build_nc(taps=(), stop_after=None):
    nc = bass.Bass("TRN2", target_bir_lowering=False, debug=False)

    def din(name, shape, dtype):
        return nc.dram_tensor(name, shape, dtype, kind="ExternalInput")

    x0T = din("x0T", [DM, LS], F32)
    waqT = din("waqT", [DM, 512], F32); wakT = din("wakT", [DM, 128], F32)
    wavT = din("wavT", [DM, 128], F32); waoT = din("waoT", [512, DM], F32)
    wbqT = din("wbqT", [DM, 512], F32); wbkT = din("wbkT", [DM, 512], F32)
    wbvT = din("wbvT", [DM, 512], F32); wboT = din("wboT", [512, DM], F32)
    bq4 = din("bq4", [128, 4], F32); bk4 = din("bk4", [128, 4], F32)
    bv4 = din("bv4", [128, 4], F32); bo32 = din("bo32", [128, DMT], F32)
    qbg = din("qbg", [2048, SHN], U8); qbu = din("qbu", [2048, SHN], U8)
    qbd = din("qbd", [704, DM], U8)
    scg_sh = din("scg_sh", [GG, SHN], F32); scu_sh = din("scu_sh", [GG, SHN], F32)
    scd = din("scd", [GD, DM], F32)
    qzg_sh = din("qzg_sh", [GG, SHN // 8], I32)
    qzu_sh = din("qzu_sh", [GG, SHN // 8], I32)
    qzd = din("qzd", [GD, 512], I32)
    gig = din("gig", [DM, 1], I32); giu = din("giu", [DM, 1], I32)
    gidp = din("gidp", [INTER_P, 1], I32); gidsh = din("gidsh", [SHN, 1], I32)
    cosR = din("cosR", [128, S], F32); sinS = din("sinS", [128, S], F32)
    tri = din("tri", [512, 512], BF16)
    iota88 = din("iota88", [128, 88], F32); iota88c = din("iota88c", [88, 1], F32)
    ident = din("ident", [128, 128], BF16)

    out_ext = nc.dram_tensor("out", [DM, LS], F32, kind="ExternalOutput")

    ln_loc = nc.dram_tensor("ln_loc", [2 * DM, 512], BF16)
    ln_full = nc.dram_tensor("ln_full", [2 * NC * DM, 512], BF16)
    partial = nc.dram_tensor("partial", [2 * NC * DM, 512], F32)
    red = nc.dram_tensor("red", [2 * DM, 512], F32)
    xA = nc.dram_tensor("xA", [DM, LS], F32)
    xB = nc.dram_tensor("xB", [DM, LS], F32)
    qaD = nc.dram_tensor("qaD", [512, S], BF16)
    kaD = nc.dram_tensor("kaD", [128, S], BF16)
    vaD = nc.dram_tensor("vaD", [S, 128], BF16)
    qbD = nc.dram_tensor("qbD", [512, S], BF16)
    kbD = nc.dram_tensor("kbD", [512, S], BF16)
    vbD = nc.dram_tensor("vbD", [S, 512], BF16)
    wsg_sh = nc.dram_tensor("wsg_sh", [128, NTIL * GBLK], BF16)
    wsu_sh = nc.dram_tensor("wsu_sh", [128, NTIL * UBLK], BF16)
    wsd_sh = nc.dram_tensor("wsd_sh", [128, DMT * DBLK], BF16)
    wsg = nc.dram_tensor("wsg", [NC * 128, NTIL * GBLK], BF16)
    wsu = nc.dram_tensor("wsu", [NC * 128, NTIL * UBLK], BF16)
    wsd = nc.dram_tensor("wsd", [NC * 128, DMT * DBLK], BF16)

    internal = {"ln_loc": ln_loc, "ln_full": ln_full, "partial": partial,
                "red": red, "xA": xA, "xB": xB, "qaD": qaD, "kaD": kaD,
                "vaD": vaD, "qbD": qbD, "kbD": kbD, "vbD": vbD,
                "wsg_sh": wsg_sh, "wsu_sh": wsu_sh, "wsd_sh": wsd_sh,
                "wsg": wsg, "wsu": wsu, "wsd": wsd}
    tap_pairs = []
    for tname in taps:
        t = internal[tname]
        tap_pairs.append((nc.dram_tensor("tap_" + tname, list(t.ap().shape),
                                         t.ap().dtype, kind="ExternalOutput"), t))

    em = Em(nc)

    R1 = em.sb("R1", [128, 45056], BF16)
    R2 = em.sb("R2", [128, 24576], BF16)
    R3 = em.sb("R3", [128, 17152], BF16)

    h_sb = R1[:, 0:45056]
    x_sb = R2[:, 0:16384]
    wo_sb = R2[:, 0:16384]

    dq_u8 = R1[:, 0:2048].bitcast(U8)
    dq_f = R1[:, 2048:10240].bitcast(F32)
    dq_b = R1[:, 10240:14336]
    zz_i = R1[:, 14336:23552].bitcast(I32)
    zz_f = R1[:, 23552:31744].bitcast(F32)
    zz_n = R1[:, 31744:39936].bitcast(F32)
    sc_bf = R1[:, 39936:44032]

    kv_k = R3[:, 0:4096]
    kv_v = R3[:, 4096:8192]
    q_mac = R3[:, 8192:10240]
    p_sb = R3[:, 10240:14336]
    att_sb = R3[:, 14336:16384]

    ones1 = em.sb("ones1", [1, 128], F32)
    onesc_b = em.sb("onesc_b", [128, 1], BF16)
    ident_s = em.sb("ident_s", [128, 128], BF16)
    tri_s = em.sb("tri_s", [128, 2048], BF16)
    iota_s = em.sb("iota_s", [128, 88], F32)
    iotac_s = em.sb("iotac_s", [88, 1], F32)
    ohxg_s = em.sb("ohxg_s", [128, DMT * GG], BF16)
    ohxu_s = em.sb("ohxu_s", [128, DMT * GG], BF16)
    bias_all = em.sb("bias_all", [128, 64], F32)
    bq_s = em.sb("bq_s", [128, 4], F32)
    bk_s = em.sb("bk_s", [128, 4], F32)
    bv_s = em.sb("bv_s", [128, 4], F32)
    bo_s = em.sb("bo_s", [128, DMT], F32)
    xg_s = em.sb("xg_s", [GG, 1024], BF16)
    hg_s = em.sb("hg_s", [88, 512], BF16)
    dq_gi = em.sb("dq_gi", [88, 256], I32)
    dq_gf = em.sb("dq_gf", [88, 256], F32)
    dq_ohT = em.sb("dq_ohT", [88, 256], BF16)
    gcol_i = em.sb("gcol_i", [128, 1], I32)
    gcol_f = em.sb("gcol_f", [128, 1], F32)
    oh1 = em.sb("oh1", [128, 88], BF16)

    t0 = em.sb("t0", [128, 1536], F32)
    t1 = em.sb("t1", [128, 1536], F32)
    tb0 = em.sb("tb0", [128, 1536], BF16)
    vtr = em.sb("vtr", [128, 512], BF16)
    rope_c = t0[:, 1024:1536]
    rope_s = t1[:, 1024:1536]
    den_s = t1[:, 0:512]
    rows = em.sb("rows", [1, 1536], F32)
    qn_s = em.sb("qn_s", [1, 64], F32)
    eps_c = em.sb("eps_c", [1, 1], F32)
    kn_s = em.sb("kn_s", [1, 80], F32)
    cneg_s = em.sb("cneg_s", [1, 64], F32)

    PA = em.psum("PA", [128, 1024])
    PB = em.psum("PB", [128, 1024])
    PB2 = em.psum("PB2", [128, 1024])
    PC = em.psum("PC", [128, 512])
    PT = em.psum("PT", [128, 512], BF16)

    sPE = em.sem("sPE"); sACT = em.sem("sACT"); sDVE = em.sem("sDVE")
    em.auto = {"vector": sDVE, "scalar": sACT}
    sCC = em.sem("sCC")
    sCN = em.sem("sCN")
    sTL = [em.sem(f"sTL{i}") for i in range(3)]
    sST = [em.sem(f"sST{i}") for i in range(3)]
    sX = em.sem("sX"); sQ = em.sem("sQ")
    sKV = [em.sem("sKV0"), em.sem("sKV1"), em.sem("sKV2"), em.sem("sKV3")]
    sW = [em.sem("sWg0"), em.sem("sWg1")]
    sD = [em.sem("sWd0"), em.sem("sWd1")]
    sPRS = [em.sem("sPRS0"), em.sem("sPRS1")]
    sDQ = em.sem("sDQ"); sDQ2 = em.sem("sDQ2"); sSC = em.sem("sSC"); sAW = em.sem("sAW")
    sWO = em.sem("sWO"); sSW = em.sem("sSW"); sSW2 = em.sem("sSW2"); sVT = em.sem("sVT")
    sQKV = em.sem("sQKV"); sWS = em.sem("sWS"); sWS2 = em.sem("sWS2"); sLNS = em.sem("sLNS")
    sOUT = em.sem("sOUT")

    rg = [list(range(NC))]

    def collective(kind, op, ins, outs):
        v = sCC.bump(1)
        def f(g, k=kind, o=op, i=ins, u=outs):
            g.collective_compute(k, o, replica_groups=rg, ins=i, outs=u).then_inc(sCC.h, 1)
        em.emit("gpsimd", f)
        return v

    em.dma("sync", ident_s[:, :], ident.ap(), sCN)
    for l in range(4):
        em.dma("sync", tri_s[:, l * 512:(l + 1) * 512],
               tri.ap()[128 * l:128 * (l + 1), :], sCN)
    em.dma("sync", iota_s[:, :], iota88.ap(), sCN)
    em.dma("sync", iotac_s[:, :], iota88c.ap(), sCN)
    em.dma("sync", bq_s[:, :], bq4.ap(), sCN)
    em.dma("sync", bk_s[:, :], bk4.ap(), sCN)
    em.dma("sync", bv_s[:, :], bv4.ap(), sCN)
    em.dma("sync", bo_s[:, :], bo32.ap(), sCN)
    for e in ("vector", "scalar", "tensor", "gpsimd"):
        em.waitf(e, sCN)
    em.vec("memset", ones1[:, :], 1.0)
    em.vec("memset", eps_c[:, :], EPS)
    em.vec("memset", tb0[:, 0:1024], 0.0)
    em.vec("memset", onesc_b[:, :], 1.0, sem=sDVE)
    consts_v = sDVE.n
    em.wait("tensor", sDVE, consts_v)
    em.wait("scalar", sDVE, consts_v)
    em.wait("gpsimd", sDVE, consts_v)
    for shd, w in ((wsg_sh, NTIL * GBLK), (wsu_sh, NTIL * UBLK),
                   (wsd_sh, DMT * DBLK)):
        w1 = (w // 1024) * 1024
        em.dma("gpsimd", shd.ap()[:, 0:w1].rearrange("p (r c) -> p r c", c=1024),
               tb0[:, 0:1024].unsqueeze(1).broadcast_to([128, w1 // 1024, 1024]),
               sWS)
        if w1 < w:
            em.dma("gpsimd", shd.ap()[:, w1:w], tb0[:, 0:w - w1], sWS)
    em.waitf("gpsimd", sWS)
    em.waitf("scalar", sWS)
    em.waitf("vector", sWS)

    def emit_merge_ln_chunk(cc, src_x, src_red, dst_x, dst_lnloc, bias_cols):
        col = cc * 512
        em.waitf("vector", sLNS)
        em.wait("vector", sPE, sPE.n)
        em.wait("sync", sDVE, sDVE.n)
        em.wait("sync", sACT, sACT.n)
        em.waitf("sync", sOUT)
        em.waitf("sync", sPRS[0])
        em.waitf("sync", sPRS[1])
        slot_act = {}
        slot_dve = {}
        prev_stat = 0
        for ft in range(DMT):
            r = ft % 3
            a = t0[:, r * 512:(r + 1) * 512]
            if ft >= 3:
                em.waitf("sync", sST[r])
                em.wait("sync", sACT, slot_act.get(ft - 3, 0))
                em.wait("sync", sDVE, slot_dve.get(ft - 3, 0))
            em.dma("sync", a, src_x[128 * ft:128 * (ft + 1), col:col + 512], sTL[r])
            if src_red is not None:
                b = t1[:, r * 512:(r + 1) * 512]
                em.dma("sync", b, src_red[128 * ft:128 * (ft + 1), 0:512],
                       sTL[r])
                em.waitf("vector", sTL[r])
                if bias_cols is not None:
                    em.vec("tensor_scalar", b, b, bias_cols[:, ft:ft + 1], None,
                           ALU.add)
                em.vec("tensor_add", a, a, b, sem=sDVE)
                mv = sDVE.n
                slot_dve[ft] = mv
                em.wait("gpsimd", sDVE, mv)
                em.dma("gpsimd", dst_x[128 * ft:128 * (ft + 1), col:col + 512], a,
                       sST[r])
                em.wait("scalar", sDVE, mv)
            else:
                em.waitf("scalar", sTL[r])
            xb = tb0[:, 0:512]
            sq = tb0[:, 512:1024]
            em.wait("scalar", sPE, prev_stat)
            em.actv(xb, a, AF.Copy)
            em.actv(sq, a, AF.Square, sem=sACT)
            av = sACT.n
            slot_act[ft] = av
            em.wait("tensor", sACT, av)
            em.mm(PA[0:1, 0:512], onesc_b[:, :], xb, start=(ft == 0),
                  stop=(ft == DMT - 1))
            prev_stat = em.mm(PA[0:1, 512:1024], onesc_b[:, :], sq, start=(ft == 0),
                              stop=(ft == DMT - 1), sem=sPE)
        pe_done = sPE.n
        em.wait("vector", sPE, pe_done)
        em.vec("tensor_scalar", rows[0:1, 0:512], PA[0:1, 0:512], 1.0 / DM, None,
               ALU.mult)
        em.vec("tensor_scalar", rows[0:1, 512:1024], PA[0:1, 512:1024], 1.0 / DM,
               None, ALU.mult)
        em.vec("tensor_mul", rows[0:1, 1024:1536], rows[0:1, 0:512],
               rows[0:1, 0:512])
        em.vec("tensor_sub", rows[0:1, 512:1024], rows[0:1, 512:1024],
               rows[0:1, 1024:1536], sem=sDVE)
        vv = sDVE.n
        em.wait("scalar", sDVE, vv)
        em.actv(rows[0:1, 1024:1536], rows[0:1, 512:1024], AF.Ln,
                bias=eps_c[0:1, :])
        em.actv(rows[0:1, 512:1024], rows[0:1, 1024:1536], AF.Exp, scale=-0.5,
                sem=sACT)
        av = sACT.n
        em.wait("vector", sACT, av)
        em.vec("tensor_mul", rows[0:1, 1024:1536], rows[0:1, 0:512],
               rows[0:1, 512:1024], sem=sDVE)
        mrv = sDVE.n
        em.wait("tensor", sDVE, mrv)
        em.mm(PB[:, 0:512], ones1[:, :], rows[0:1, 512:1024], start=True, stop=True)
        em.mm(PB[:, 512:1024], ones1[:, :], rows[0:1, 1024:1536], start=True,
              stop=True, sem=sPE)
        bcv = sPE.n
        rdsrc = dst_x if (src_red is not None and dst_x is not None) else src_x
        for i in range(3):
            em.waitf("sync", sST[i])
        em.wait("sync", sDVE, sDVE.n)
        em.wait("sync", sACT, sACT.n)
        slot_dve2 = {}
        for ft in range(DMT):
            r = ft % 3
            a = t0[:, r * 512:(r + 1) * 512]
            if ft >= 3:
                em.wait("sync", sDVE, slot_dve2.get(ft - 3, 0))
            em.dma("sync", a, rdsrc[128 * ft:128 * (ft + 1), col:col + 512], sTL[r])
            em.waitf("vector", sTL[r])
            em.wait("vector", sPE, bcv)
            b = t1[:, r * 512:(r + 1) * 512]
            em.vec("tensor_mul", b, a, PB[:, 0:512])
            ln_t = x_sb[:, ft * 512:(ft + 1) * 512]
            em.vec("tensor_sub", ln_t, b, PB[:, 512:1024], sem=sDVE)
            lv = sDVE.n
            slot_dve2[ft] = lv
            if dst_lnloc is not None:
                em.wait("gpsimd", sDVE, lv)
                em.dma("gpsimd", dst_lnloc[128 * ft:128 * (ft + 1), 0:512],
                       ln_t, sLNS)

    ag_half = {}

    def mk_ag_ln(h):
        return collective(
            "AllGather", ALU.bypass,
            [ln_loc.ap()[h * DM:(h + 1) * DM, :].opt()],
            [ln_full.ap()[h * NC * DM:(h + 1) * NC * DM, :].opt()])

    for cc in range(NCH):
        emit_merge_ln_chunk(cc, x0T.ap(), None, None,
                            ln_loc.ap()[cc * DM:(cc + 1) * DM, :], None)
        em.waitf("gpsimd", sLNS)
        ag_half[cc] = mk_ag_ln(cc)
    ag1 = ag_half[1]

    if stop_after == "ln1":
        _finish(em, tap_pairs, sOUT, sCC)
        em.replay_into(nc)
        return nc

    def blocked(dst_dram, blk):
        return dst_dram.ap().rearrange("p (b c) -> p b c", c=blk)

    def emit_dequant_gen(qb_ap, sc_ap, qz_ap, gidx_ap, nkt, ncols, G, nblk, blk,
                         dst_dram, want_ohh, gidx_oh_ap, wofs):
        em.wait("sync", sDVE, sDVE.n)
        em.wait("gpsimd", sPE, sPE.n)
        em.dma("gpsimd", sc_bf[0:G, 0:ncols], sc_ap, sSC)
        em.waitf("tensor", sSC)
        dst = blocked(dst_dram, blk)
        dbuf = ncols <= 2048
        back = 2 if dbuf else 1
        unp0, mm0 = sDVE.n, sPE.n
        st0 = {0: sWS.n, 1: sWS2.n}
        unp_mark = {}
        mm_mark = {}
        store_mark = {}
        for kt in range(nkt):
            p = (kt % 2) if dbuf else 0
            u8v = dq_u8[:, p * 2048:p * 2048 + ncols]
            fv = dq_f[:, p * 2048:p * 2048 + ncols]
            bv = dq_b[:, p * 2048:p * 2048 + ncols]
            giv = dq_gi[0:G, p * 128:p * 128 + 128]
            gfv = dq_gf[0:G, p * 128:p * 128 + 128]
            ohv = dq_ohT[0:G, p * 128:p * 128 + 128]
            em.wait("sync", sDVE, unp_mark.get(kt - back, unp0))
            bsrc = qb_ap[64 * kt:64 * (kt + 1), :].unsqueeze(0) \
                .broadcast_to([2, 64, ncols])
            dqsem = sDQ2 if (dbuf and p == 1) else sDQ
            em.dma("sync", u8v, bsrc, dqsem)
            gsrc = gidx_ap[kt * 128:(kt + 1) * 128, 0:1] \
                .rearrange("k x -> x k").broadcast_to([G, 128])
            em.dma("sync", giv, gsrc, dqsem)
            em.waitf("vector", dqsem)
            wsem = sWS2 if (dbuf and p == 1) else sWS
            em.wait("vector", wsem, store_mark.get(kt - back, st0[p]))
            em.wait("vector", sPE, mm_mark.get(kt - back, mm0))
            em.vec("tensor_copy", gfv,
                   giv.rearrange("g (k2 o) -> g k2 o", o=2)
                   .transpose([0, 2, 1]))
            em.vec("tensor_scalar", ohv, gfv, iotac_s[0:G, :],
                   None, ALU.is_equal)
            em.vec("tensor_scalar", u8v[0:64, :], u8v[0:64, :],
                   15, None, ALU.bitwise_and)
            em.vec("tensor_scalar", u8v[64:128, :], u8v[64:128, :],
                   4, 15, ALU.logical_shift_right, ALU.bitwise_and)
            em.vec("tensor_copy", fv, u8v, sem=sDVE)
            dv = sDVE.n
            unp_mark[kt] = dv
            nchk = (ncols + 511) // 512
            for ch in range(nchk):
                a0 = ch * 512
                a1 = min(ncols, a0 + 512)
                em.wait("tensor", sDVE, dv if ch == 0 else sDVE.n)
                em.mm(PC[:, 0:a1 - a0], ohv, sc_bf[0:G, a0:a1],
                      start=True, stop=True, sem=sPE)
                em.wait("vector", sPE, sPE.n)
                em.vec("tensor_mul", bv[:, a0:a1], fv[:, a0:a1],
                       PC[:, 0:a1 - a0], sem=sDVE)
            mm_mark[kt] = sPE.n
            em.wait("gpsimd", sDVE, sDVE.n)
            dstp = dst_dram.ap().rearrange("(k2 o) q -> k2 o q", o=2) \
                .transpose([1, 0, 2]) \
                .rearrange("o k2 (b c) -> o k2 b c", c=blk)[
                    :, :, :, kt * 128:(kt + 1) * 128]
            cpb = ncols // nblk
            bstep2 = max(1, 1536 // 64)
            for o in range(2):
                for b0 in range(0, nblk, bstep2):
                    b1 = min(nblk, b0 + bstep2)
                    em.dma("gpsimd", dstp[o][:, b0:b1, :],
                           bv[64 * o:64 * (o + 1), b0 * cpb:b1 * cpb]
                           .rearrange("p (b c) -> p b c", b=b1 - b0), wsem)
            store_mark[kt] = wsem.n
            yield
        unpack_mark = unp_mark.get(nkt - 1, 0)
        nq = ncols // 8
        em.wait("sync", sDVE, unpack_mark)
        em.dma("sync", zz_i[0:G, 0:nq], qz_ap, sDQ)
        em.dma("sync", zz_n[0:G, 0:ncols], sc_ap, sDQ)
        em.waitf("vector", sDQ)
        em.waitf("vector", sWS)
        em.waitf("vector", sWS2)
        for sh in range(8):
            em.vec("tensor_scalar",
                   zz_i[0:G, 512:512 + ncols].rearrange("g (c e) -> g c e", e=8)[
                       :, :, sh:sh + 1],
                   zz_i[0:G, 0:nq], 4 * sh, 15,
                   ALU.logical_shift_right, ALU.bitwise_and)
        em.vec("tensor_copy", zz_f[0:G, 0:ncols], zz_i[0:G, 512:512 + ncols])
        em.vec("tensor_scalar", zz_n[0:G, 0:ncols], zz_n[0:G, 0:ncols], -1.0, None,
               ALU.mult)
        em.vec("scalar_tensor_tensor", dq_b[0:G, 0:ncols], zz_f[0:G, 0:ncols], 1.0,
               zz_n[0:G, 0:ncols], ALU.add, ALU.mult, sem=sDVE)
        em.wait("gpsimd", sDVE, sDVE.n)
        bstep = max(1, 1536 // G)
        for b0 in range(0, nblk, bstep):
            b1 = min(nblk, b0 + bstep)
            em.dma("gpsimd", dst[0:G, b0:b1, wofs:wofs + 128],
                   dq_b[0:G, b0 * (ncols // nblk):b1 * (ncols // nblk)]
                   .rearrange("g (b c) -> g b c", b=b1 - b0), sWS)
        yield
        if want_ohh:
            for bl in range(nblk):
                em.wait("sync", sDVE, sDVE.n)
                em.dma("sync", gcol_i[:, :],
                       gidx_oh_ap[bl * 128:(bl + 1) * 128, 0:1], sDQ)
                em.waitf("vector", sDQ)
                em.waitf("vector", sWS)
                em.waitf("vector", sWS2)
                em.vec("tensor_copy", gcol_f[:, :], gcol_i[:, :])
                em.vec("tensor_scalar", oh1[:, :], iota_s[:, 0:88], gcol_f[:, :],
                       None, ALU.is_equal, sem=sDVE)
                em.wait("gpsimd", sDVE, sDVE.n)
                em.dma("gpsimd", dst[:, bl:bl + 1, wofs + 128:wofs + 216],
                       oh1[:, :].unsqueeze(1), sWS)
                yield

    agd_d = {}

    def emit_ohx_gen(gidx_ap, dst):
        for kt in range(DMT):
            em.wait("sync", sDVE, sDVE.n)
            em.dma("sync", gcol_i[:, :], gidx_ap[kt * 128:(kt + 1) * 128, 0:1], sDQ)
            em.waitf("vector", sDQ)
            em.vec("tensor_copy", gcol_f[:, :], gcol_i[:, :])
            em.vec("tensor_scalar", dst[:, kt * GG:(kt + 1) * GG], iota_s[:, 0:GG],
                   gcol_f[:, :], None, ALU.is_equal, sem=sDVE)
            yield

    def deq_master():
        yield from emit_dequant_gen(qbg.ap(), scg_sh.ap(), qzg_sh.ap(), gig.ap(),
                                    DMT, SHN, GG, NTIL, GBLK, wsg_sh, True,
                                    gidsh.ap(), DM)
        em.waitf("gpsimd", sWS)
        em.waitf("gpsimd", sWS2)
        agd_d["agg"] = collective("AllGather", ALU.bypass, [wsg_sh.ap().opt()],
                                  [wsg.ap().opt()])
        yield
        yield from emit_dequant_gen(qbu.ap(), scu_sh.ap(), qzu_sh.ap(), giu.ap(),
                                    DMT, SHN, GG, NTIL, UBLK, wsu_sh, False,
                                    None, DM)
        em.waitf("gpsimd", sWS)
        em.waitf("gpsimd", sWS2)
        agd_d["agu"] = collective("AllGather", ALU.bypass, [wsu_sh.ap().opt()],
                                  [wsu.ap().opt()])
        yield
        yield from emit_dequant_gen(qbd.ap(), scd.ap(), qzd.ap(), gidsh.ap(),
                                    NTIL, DM, GD, DMT, DBLK, wsd_sh, False,
                                    None, SHN)
        em.waitf("gpsimd", sWS)
        em.waitf("gpsimd", sWS2)
        agd_d["agd"] = collective("AllGather", ALU.bypass, [wsd_sh.ap().opt()],
                                  [wsd.ap().opt()])
        yield
        yield from emit_ohx_gen(gig.ap(), ohxg_s)
        yield from emit_ohx_gen(giu.ap(), ohxu_s)

    deq = deq_master()
    if stop_after == "dequant":
        for _ in deq:
            pass
        _finish(em, tap_pairs, sOUT, sCC)
        em.replay_into(nc)
        return nc

    def emit_attn(cfg):
        nkt = cfg["kw"] // 128
        WREG = cfg["wreg"]
        XBUF = cfg["xbuf"]
        STP = cfg["statps"]
        if cfg["per_head_kv"]:
            passes = [[("q", 0), ("q", 1), ("q", 2), ("q", 3), ("k", 0), ("k", 1),
                       ("k", 2), ("k", 3), ("v", 0), ("v", 1), ("v", 2), ("v", 3)]]
        else:
            passes = [[("q", 0), ("q", 1), ("q", 2), ("q", 3), ("k", 0), ("v", 0)]]

        def wslot(wi):
            if wi < 11:
                return WREG[:, wi * DM:(wi + 1) * DM]
            return R3[:, 0:DM]
        dram_of = {"q": cfg["qT"], "k": cfg["kT"], "v": cfg["vT"]}
        wsrc_of = {"q": cfg["wq"], "k": cfg["wk"], "v": cfg["wv"]}
        for pi, plist in enumerate(passes):
            em.wait("gpsimd", sPE, sPE.n)
            em.wait("gpsimd", sDVE, sDVE.n)
            em.waitf("gpsimd", sLNS)
            em.waitf("sync", sLNS)
            for wi, (nm, nt) in enumerate(plist):
                src = wsrc_of[nm][:, nt * 128:(nt + 1) * 128] \
                    .rearrange("(dt p) n -> p dt n", p=128)
                em.dma("gpsimd", wslot(wi)
                       .rearrange("p (dt n) -> p dt n", dt=DMT), src, sAW)
            em.waitf("tensor", sAW)
            last_evac = {}
            for sc_i in (list(range(0, NMAC, 2)) + list(range(1, NMAC, 2))):
                em.wait("sync", sCC, cfg["wait_cc_half"][sc_i % 2])
                em.wait("sync", sPE, sPE.n)
                em.wait("sync", sDVE, sDVE.n)
                base = (sc_i % 2) * NC * DM + (sc_i // 2) * DM
                xsrc = ln_full.ap()[base:base + DM, 0:512] \
                    .rearrange("(dt p) c -> p dt c", p=128)
                em.dma("sync", XBUF[:, :].rearrange("p (dt c) -> p dt c", dt=DMT),
                       xsrc, sX)
                if cfg["rope"]:
                    em.dma("sync", rope_c,
                           cosR.ap()[:, sc_i * 512:(sc_i + 1) * 512], sX)
                    em.dma("sync", rope_s,
                           sinS.ap()[:, sc_i * 512:(sc_i + 1) * 512], sX)
                em.waitf("tensor", sX)
                em.waitf("vector", sX)
                for wi, (nm, nt) in enumerate(plist):
                    half = wi % 2
                    acc = PA[:, half * 512:(half + 1) * 512]
                    prevev = last_evac.get(half)
                    if prevev is not None:
                        em.wait("tensor", prevev[0], prevev[1])
                    wsl = wslot(wi)
                    for dt_i in range(DMT):
                        em.mm(acc, wsl[:, dt_i * 128:dt_i * 128 + 128],
                              XBUF[:, dt_i * 512:(dt_i + 1) * 512],
                              start=(dt_i == 0), stop=(dt_i == DMT - 1),
                              sem=(sPE if dt_i == DMT - 1 else None))
                    mmv = sPE.n
                    em.wait("scalar", sPE, mmv)
                    if nm == "v":
                        vt = tb0[:, 0:512]
                        em.wait("vector", sPE, sPE.n)
                        if cfg["biased"]:
                            em.vec("tensor_scalar", vt, acc, bv_s[:, nt:nt + 1],
                                   None, ALU.add, sem=sDVE)
                        else:
                            em.vec("tensor_copy", vt, acc, sem=sDVE)
                        av = sDVE.n
                        last_evac[half] = (sDVE, av)
                        em.wait("tensor", sDVE, av)
                        for sub in range(4):
                            em.trans(PT[:, sub * 128:(sub + 1) * 128],
                                     vt[:, sub * 128:(sub + 1) * 128],
                                     ident_s[:, :],
                                     sem=(sPE if sub == 3 else None))
                        tv = sPE.n
                        em.wait("vector", sPE, tv)
                        em.waitf("vector", sVT)
                        em.vec("tensor_copy", vtr[:, :], PT[:, :], sem=sDVE)
                        em.wait("sync", sDVE, sDVE.n)
                        em.dma("sync",
                               dram_of["v"][sc_i * 512:(sc_i + 1) * 512,
                                            nt * 128:(nt + 1) * 128]
                               .rearrange("(sub s) d -> sub s d", sub=4)
                               .transpose([1, 0, 2]),
                               vtr[:, :].rearrange("p (sub d) -> p sub d", sub=4),
                               sVT)
                        continue
                    if cfg["rope"]:
                        a = t0[:, 0:512]
                        em.waitf("scalar", sSW)
                        em.wait("scalar", sDVE, sDVE.n)
                        em.actv(a, acc, AF.Copy, sem=sACT)
                        av = sACT.n
                        last_evac[half] = (sACT, av)
                        em.wait("sync", sACT, av)
                        em.wait("sync", sDVE, sDVE.n)
                        em.dma("sync", t1[0:64, 0:512], a[64:128, 0:512], sSW)
                        em.dma("sync", t1[64:128, 0:512], a[0:64, 0:512], sSW)
                        em.waitf("vector", sSW)
                        em.vec("tensor_mul", t0[:, 512:1024], a, rope_c)
                        em.vec("tensor_mul", t1[:, 512:1024], t1[:, 0:512], rope_s)
                        qk_t = tb0[:, 512:1024]
                        em.waitf("vector", sQKV)
                        em.vec("tensor_add", qk_t, t0[:, 512:1024],
                               t1[:, 512:1024], sem=sDVE)
                        dv = sDVE.n
                        em.wait("sync", sDVE, dv)
                        em.wait("scalar", sDVE, dv)
                    else:
                        bias_ap = (bq_s if nm == "q" else bk_s)[:, nt:nt + 1]
                        qk_t = tb0[:, 512:1024]
                        em.waitf("vector", sQKV)
                        em.wait("vector", sPE, sPE.n)
                        em.wait("vector", sACT, sACT.n)
                        em.vec("tensor_scalar", qk_t, acc, bias_ap, None, ALU.add,
                               sem=sDVE)
                        av = sDVE.n
                        last_evac[half] = (sDVE, av)
                        em.wait("sync", sDVE, av)
                    em.dma("sync", dram_of[nm][nt * 128:(nt + 1) * 128,
                                                 sc_i * 512:(sc_i + 1) * 512],
                           qk_t, sQKV)
                    em.wait("vector", sPE, sPE.n)
                    em.vec("tensor_mul", tb0[:, 1024:1536], qk_t, qk_t,
                           sem=sDVE)
                    em.wait("tensor", sDVE, sDVE.n)
                    em.mm(STP[0:1, 0:512], onesc_b[:, :], tb0[:, 1024:1536],
                          start=True, stop=True, sem=sPE)
                    em.wait("vector", sPE, sPE.n)
                    dst_row = qn_s if nm == "q" else kn_s
                    em.vec("tensor_reduce",
                           dst_row[0:1, nt * 16 + sc_i:nt * 16 + sc_i + 1],
                           STP[0:1, 0:512], AXX, ALU.max, sem=sDVE)
                    if wi == 2 and cfg.get("interleave"):
                        cfg["interleave"]()
                if cfg.get("interleave"):
                    cfg["interleave"]()

        em.wait("vector", sDVE, sDVE.n)
        for j in range(nkt):
            em.vec("tensor_reduce", kn_s[0:1, 64 + j:65 + j],
                   kn_s[0:1, j * 16:(j + 1) * 16], AXX, ALU.max)
        if nkt == 1:
            em.vec("tensor_mul", cneg_s[0:1, :], qn_s[0:1, :],
                   kn_s[0:1, 64:65].broadcast_to([1, 64]), sem=sDVE)
        else:
            em.vec("tensor_mul",
                   cneg_s[0:1, :].rearrange("o (h m) -> o h m", h=4),
                   qn_s[0:1, :].rearrange("o (h m) -> o h m", h=4),
                   kn_s[0:1, 64:68].unsqueeze(2).broadcast_to([1, 4, 16]),
                   sem=sDVE)
        dv = sDVE.n
        em.wait("scalar", sDVE, dv)
        em.actv(rows[0:1, 0:64], cneg_s[0:1, :], AF.Ln)
        em.actv(rows[0:1, 64:128], rows[0:1, 0:64], AF.Exp, scale=0.5, sem=sACT)
        av = sACT.n
        em.wait("vector", sACT, av)
        em.vec("tensor_scalar", cneg_s[0:1, :], rows[0:1, 64:128], -SCALE, None,
               ALU.mult, sem=sDVE)
        dv = sDVE.n
        em.wait("tensor", sDVE, dv)
        em.mm(PC[:, 0:64], ones1[:, :], cneg_s[0:1, :], start=True, stop=True,
              sem=sPE)
        em.wait("vector", sPE, sPE.n)
        em.vec("tensor_copy", bias_all[:, :], PC[:, 0:64], sem=sDVE)
        em.wait("tensor", sDVE, sDVE.n)

        em.wait("gpsimd", sPE, sPE.n)
        wo_src = cfg["wo"].rearrange("(et p) f -> p et f", p=128)
        em.dma("gpsimd", wo_sb[:, :].rearrange("p (et f) -> p et f", et=4),
               wo_src, sWO)

        ones33 = R3[:, 16384:16640].bitcast(F32)
        em.vec("memset", ones33[32:33, 0:128], 1.0, sem=sDVE)
        em.waitf("sync", sQKV)
        em.waitf("sync", sVT)
        exp_done = {0: 0, 1: 0}
        pslot_pv = [0, 0, 0, 0]
        bctr = 0
        pend = [None]
        kv_done = {}

        def flush_pend():
            if pend[0] is None:
                return
            (fh, fh2, fkc, fb, fslot, fmi, fuse_sem, fuse_val, fpar2) = pend[0]
            pend[0] = None
            em.wait("tensor", fuse_sem, fuse_val)
            pq = p_sb[:, fslot * 1024: fslot * 1024 + 1024]
            first = (fkc == 0)
            last = (fkc == fmi)
            for j in range(2):
                let = 2 * fb + j
                em.mm(PC[32 * fh2:32 * fh2 + 1, 0:512], onesc_b[:, :],
                      pq[:, j * 512:(j + 1) * 512],
                      start=(first and let == 0), stop=(last and let == 3),
                      tile_position=(0, 32 * fh2))
            v = None
            for j in range(2):
                let = 2 * fb + j
                if cfg["per_head_kv"]:
                    vl = kv_v[:, fpar2 * 2048 + let * 512 + fh * 128:
                              fpar2 * 2048 + let * 512 + fh * 128 + 128]
                else:
                    vl = kv_v[:, fpar2 * 512 + let * 128:
                              fpar2 * 512 + let * 128 + 128]
                v = em.mm(PA[:, fh2 * 512:(fh2 + 1) * 512], vl,
                          pq[:, j * 512:(j + 1) * 512],
                          start=(first and let == 0), stop=(last and let == 3),
                          sem=sPE)
            pslot_pv[fslot] = v
            if fh2 == 1 and fb == 1:
                kv_done[fkc] = v

        macro_order = list(range(0, NMAC, 2)) + list(range(1, NMAC, 2))
        for mseq, mi in enumerate(macro_order):
            em.wait("sync", sPE, sPE.n)
            em.dma("sync", q_mac[:, :].rearrange("p (h c) -> p h c", h=4),
                   cfg["qT"][0:512, mi * 512:(mi + 1) * 512]
                   .rearrange("(h p) c -> p h c", p=128), sQ)
            phk = cfg["per_head_kv"]
            nslot = 2 if phk else 4
            swid = 2048 if phk else 512
            pre = 1 if phk else 2
            for hp in range(2):
                kv_done.clear()
                kv_mark = {}
                em.wait("sync", sPE, sPE.n)

                def issue_kv(kc, mi=mi):
                    sl = kc % nslot
                    g = kv_done.get(kc - nslot)
                    if g:
                        em.wait("sync", sPE, g)
                    base = sl * swid
                    if phk:
                        for hf in range(2):
                            ks = cfg["kT"][hf * 256:(hf + 1) * 256,
                                           kc * 512:(kc + 1) * 512] \
                                .rearrange("(h p) c -> p h c", p=128)
                            em.dma("sync",
                                   kv_k[:, base + hf * 1024:base + hf * 1024 + 1024]
                                   .rearrange("p (h c) -> p h c", h=2), ks,
                                   sKV[sl])
                            vs = cfg["vT"][kc * 512 + hf * 256:
                                           kc * 512 + (hf + 1) * 256, 0:512] \
                                .rearrange("(l p) d -> p l d", p=128)
                            em.dma("sync",
                                   kv_v[:, base + hf * 1024:base + hf * 1024 + 1024]
                                   .rearrange("p (l d) -> p l d", l=2), vs,
                                   sKV[sl])
                    else:
                        ks = cfg["kT"][0:128, kc * 512:(kc + 1) * 512]
                        vs = cfg["vT"][kc * 512:(kc + 1) * 512, 0:128] \
                            .rearrange("(l p) d -> p l d", p=128)
                        em.dma("sync", kv_k[:, base:base + 512], ks, sKV[sl])
                        em.dma("sync", kv_v[:, base:base + 512]
                               .rearrange("p (l d) -> p l d", l=4), vs, sKV[sl])
                    kv_mark[kc] = sKV[sl].n

                for kp in range(min(pre, mi + 1)):
                    issue_kv(kp)
                for kc in range(mi + 1):
                    par2 = kc % nslot
                    em.wait("tensor", sKV[par2], kv_mark[kc])
                    nxt = kc + pre
                    if nxt <= mi and nxt not in kv_mark:
                        if phk:
                            flush_pend()
                        issue_kv(nxt)
                    if kc == 0:
                        em.waitf("tensor", sQ)
                    for h2 in range(2):
                        h = hp * 2 + h2
                        for b in range(2):
                            par = bctr % 2
                            slot = bctr % 4
                            bctr += 1
                            buf = PB if par == 0 else PB2
                            if exp_done[par]:
                                em.wait("tensor", sACT, exp_done[par])
                            qk = None
                            for j in range(2):
                                let = 2 * b + j
                                if cfg["per_head_kv"]:
                                    lt = kv_k[:, par2 * 2048 + h * 512 + let * 128:
                                              par2 * 2048 + h * 512 + let * 128 + 128]
                                else:
                                    lt = kv_k[:, par2 * 512 + let * 128:
                                              par2 * 512 + let * 128 + 128]
                                qk = em.mm(buf[:, j * 512:(j + 1) * 512], lt,
                                           q_mac[:, h * 512:(h + 1) * 512],
                                           start=True, stop=True, sem=sPE)
                            flush_pend()
                            em.wait("scalar", sPE, max(qk, pslot_pv[slot]))
                            pq = p_sb[:, slot * 1024: slot * 1024 + 1024]
                            em.actv(pq, buf[:, :], AF.Exp,
                                    bias=bias_all[:, h * 16 + mi:h * 16 + mi + 1],
                                    scale=SCALE, sem=sACT)
                            ev = sACT.n
                            exp_done[par] = ev
                            use_sem, use_val = sACT, ev
                            if kc == mi:
                                em.wait("vector", sACT, ev)
                                em.vec("tensor_mul", pq, pq,
                                       tri_s[:, b * 1024:(b + 1) * 1024], sem=sDVE)
                                use_sem, use_val = sDVE, sDVE.n
                            pend[0] = (h, h2, kc, b, slot, mi, use_sem, use_val,
                                       par2)
                flush_pend()
                pv_all = sPE.n
                em.wait("vector", sPE, pv_all)
                em.waitf("vector", sPRS[0])
                em.waitf("vector", sPRS[1])
                em.waitf("vector", sSW)
                em.vec("tensor_copy", den_s[0:1, :], PC[0:1, :])
                em.vec("tensor_copy", den_s[32:33, :], PC[32:33, :])
                em.vec("reciprocal", den_s[0:1, :], den_s[0:1, :])
                em.vec("reciprocal", den_s[32:33, :], den_s[32:33, :], sem=sDVE)
                rv = sDVE.n
                em.wait("tensor", sDVE, rv)
                for h2 in range(2):
                    h = hp * 2 + h2
                    dsrc = den_s[0:1, :] if h2 == 0 else den_s[32:33, :]
                    lsrc = ones1[:, :] if h2 == 0 else ones33[32:33, :]
                    em.wait("tensor", sDVE, sDVE.n)
                    em.mm(PB[:, 0:512], lsrc, dsrc,
                          start=True, stop=True, sem=sPE)
                    bv_ = sPE.n
                    em.wait("scalar", sPE, bv_)
                    em.wait("scalar", sDVE, sDVE.n)
                    em.actv(t0[:, 0:512], PA[:, h2 * 512:(h2 + 1) * 512], AF.Copy,
                            sem=sACT)
                    av = sACT.n
                    em.wait("vector", sACT, av)
                    em.wait("vector", sPE, bv_)
                    em.vec("tensor_mul", att_sb[:, h * 512:(h + 1) * 512],
                           t0[:, 0:512], PB[:, 0:512], sem=sDVE)
                em.wait("tensor", sDVE, sDVE.n)
                exp_done[0] = 0
                exp_done[1] = 0
                em.wait("tensor", sACT, sACT.n)
            em.waitf("tensor", sWO)
            op_evac = {}
            for ft in range(DMT):
                half = ft % 2
                bank = PB2[:, half * 512:half * 512 + 512]
                if ft >= 2:
                    em.wait("tensor", sDVE, op_evac[ft - 2])
                for et in range(4):
                    em.mm(bank, wo_sb[:, (et * DMT + ft) * 128:
                                      (et * DMT + ft) * 128 + 128],
                          att_sb[:, et * 512:(et + 1) * 512],
                          start=(et == 0), stop=(et == 3),
                          sem=(sPE if et == 3 else None))
                pv = sPE.n
                em.wait("vector", sPE, pv)
                em.waitf("vector", sPRS[0])
                em.waitf("vector", sPRS[1])
                em.vec("tensor_copy", t1[:, half * 512:half * 512 + 512], bank,
                       sem=sDVE)
                op_evac[ft] = sDVE.n
                em.wait("gpsimd", sDVE, op_evac[ft])
                em.dma("gpsimd",
                       partial.ap()[(mi % 2) * NC * DM + (mi // 2) * DM + 128 * ft:
                                    (mi % 2) * NC * DM + (mi // 2) * DM + 128 * (ft + 1),
                                    0:512],
                       t1[:, half * 512:half * 512 + 512], sPRS[mi % 2])
            em.wait("tensor", sACT, sACT.n)
            em.wait("tensor", sDVE, sDVE.n)
            if mseq == NMAC // 2 - 1:
                em.waitf("gpsimd", sPRS[0])
                cfg["rs_out"].append(cfg["mk_rs"](0))
            elif mseq == NMAC - 1:
                em.waitf("gpsimd", sPRS[1])
                cfg["rs_out"].append(cfg["mk_rs"](1))

    def mk_rs(halfsel):
        return collective(
            "ReduceScatter", ALU.add,
            [partial.ap()[halfsel * NC * DM:(halfsel + 1) * NC * DM, :].opt()],
            [red.ap()[halfsel * DM:(halfsel + 1) * DM, :].opt()])

    def inter_deq():
        for _ in range(6):
            if next(deq, "DONE") == "DONE":
                break

    cfg_a = dict(qT=qaD.ap(), kT=kaD.ap(), vT=vaD.ap(), per_head_kv=False,
                 rope=True, biased=False, wq=waqT.ap(), wk=wakT.ap(),
                 wv=wavT.ap(), wo=waoT.ap(), kw=128,
                 wait_cc_half=(ag_half[0], ag_half[1]),
                 wreg=R2, xbuf=R3[:, 0:16384], statps=PB2,
                 interleave=inter_deq, mk_rs=mk_rs, rs_out=[])
    emit_attn(cfg_a)
    for _ in deq:
        pass
    rs1 = cfg_a["rs_out"][1]

    if stop_after == "attn_a":
        _finish(em, tap_pairs, sOUT, sCC)
        em.replay_into(nc)
        return nc

    def emit_ffn(cc, src_x, dst_x, dst_lnloc, is_last):
        col = cc * 512
        em.wait("sync", sPE, sPE.n)
        em.wait("tensor", sDVE, sDVE.n)
        for kt in range(DMT):
            em.mm(PC[0:GG, 0:512], ohxg_s[:, kt * GG:(kt + 1) * GG],
                  x_sb[:, kt * 512:(kt + 1) * 512], start=(kt == 0),
                  stop=(kt == DMT - 1))
        for kt in range(DMT):
            em.mm(PB2[0:GG, 0:512], ohxu_s[:, kt * GG:(kt + 1) * GG],
                  x_sb[:, kt * 512:(kt + 1) * 512], start=(kt == 0),
                  stop=(kt == DMT - 1), sem=(sPE if kt == DMT - 1 else None))
        pv = sPE.n
        em.wait("vector", sPE, pv)
        em.vec("tensor_copy", xg_s[:, 0:512], PC[0:GG, 0:512])
        em.vec("tensor_copy", xg_s[:, 512:1024], PB2[0:GG, 0:512], sem=sDVE)
        em.wait("tensor", sDVE, sDVE.n)

        wpar_done = {}
        hpar_done = {}
        for nt in range(NTI):
            parw = nt % 2
            r, ntl = divmod(nt, NTIL)
            woff = parw * (GBLK + UBLK)
            if nt >= 2:
                em.wait("sync", sPE, wpar_done[nt - 2])
            em.dma("sync", R3[:, woff:woff + GBLK],
                   wsg.ap()[r * 128:(r + 1) * 128, ntl * GBLK:(ntl + 1) * GBLK],
                   sW[parw])
            em.dma("sync", R3[:, woff + GBLK:woff + GBLK + UBLK],
                   wsu.ap()[r * 128:(r + 1) * 128, ntl * UBLK:(ntl + 1) * UBLK],
                   sW[parw])
            em.waitf("tensor", sW[parw])
            half = nt % 2
            if nt >= 2:
                em.wait("tensor", sACT, hpar_done[nt - 2][0])
                em.wait("tensor", sDVE, hpar_done[nt - 2][1])
            for gi_, base in ((0, woff), (1, woff + GBLK)):
                acc = (PA if gi_ == 0 else PB)[:, half * 512:(half + 1) * 512]
                for kt in range(DMT):
                    em.mm(acc, R3[:, base + kt * 128:base + kt * 128 + 128],
                          x_sb[:, kt * 512:(kt + 1) * 512], start=(kt == 0),
                          stop=False)
                em.mm(acc, R3[0:GG, base + DM:base + DM + 128],
                      xg_s[0:GG, gi_ * 512:gi_ * 512 + 512],
                      start=False, stop=True, sem=sPE)
            gu = sPE.n
            em.wait("scalar", sPE, gu)
            sil = tb0[:, half * 512:half * 512 + 512]
            em.actv(sil, PA[:, half * 512:(half + 1) * 512], AF.Silu, sem=sACT)
            av = sACT.n
            em.wait("vector", sACT, av)
            em.vec("tensor_mul", h_sb[:, nt * 512:(nt + 1) * 512], sil,
                   PB[:, half * 512:(half + 1) * 512], sem=sDVE)
            hv = sDVE.n
            hpar_done[nt] = (av, hv)
            em.wait("tensor", sDVE, hv)
            wpar_done[nt] = em.mm(
                PC[0:88, 0:512], R3[:, woff + DM + 128:woff + DM + 216],
                h_sb[:, nt * 512:(nt + 1) * 512], start=(nt == 0),
                stop=(nt == NTI - 1), sem=sPE)
        hgv = sPE.n
        em.wait("vector", sPE, hgv)
        em.vec("tensor_copy", hg_s[:, :], PC[0:88, 0:512], sem=sDVE)
        em.wait("tensor", sDVE, sDVE.n)

        em.wait("sync", sPE, sPE.n)
        em.wait("sync", sDVE, sDVE.n)
        em.wait("scalar", sDVE, sDVE.n)
        prev_stat2 = 0
        wd_done = {}
        dn_evac = {}
        slot_dve = {}
        for ft in range(DMT):
            parw = ft % 2
            wdoff = parw * (NC * DBLK)
            if ft >= 2:
                em.wait("sync", sPE, wd_done[ft - 2])
            for r in range(NC):
                em.dma("sync", R2[:, wdoff + r * DBLK: wdoff + (r + 1) * DBLK],
                       wsd.ap()[r * 128:(r + 1) * 128,
                                ft * DBLK:(ft + 1) * DBLK], sD[parw])
            em.waitf("tensor", sD[parw])
            bank = PB2[:, parw * 512:parw * 512 + 512]
            if ft >= 2:
                em.wait("tensor", sACT, dn_evac[ft - 2])
            for kt in range(NTI):
                r2, kl = divmod(kt, NTIL)
                em.mm(bank, R2[:, wdoff + r2 * DBLK + kl * 128:
                               wdoff + r2 * DBLK + kl * 128 + 128],
                      h_sb[:, kt * 512:(kt + 1) * 512], start=(kt == 0),
                      stop=False)
            em.mm(bank, R2[0:GD, wdoff + SHN: wdoff + SHN + 128],
                  hg_s[0:GD, 0:512], start=False, stop=True, sem=sPE)
            dv_ = sPE.n
            wd_done[ft] = dv_
            em.wait("scalar", sPE, dv_)
            rsl = ft % 3
            a = t0[:, rsl * 512:(rsl + 1) * 512]
            if ft >= 3:
                em.waitf("scalar", sST[rsl])
                if is_last:
                    em.waitf("scalar", sOUT)
                em.wait("sync", sDVE, slot_dve.get(ft - 3, 0))
            em.actv(a, bank, AF.Copy, sem=sACT)
            dn_evac[ft] = sACT.n
            b = t1[:, rsl * 512:(rsl + 1) * 512]
            em.dma("sync", b, src_x[128 * ft:128 * (ft + 1), col:col + 512],
                   sTL[rsl])
            em.waitf("vector", sTL[rsl])
            em.wait("vector", sACT, dn_evac[ft])
            em.vec("tensor_add", a, a, b, sem=sDVE)
            mv = sDVE.n
            slot_dve[ft] = mv
            em.wait("gpsimd", sDVE, mv)
            if is_last:
                em.dma("gpsimd", out_ext.ap()[128 * ft:128 * (ft + 1),
                                              col:col + 512], a, sOUT)
            else:
                em.dma("gpsimd", dst_x[128 * ft:128 * (ft + 1), col:col + 512], a,
                       sST[rsl])
            if dst_lnloc is not None:
                xb = tb0[:, 1024:1536]
                em.wait("scalar", sPE, prev_stat2)
                em.wait("scalar", sDVE, mv)
                em.actv(xb, a, AF.Copy, sem=sACT)
                av2 = sACT.n
                em.wait("tensor", sACT, av2)
                em.mm(PA[0:1, 0:512], onesc_b[:, :], xb, start=(ft == 0),
                      stop=(ft == DMT - 1), sem=sPE)
                em.wait("scalar", sPE, sPE.n)
                em.actv(xb, a, AF.Square, sem=sACT)
                av3 = sACT.n
                em.wait("tensor", sACT, av3)
                prev_stat2 = em.mm(PA[0:1, 512:1024], onesc_b[:, :], xb,
                                   start=(ft == 0), stop=(ft == DMT - 1), sem=sPE)
        if dst_lnloc is None:
            return
        pe_done = sPE.n
        em.wait("vector", sPE, pe_done)
        em.vec("tensor_scalar", rows[0:1, 0:512], PA[0:1, 0:512], 1.0 / DM, None,
               ALU.mult)
        em.vec("tensor_scalar", rows[0:1, 512:1024], PA[0:1, 512:1024], 1.0 / DM,
               None, ALU.mult)
        em.vec("tensor_mul", rows[0:1, 1024:1536], rows[0:1, 0:512],
               rows[0:1, 0:512])
        em.vec("tensor_sub", rows[0:1, 512:1024], rows[0:1, 512:1024],
               rows[0:1, 1024:1536], sem=sDVE)
        vv = sDVE.n
        em.wait("scalar", sDVE, vv)
        em.actv(rows[0:1, 1024:1536], rows[0:1, 512:1024], AF.Ln,
                bias=eps_c[0:1, :])
        em.actv(rows[0:1, 512:1024], rows[0:1, 1024:1536], AF.Exp, scale=-0.5,
                sem=sACT)
        av = sACT.n
        em.wait("vector", sACT, av)
        em.vec("tensor_mul", rows[0:1, 1024:1536], rows[0:1, 0:512],
               rows[0:1, 512:1024], sem=sDVE)
        mrv = sDVE.n
        em.wait("tensor", sDVE, mrv)
        em.mm(PB[:, 0:512], ones1[:, :], rows[0:1, 512:1024], start=True, stop=True)
        em.mm(PB[:, 512:1024], ones1[:, :], rows[0:1, 1024:1536], start=True,
              stop=True, sem=sPE)
        bcv = sPE.n
        for i in range(3):
            em.waitf("sync", sST[i])
        em.waitf("vector", sLNS)
        em.wait("sync", sDVE, sDVE.n)
        em.wait("sync", sACT, sACT.n)
        slot_dve2 = {}
        for ft in range(DMT):
            rsl = ft % 3
            a = t0[:, rsl * 512:(rsl + 1) * 512]
            if ft >= 3:
                em.wait("sync", sDVE, slot_dve2.get(ft - 3, 0))
            em.dma("sync", a, dst_x[128 * ft:128 * (ft + 1), col:col + 512],
                   sTL[rsl])
            em.waitf("vector", sTL[rsl])
            em.wait("vector", sPE, bcv)
            b = t1[:, rsl * 512:(rsl + 1) * 512]
            em.vec("tensor_mul", b, a, PB[:, 0:512])
            ln_t = x_sb[:, ft * 512:(ft + 1) * 512]
            em.vec("tensor_sub", ln_t, b, PB[:, 512:1024], sem=sDVE)
            slot_dve2[ft] = sDVE.n
            em.wait("gpsimd", sDVE, sDVE.n)
            em.dma("gpsimd", dst_lnloc[128 * ft:128 * (ft + 1), 0:512],
                   ln_t, sLNS)

    em.wait("tensor", sCC, agd_d["agu"])
    em.wait("sync", sCC, agd_d["agd"])
    ag2_half = {}
    for cc in range(NCH):
        rsw = cfg_a["rs_out"][cc]
        em.wait("sync", sCC, rsw)
        em.wait("vector", sCC, rsw)
        emit_merge_ln_chunk(cc, x0T.ap(),
                            red.ap()[cc * DM:(cc + 1) * DM, :], xA.ap(), None,
                            None)
        em.wait("gpsimd", sCC, ag_half[1])
        emit_ffn(cc, xA.ap(), xB.ap(),
                 ln_loc.ap()[cc * DM:(cc + 1) * DM, :], is_last=False)
        em.waitf("gpsimd", sLNS)
        ag2_half[cc] = mk_ag_ln(cc)
    ag2a, ag2b = ag2_half[0], ag2_half[1]

    if stop_after == "ffn1":
        _finish(em, tap_pairs, sOUT, sCC)
        em.replay_into(nc)
        return nc

    cfg_b = dict(qT=qbD.ap(), kT=kbD.ap(), vT=vbD.ap(), per_head_kv=True,
                 rope=False, biased=True, wq=wbqT.ap(), wk=wbkT.ap(),
                 wv=wbvT.ap(), wo=wboT.ap(), kw=512,
                 wait_cc_half=(ag2a, ag2b),
                 wreg=R1, xbuf=x_sb, statps=PC,
                 interleave=None, mk_rs=mk_rs, rs_out=[])
    emit_attn(cfg_b)
    rs2 = cfg_b["rs_out"][1]

    for cc in range(NCH):
        rsw = cfg_b["rs_out"][cc]
        em.wait("sync", sCC, rsw)
        em.wait("vector", sCC, rsw)
        emit_merge_ln_chunk(cc, xB.ap(),
                            red.ap()[cc * DM:(cc + 1) * DM, :], xA.ap(), None,
                            bo_s)
        emit_ffn(cc, xA.ap(), None, None, is_last=True)

    _finish(em, tap_pairs, sOUT, sCC)
    em.replay_into(nc)
    return nc


def _finish(em, tap_pairs, sOUT, sCC):
    if sCC.n:
        em.wait("gpsimd", sCC, sCC.n)
    for tout, tsrc in tap_pairs:
        em.dma("gpsimd", tout.ap(), tsrc.ap(), sOUT)
    em.waitf("gpsimd", sOUT)
    em.waitf("sync", sOUT)


def kernel(**inputs):
    in_maps = _host_prep(inputs)
    if "nc" not in _NC_CACHE:
        _NC_CACHE["nc"] = build_nc()
    nc = _NC_CACHE["nc"]
    res = run_bass_kernel_spmd(nc, in_maps, core_ids=list(range(NC)))
    outs = [res.results[c]["out"] for c in range(NC)]
    full = np.concatenate([o.T for o in outs], axis=0)
    return full[None, :, :].astype(np.float32, copy=False)
